# version 07
# baseline (speedup 1.0000x reference)
import numpy as np
from contextlib import ExitStack
import concourse.bass as bass
import concourse.mybir as mybir
from concourse.bass_utils import run_bass_kernel_spmd

F32 = mybir.dt.float32
BF16 = mybir.dt.bfloat16
AF = mybir.ActivationFunctionType
ALU = mybir.AluOpType
AX = mybir.AxisListType

EXP_BIAS = -16.0
NEGBIG = -1.0e38
ARENA = 52736


class Sched:
    ENGS = ("sync", "gpsimd", "tensor", "vector", "scalar")

    def __init__(self):
        self.ops = {e: [] for e in self.ENGS}
        self.cnt = {"pe": 0, "dv": 0, "ac": 0}
        self.sems = {}
        self.semh = {}
        self.ps6 = None

    def _add(self, eng, fn):
        self.ops[eng].append(fn)

    def op(self, eng, build, sem=None, inc=0):
        tgt = None
        if sem is not None:
            self.cnt[sem] += inc
            tgt = self.cnt[sem]

        def fn(e, b=build, s=sem, i=inc, self_=self):
            inst = b(e)
            if s is not None:
                inst.then_inc(self_.semh[s], i)
        self._add(eng, fn)
        return tgt

    def wait(self, eng, sem, val):
        if not val:
            return
        def fn(e, s=sem, v=val, self_=self):
            e.wait_ge(self_.semh[s], v)
        self._add(eng, fn)

    def waitp(self, eng, pair):
        if pair:
            self.wait(eng, pair[0], pair[1])

    def wait_total(self, eng, names):
        for n in names:
            if self.sems.get(n):
                self.wait(eng, n, self.sems[n])

    def v(self, build, inc=False):
        self._add("vector", lambda e: e.drain())
        return self.op("vector", build, "dv" if inc else None, 1 if inc else 0)

    def s(self, build, inc=False):
        self._add("scalar", lambda e: e.drain())
        return self.op("scalar", build, "ac" if inc else None, 1 if inc else 0)

    def pe(self, build, inc=False):
        return self.op("tensor", build, "pe" if inc else None, 1 if inc else 0)

    def _dma(self, eng, out, in_, sem):
        self.sems[sem] = self.sems.get(sem, 0) + 16
        tgt = self.sems[sem]

        def fn(e, o=out, i=in_, s=sem, self_=self):
            e.dma_start(out=o, in_=i).then_inc(self_.semh[s], 16)
        self._add(eng, fn)
        return (sem, tgt)

    def load(self, out, in_, sem):
        return self._dma("sync", out, in_, sem)

    def store(self, out, in_, sem):
        return self._dma("gpsimd", out, in_, sem)


def build_program(T, E, I):
    H = E // 128
    EC = E // 128
    TC = T // 128
    QT = T // 512
    NV = min(512, E); FSV = E // NV
    N2 = min(256, E); FS2 = E // N2
    JS = I // 512
    IC = I // 128
    NB6 = T // 256
    NF = min(512, E); NFT = E // NF
    STRIP_E = min(1024, E)
    STRIP_I = min(1024, I)

    nc = bass.Bass("TRN2", target_bir_lowering=False)
    dt = lambda n, s, d=F32, k="ExternalInput": nc.dram_tensor(n, s, d, kind=k).ap()

    x_in = dt("x_in", [T, E])
    wq_t = dt("wq_t", [E, E]); wk_t = dt("wk_t", [E, E])
    wv_r = dt("wv_r", [FSV * 128, EC * NV])
    wo_r = dt("wo_r", [FS2 * 128, EC * N2])
    bq_c = dt("bq_c", [128, EC]); bk_c = dt("bk_c", [128, EC])
    bv_row = dt("bv_row", [1, E]); bo_row = dt("bo_row", [1, E])
    wmod = dt("wmod", [E, 6 * E])
    c_col = dt("c_col", [128, EC])
    wg_r = dt("wg_r", [JS * 128, EC * 512])
    wu_r = dt("wu_r", [JS * 128, EC * 512])
    wd_t = dt("wd_t", [I, E])
    lnw_row = dt("lnw_row", [1, I])
    chi_in = dt("chi_in", [128, T]); clo_in = dt("clo_in", [128, T])
    shi_in = dt("shi_in", [128, T]); slo_in = dt("slo_in", [128, T])
    mask_in = dt("mask_in", [128, 4 * 512])
    id_in = dt("id_in", [128, 128])

    q_sc = dt("q_sc", [H * 128, T], F32, "Internal")
    k_sc = dt("k_sc", [H * 128, T], F32, "Internal")
    v_sc = dt("v_sc", [H * T, 128], F32, "Internal")
    x2_sc = dt("x2_sc", [T, E], F32, "Internal")
    h_sc = dt("h_sc", [T, I], F32, "Internal")
    tern_sc = dt("tern_sc", [I, E], BF16, "Internal")
    tg_sc = dt("tg_sc", [JS * 128, EC * 512], BF16, "Internal")
    tu_sc = dt("tu_sc", [JS * 128, EC * 512], BF16, "Internal")
    mod_sc = dt("mod_sc", [1, 6 * E], F32, "Internal")
    y_out = dt("y_out", [T, E], F32, "ExternalOutput")

    S = Sched()
    es = ExitStack()
    with es:
        AR = es.enter_context(nc.sbuf_tensor("arena", [128, ARENA], F32))
        ARB = AR.bitcast(BF16)

        def f32(off, n):
            assert off + n <= ARENA, ("f32 overflow", off, n)
            return AR[:, off:off + n]

        def b16(off_f32, n_b16):
            assert 2 * off_f32 + n_b16 <= 2 * ARENA, ("b16 overflow", off_f32, n_b16)
            return ARB[:, 2 * off_f32:2 * off_f32 + n_b16]

        class Bump:
            def __init__(self, start):
                self.off = start
            def f(self, n):
                o = self.off; self.off += n
                assert self.off <= ARENA, ("bump overflow", self.off)
                return f32(o, n)
            def b(self, n_b16):
                n = (n_b16 + 1) // 2
                o = self.off; self.off += n
                assert self.off <= ARENA, ("bump overflow b16", self.off)
                return b16(o, n_b16)

        st_ = Bump(0)
        idf = st_.f(128)
        idb = st_.b(128)
        rowst_full = st_.f(512); rowst = rowst_full[0:1, :]
        rowst2 = st_.f(512)[0:1, :]
        rowst3 = st_.f(512)[0:1, :]
        ones_blk = st_.f(128)
        ones_col = ones_blk[:, 0:1]
        ones_row = ones_blk[0:1, :]
        scal = st_.f(64)
        sm = lambda i: scal[:, i:i + 1]
        (s_s1, s_ss, s_m, s_mm2, s_var, s_rstd, s_amax, s_sx, s_zr, s_swg,
         s_hfg, s_swu, s_hfu, s_swd, s_hfd, s_ms, s_rstd2, s_rm) = [sm(i) for i in range(18)]
        s_invs2 = [[sm(18), sm(19)], [sm(28), sm(29)], [sm(55), sm(56)], [sm(57), sm(58)]]
        s_scld = [sm(20), sm(21)]
        s_t11 = sm(22)
        c_eps5, c_eps6, c_mhalf, c_phalf, c_expb = sm(23), sm(24), sm(25), sm(26), sm(27)
        sp_s1 = [sm(30), sm(31)]; sp_ss = [sm(32), sm(33)]
        sp_m = [sm(34), sm(35)]; sp_mm2 = [sm(36), sm(37)]
        sp_var = [sm(38), sm(39)]; sp_rstd = [sm(40), sm(41)]
        sp_amax = [sm(42), sm(43)]; sp_sx = [sm(44), sm(45)]
        sp_ms = [sm(46), sm(47)]; sp_rstd2 = [sm(48), sm(49)]
        sp_rm = [sm(50), sm(51)]; sp_sx2 = [sm(52), sm(53)]
        c_m128 = sm(54)
        bq_sb = st_.f(EC); bk_sb = st_.f(EC)
        cc_sb = st_.f(EC); scc_sb = st_.f(EC)
        invsx = st_.f(TC); sclg = st_.f(TC); sclu = st_.f(TC)
        parts = [st_.f(128) for _ in range(3)]
        PH = st_.off

        ps = [es.enter_context(nc.psum_tensor(f"ps{i}", [128, 512], F32)) for i in range(8)]
        psb = ps[7].bitcast(BF16)


        def ladder_sd(src_ap, scale_ap, dst_ap, width, ytiles, rtiles, stiles, sw):
            strips = [(o, min(sw, width - o)) for o in range(0, width, sw)]
            t_last = None
            for g0 in range(0, len(strips), 2):
                grp = strips[g0:g0 + 2]
                ydv = []
                for i, (o, w) in enumerate(grp):
                    ydv.append(S.v(lambda e, d=ytiles[i][:, 0:w], s0=src_ap[:, o:o + w],
                                   sc=scale_ap: e.tensor_scalar(
                        out=d, in0=s0, scalar1=sc, scalar2=128.5,
                        op0=ALU.mult, op1=ALU.add), inc=True))
                rdv = list(ydv)
                rsrc = [ytiles[i] for i in range(len(grp))]
                for li, t2 in enumerate((128.0, 64.0, 32.0, 16.0, 8.0, 4.0, 2.0, 1.0)):
                    sac = []
                    for i, (o, w) in enumerate(grp):
                        S.wait("scalar", "dv", rdv[i])
                        sac.append(S.s(lambda e, d=stiles[i][:, 0:w],
                                       s0=rsrc[i][:, 0:w]: e.activation(
                            out=d, in_=s0, func=AF.Sign, bias=c_m128), inc=True))
                    for i, (o, w) in enumerate(grp):
                        S.wait("vector", "ac", sac[i])
                        rdv[i] = S.v(lambda e, d=rtiles[i][:, 0:w], s0=stiles[i][:, 0:w],
                                     s1=rsrc[i][:, 0:w], tv=t2 / 2: e.scalar_tensor_tensor(
                            out=d, in0=s0, scalar=-tv, in1=s1,
                            op0=ALU.mult, op1=ALU.add), inc=True)
                    rsrc = [rtiles[i] for i in range(len(grp))]
                for i, (o, w) in enumerate(grp):
                    t_last = S.v(lambda e, d=dst_ap[:, o:o + w], s0=ytiles[i][:, 0:w],
                                 s1=rtiles[i][:, 0:w]: e.scalar_tensor_tensor(
                        out=d, in0=s0, scalar=-0.5, in1=s1,
                        op0=ALU.add, op1=ALU.subtract), inc=True)
            return t_last

        def fence_sync():
            S.wait("sync", "pe", S.cnt["pe"])
            S.wait("sync", "dv", S.cnt["dv"])
            S.wait("sync", "ac", S.cnt["ac"])
            S.wait_total("sync", sorted(S.sems))

        def bcast(dst, src_row_ap_fn, width, transform, st_guard=None):
            last = None
            seg = min(512, width)
            rw = rowst_full[0:1, 0:seg]
            rw2 = rowst2[0:1, 0:seg]
            rw3 = rowst3[0:1, 0:seg]
            for sl0 in range(width // seg):
                if st_guard:
                    S.wait_total("sync", st_guard)
                t_l = S.load(rw, src_row_ap_fn(sl0 * seg, seg), "row")
                t_r = None
                if transform == "add1":
                    S.waitp("vector", t_l)
                    t_r = S.v(lambda e: e.tensor_scalar(
                        out=rw, in0=rw, scalar1=1.0, scalar2=None, op0=ALU.add), inc=True)
                elif transform == "signclamp":
                    S.waitp("vector", t_l)
                    S.v(lambda e: e.tensor_scalar(
                        out=rw2, in0=rw, scalar1=0.0, scalar2=None, op0=ALU.is_ge))
                    S.v(lambda e: e.tensor_scalar(
                        out=rw2, in0=rw2, scalar1=2.0, scalar2=-1.0,
                        op0=ALU.mult, op1=ALU.add))
                    S.v(lambda e: e.tensor_scalar(
                        out=rw3, in0=rw, scalar1=-1.0, scalar2=None, op0=ALU.mult))
                    S.v(lambda e: e.tensor_tensor(out=rw, in0=rw, in1=rw3, op=ALU.max))
                    S.v(lambda e: e.tensor_scalar(
                        out=rw, in0=rw, scalar1=0.25, scalar2=None, op0=ALU.max))
                    t_r = S.v(lambda e: e.tensor_tensor(
                        out=rw, in0=rw, in1=rw2, op=ALU.mult), inc=True)
                if t_r:
                    S.wait("tensor", "dv", t_r)
                else:
                    S.waitp("tensor", t_l)
                S.waitp("tensor", S.ps6)
                t_mm = S.pe(lambda e: e.matmul(out=ps[6][:, 0:seg], lhsT=ones_row,
                                               rhs=rw, start=True, stop=True), inc=True)
                S.wait("scalar", "pe", t_mm)
                t_e = S.s(lambda e, d=dst[:, sl0 * seg:(sl0 + 1) * seg]: e.activation(
                    out=d, in_=ps[6][:, 0:seg], func=AF.Copy), inc=True)
                S.ps6 = ("ac", t_e)
                S.wait("sync", "pe", t_mm)
                last = t_e
            return last

        S.load(idf, id_in[:, :], "ini")
        S.load(bq_sb, bq_c[:, :], "ini"); S.load(bk_sb, bk_c[:, :], "ini")
        t_cc = S.load(cc_sb, c_col[:, :], "ini")
        S.waitp("vector", t_cc)
        S.v(lambda e: e.memset(ones_blk, 1.0))
        S.v(lambda e: e.memset(c_eps5, 1e-5))
        S.v(lambda e: e.memset(c_eps6, 1e-6))
        S.v(lambda e: e.memset(c_mhalf, -0.5))
        S.v(lambda e: e.memset(c_phalf, 0.5))
        S.v(lambda e: e.memset(c_expb, EXP_BIAS))
        S.v(lambda e: e.memset(c_m128, -128.0))
        S.v(lambda e: e.tensor_copy(out=idb, in_=idf), inc=True)
        S.waitp("scalar", t_cc)
        t_silu = S.s(lambda e: e.activation(out=scc_sb, in_=cc_sb, func=AF.Silu), inc=True)

        p0 = Bump(PH)
        wch = [p0.f(6 * E), p0.f(6 * E)]
        acc = p0.f(6 * E)
        modst = [f32(PH, 512)[0:1, :], f32(PH + 512, 512)[0:1, :]]
        use_t = [None, None]
        for ec in range(EC):
            sl = ec % 2
            if use_t[sl]:
                S.wait("sync", "dv", use_t[sl])
            t_w = S.load(wch[sl], wmod[ec * 128:(ec + 1) * 128, :], f"wch{sl}")
            S.waitp("vector", t_w)
            if ec == 0:
                S.wait("vector", "ac", t_silu)
                use_t[sl] = S.v(lambda e, w=wch[sl]: e.tensor_scalar(
                    out=acc, in0=w, scalar1=scc_sb[:, 0:1], scalar2=None,
                    op0=ALU.mult), inc=True)
            else:
                use_t[sl] = S.v(lambda e, w=wch[sl], c=ec: e.scalar_tensor_tensor(
                    out=acc, in0=w, scalar=scc_sb[:, c:c + 1], in1=acc,
                    op0=ALU.mult, op1=ALU.add), inc=True)
        t_acc = use_t[(EC - 1) % 2]
        S.wait("tensor", "dv", t_acc)
        mst_use = [None, None]
        t_cp = None
        for nt in range(6 * E // 512):
            sl = nt % 2
            if t_cp:
                S.wait("tensor", "dv", t_cp)
            t_mm = S.pe(lambda e, n=nt: e.matmul(
                out=ps[0][:, 0:512], lhsT=ones_blk,
                rhs=acc[:, n * 512:(n + 1) * 512], start=True, stop=True), inc=True)
            if mst_use[sl]:
                S.waitp("vector", mst_use[sl])
            S.wait("vector", "pe", t_mm)
            t_cp = S.v(lambda e, d=modst[sl]: e.tensor_copy(
                out=d, in_=ps[0][0:1, 0:512]), inc=True)
            S.wait("gpsimd", "dv", t_cp)
            mst_use[sl] = S.store(mod_sc[0:1, nt * 512:(nt + 1) * 512], modst[sl], f"mo{sl}")
        mod_st_done = ["mo0", "mo1"]

        def modrow(g):
            return lambda o, n, gg=g: mod_sc[0:1, gg * E + o: gg * E + o + n]

        p0b = Bump(PH)
        hF = p0b.f(EC * T)
        O_AFTER_HF = p0b.off
        bc0 = p0b.f(E); bc1 = p0b.f(E)
        xt = [p0b.f(E), p0b.f(E)]
        h1t = p0b.f(E)
        t_b0 = bcast(bc0, modrow(1), E, "add1", st_guard=mod_st_done)
        t_b1 = bcast(bc1, modrow(0), E, None, st_guard=mod_st_done)
        xt_use = [None, None]
        tr_use = [None, None]
        last_hF_evac = None
        for tt in range(TC):
            sl = tt % 2
            if xt_use[sl]:
                S.wait("sync", "dv", xt_use[sl])
            t_x = S.load(xt[sl], x_in[tt * 128:(tt + 1) * 128, :], f"xt{sl}")
            S.waitp("vector", t_x)
            S.v(lambda e, x=xt[sl]: e.reduce_sum(out=s_s1, in_=x, axis=AX.X))
            S.waitp("scalar", t_x)
            t_sq = S.s(lambda e, x=xt[sl]: e.activation(
                out=h1t, in_=x, func=AF.Square, accum_out=s_ss), inc=True)
            S.v(lambda e: e.tensor_scalar(out=s_m, in0=s_s1, scalar1=1.0 / E,
                                          scalar2=None, op0=ALU.mult))
            S.v(lambda e: e.tensor_tensor(out=s_mm2, in0=s_m, in1=s_m, op=ALU.mult))
            S.wait("vector", "ac", t_sq)
            t_var = S.v(lambda e: e.scalar_tensor_tensor(
                out=s_var, in0=s_ss, scalar=1.0 / E, in1=s_mm2,
                op0=ALU.mult, op1=ALU.subtract), inc=True)
            S.wait("scalar", "dv", t_var)
            t_rstd = S.s(lambda e: e.activation(
                out=s_rstd, in_=s_var, func=AF.Sqrt, bias=c_eps5), inc=True)
            S.wait("vector", "ac", t_rstd)
            S.v(lambda e: e.reciprocal(out=s_rstd, in_=s_rstd))
            if tt == 0:
                S.wait("vector", "ac", t_b0)
                S.wait("vector", "ac", t_b1)
            xt_use[sl] = S.v(lambda e, x=xt[sl]: e.tensor_scalar(
                out=h1t, in0=x, scalar1=s_m, scalar2=s_rstd,
                op0=ALU.subtract, op1=ALU.mult), inc=True)
            S.v(lambda e: e.tensor_tensor(out=h1t, in0=h1t, in1=bc0, op=ALU.mult))
            t_h1 = S.v(lambda e: e.tensor_tensor(out=h1t, in0=h1t, in1=bc1, op=ALU.add), inc=True)
            S.wait("tensor", "dv", t_h1)
            for ec in range(EC):
                bk = ec % 2
                if tr_use[bk]:
                    S.wait("tensor", "ac", tr_use[bk])
                t_tr = S.pe(lambda e, c=ec, b=bk: e.transpose(
                    out=ps[1 + b][:, 0:128], in_=h1t[:, c * 128:(c + 1) * 128],
                    identity=idf), inc=True)
                S.wait("scalar", "pe", t_tr)
                tr_use[bk] = S.s(lambda e, c=ec, b=bk, t=tt: e.activation(
                    out=hF[:, c * T + t * 128: c * T + t * 128 + 128],
                    in_=ps[1 + b][:, 0:128], func=AF.Copy), inc=True)
            last_hF_evac = tr_use[(EC - 1) % 2]

        p1 = Bump(O_AFTER_HF)
        chi = p1.f(T); clo = p1.f(T); shi = p1.f(T); slo = p1.f(T)
        wt = [p1.f(E), p1.f(E)]
        qkst = [p1.f(512), p1.f(512)]
        swpst = [p1.f(512), p1.f(512)]
        rs1 = [p1.f(512), p1.f(512)]
        rs2 = p1.f(512)
        fence_sync()
        S.load(chi, chi_in[:, :], "tab"); S.load(clo, clo_in[:, :], "tab")
        S.load(shi, shi_in[:, :], "tab")
        t_tab = S.load(slo, slo_in[:, :], "tab")
        S.waitp("vector", t_tab)
        S.wait("tensor", "ac", last_hF_evac)
        wt_use = [None, None]
        st_use = [None, None]
        sw_use = [None, None]
        rs_store = [None, None]
        psA_use = [None, None]
        for (wdram, bias_sb, dst_sc) in ((wq_t, bq_sb, q_sc), (wk_t, bk_sb, k_sc)):
            for ft in range(EC):
                sl = ft % 2
                if wt_use[sl]:
                    S.wait("sync", "pe", wt_use[sl])
                t_w = S.load(wt[sl], wdram[ft * 128:(ft + 1) * 128, :], f"wt{sl}")
                S.waitp("tensor", t_w)
                for tq in range(QT):
                    bk = tq % 2
                    pA = ps[0] if bk == 0 else ps[3]
                    S.waitp("tensor", psA_use[bk])
                    for ec in range(EC):
                        t_mm = S.pe(lambda e, c=ec, q=tq, w=wt[sl], p=pA: e.matmul(
                            out=p[:, 0:512], lhsT=w[:, c * 128:(c + 1) * 128],
                            rhs=hF[:, c * T + q * 512: c * T + (q + 1) * 512],
                            start=(c == 0), stop=(c == EC - 1),
                            skip_group_check=True), inc=(ec == EC - 1))
                    wt_use[sl] = t_mm
                    ssl = (ft * QT + tq) % 2
                    if st_use[ssl]:
                        S.wait("scalar", "dv", st_use[ssl])
                    S.wait("scalar", "pe", t_mm)
                    t_ev = S.s(lambda e, f=ft, d=qkst[ssl], p=pA, b=bias_sb: e.activation(
                        out=d, in_=p[:, 0:512], func=AF.Identity, bias=b[:, f:f + 1]), inc=True)
                    psA_use[bk] = ("ac", t_ev)
                    if sw_use[ssl]:
                        S.wait("gpsimd", "dv", sw_use[ssl])
                    S.wait("gpsimd", "ac", t_ev)
                    S.store(swpst[ssl][0:64, :], qkst[ssl][64:128, :], f"sw{ssl}")
                    t_sw = S.store(swpst[ssl][64:128, :], qkst[ssl][0:64, :], f"sw{ssl}")
                    if rs_store[ssl]:
                        S.waitp("vector", rs_store[ssl])
                    S.waitp("vector", t_sw)
                    c0, c1 = tq * 512, (tq + 1) * 512
                    r1 = rs1[ssl]
                    S.v(lambda e, s=qkst[ssl], a=c0, b2=c1, r=r1: e.tensor_tensor(
                        out=r, in0=s, in1=chi[:, a:b2], op=ALU.mult))
                    S.v(lambda e, s=qkst[ssl], a=c0, b2=c1: e.tensor_tensor(
                        out=rs2, in0=s, in1=clo[:, a:b2], op=ALU.mult))
                    st_use[ssl] = S.v(lambda e, r=r1: e.tensor_tensor(
                        out=r, in0=r, in1=rs2, op=ALU.add), inc=True)
                    S.v(lambda e, s=swpst[ssl], a=c0, b2=c1: e.tensor_tensor(
                        out=rs2, in0=s, in1=shi[:, a:b2], op=ALU.mult))
                    S.v(lambda e, r=r1: e.tensor_tensor(out=r, in0=r, in1=rs2, op=ALU.add))
                    sw_use[ssl] = S.v(lambda e, s=swpst[ssl], a=c0, b2=c1: e.tensor_tensor(
                        out=rs2, in0=s, in1=slo[:, a:b2], op=ALU.mult), inc=True)
                    t_rot = S.v(lambda e, r=r1: e.tensor_tensor(
                        out=r, in0=r, in1=rs2, op=ALU.add), inc=True)
                    S.wait("gpsimd", "dv", t_rot)
                    rs_store[ssl] = S.store(
                        dst_sc[ft * 128:(ft + 1) * 128, tq * 512:(tq + 1) * 512], r1, f"rq{ssl}")

        pv = Bump(O_AFTER_HF)
        vsl = pv.f(EC * NV)
        vst = [pv.f(NV), pv.f(NV)]
        bvB = pv.f(E)
        fence_sync()
        t_bv = bcast(bvB, lambda o, n: bv_row[0:1, o:o + n], E, None)
        vst_use = [None, None]
        last_vmm = None
        for fs in range(FSV):
            if last_vmm:
                S.wait("sync", "pe", last_vmm)
            t_v = S.load(vsl, wv_r[fs * 128:(fs + 1) * 128, :], "vsl")
            S.waitp("tensor", t_v)
            for tt in range(TC):
                bk = tt % 2
                pA = ps[0] if bk == 0 else ps[3]
                S.waitp("tensor", psA_use[bk])
                for ec in range(EC):
                    t_mm = S.pe(lambda e, c=ec, t=tt, p=pA: e.matmul(
                        out=p[:, 0:NV], lhsT=hF[:, c * T + t * 128: c * T + t * 128 + 128],
                        rhs=vsl[:, c * NV:(c + 1) * NV], start=(c == 0), stop=(c == EC - 1),
                        skip_group_check=True), inc=(ec == EC - 1))
                last_vmm = t_mm
                ssl = tt % 2
                if vst_use[ssl]:
                    S.waitp("vector", vst_use[ssl])
                S.wait("vector", "pe", t_mm)
                if fs == 0 and tt == 0:
                    S.wait("vector", "ac", t_bv)
                t_vv = S.v(lambda e, d=vst[ssl], p=pA, f=fs: e.tensor_tensor(
                    out=d, in0=p[:, 0:NV], in1=bvB[:, f * NV:(f + 1) * NV],
                    op=ALU.add), inc=True)
                psA_use[bk] = ("dv", t_vv)
                S.wait("gpsimd", "dv", t_vv)
                for hh in range(NV // 128):
                    h = fs * (NV // 128) + hh
                    vst_use[ssl] = S.store(
                        v_sc[h * T + tt * 128: h * T + (tt + 1) * 128, :],
                        vst[ssl][:, hh * 128:(hh + 1) * 128], f"vs{ssl}")
        spill_done = ["rq0", "rq1", "vs0", "vs1"]

        p2 = Bump(O_AFTER_HF)
        qrot = [p2.f(T), p2.f(T)]
        krot = [p2.f(T), p2.f(T)]
        VRW = TC * 129
        vrhs = [p2.f(VRW + 7), p2.f(VRW + 7)]
        pT = [p2.f(512), p2.f(512), p2.f(512)]
        yst = [p2.f(128), p2.f(128)]
        masks_t = p2.f(4 * 512)
        masks = [masks_t[:, i * 512:(i + 1) * 512] for i in range(4)]
        fence_sync()
        t_mload = S.load(masks_t, mask_in[:, :], "msk")
        t_ones = None
        for vb in range(2):
            for c in range(TC):
                t_ones = S.v(lambda e, b=vb, cc=c: e.memset(
                    vrhs[b][:, cc * 129 + 128: cc * 129 + 129], 1.0), inc=True)
        S.wait_total("sync", spill_done)
        S.wait("sync", "dv", t_ones)

        head_use_pe = [None, None]
        ptslot_use = [None, None, None]
        yst_use = [None, None]
        ptr_use = None
        av_dv = [None] * 4
        first_mask_wait = True
        yF_done = None
        for h in range(H):
            hb = h % 2
            if head_use_pe[hb]:
                S.wait("sync", "pe", head_use_pe[hb])
            S.load(qrot[hb], q_sc[h * 128:(h + 1) * 128, :], f"hd{hb}")
            S.load(krot[hb], k_sc[h * 128:(h + 1) * 128, :], f"hd{hb}")
            for c in range(TC):
                t_v = S.load(vrhs[hb][:, c * 129:c * 129 + 128],
                             v_sc[h * T + c * 128: h * T + (c + 1) * 128, :], f"hd{hb}")
            S.waitp("tensor", t_v)
            exp_done = {}
            avlast = {}
            SBANKS = (0, 1, 7)
            for j in range(QT):
                nact = min(TC, 4 * (j + 1))
                for c0 in range(0, nact, 3):
                    batch = list(range(c0, min(c0 + 3, nact)))
                    for c in batch:
                        sbk = c % 3
                        pS = ps[SBANKS[sbk]]
                        if ptslot_use[sbk]:
                            S.wait("tensor", "ac", exp_done.get(c - 3, 0) or 0)
                        t_sc2 = S.pe(lambda e, cc=c, jj=j, p=pS, b=hb: e.matmul(
                            out=p[:, 0:512], lhsT=krot[b][:, cc * 128:(cc + 1) * 128],
                            rhs=qrot[b][:, jj * 512:(jj + 1) * 512],
                            start=True, stop=True, skip_group_check=True), inc=True)
                        diag = c >= 4 * j
                        if diag:
                            if first_mask_wait:
                                S.waitp("vector", t_mload)
                                first_mask_wait = False
                            S.wait("vector", "pe", t_sc2)
                            t_mk = S.v(lambda e, p=pS, r=c - 4 * j: e.tensor_tensor(
                                out=p[:, 0:512], in0=p[:, 0:512], in1=masks[r],
                                op=ALU.add), inc=True)
                            S.wait("scalar", "dv", t_mk)
                        else:
                            S.wait("scalar", "pe", t_sc2)
                        if ptslot_use[sbk]:
                            S.wait("scalar", "pe", ptslot_use[sbk])
                        t_exp = S.s(lambda e, p=pS, d=pT[sbk]: e.activation(
                            out=d, in_=p[:, 0:512], func=AF.Exp, bias=c_expb), inc=True)
                        exp_done[c] = t_exp
                    S.wait("tensor", "ac", exp_done[batch[-1]])
                    for c in batch:
                        sbk = c % 3
                        last = None
                        for qs in range(4):
                            qq = 4 * j + qs
                            if c > qq:
                                continue
                            if c == 0 and av_dv[qs] is not None:
                                S.wait("tensor", "dv", av_dv[qs])
                            last = S.pe(lambda e, cc=c, q=qs, b=hb, qqv=qq, sb2=sbk: e.matmul(
                                out=ps[2 + q][:, 0:129], lhsT=pT[sb2][:, q * 128:(q + 1) * 128],
                                rhs=vrhs[b][:, cc * 129: cc * 129 + 129],
                                start=(cc == 0), stop=(cc == qqv),
                                skip_group_check=True), inc=True)
                        avlast[c] = last
                        ptslot_use[sbk] = last
                for qs in range(4):
                    ssl = qs % 2
                    S.wait("vector", "pe", avlast[4 * j + qs])
                    if yst_use[ssl]:
                        S.wait("vector", "pe", yst_use[ssl])
                    S.v(lambda e, q=qs: e.reciprocal(out=s_zr, in_=ps[2 + q][:, 128:129]))
                    av_dv[qs] = S.v(lambda e, q=qs, d=yst[ssl]: e.tensor_scalar(
                        out=d, in0=ps[2 + q][:, 0:128], scalar1=s_zr, scalar2=None,
                        op0=ALU.mult), inc=True)
                    S.wait("tensor", "dv", av_dv[qs])
                    if ptr_use:
                        S.wait("tensor", "ac", ptr_use)
                    S.waitp("tensor", S.ps6)
                    yst_use[ssl] = S.pe(lambda e, d=yst[ssl]: e.transpose(
                        out=ps[6][:, 0:128], in_=d, identity=idf), inc=True)
                    S.wait("scalar", "pe", yst_use[ssl])
                    ptr_use = S.s(lambda e, hh2=h, qq2=4 * j + qs: e.activation(
                        out=hF[:, hh2 * T + qq2 * 128: hh2 * T + qq2 * 128 + 128],
                        in_=ps[6][:, 0:128], func=AF.Copy), inc=True)
                    S.ps6 = ("ac", ptr_use)
            head_use_pe[hb] = avlast[nact - 1]
        yF_done = ptr_use

        p3 = Bump(O_AFTER_HF)
        bcA = p3.f(E); bcB = p3.f(E)
        wt2 = [p3.f(EC * N2), p3.f(EC * N2)]
        xinst = [p3.f(N2), p3.f(N2)]
        x2st = [p3.f(N2), p3.f(N2)]
        stst = [p3.f(1024), p3.f(1024)]
        stdmp = p3.f(1024)
        S.wait("tensor", "ac", yF_done)
        fence_sync()
        t_bo = bcast(bcA, lambda o, n: bo_row[0:1, o:o + n], E, None)
        t_ga = bcast(bcB, modrow(2), E, "signclamp", st_guard=mod_st_done)

        stats_jobs = []
        nch_m = [0, 0, 0]
        for mi, (wdram, rows, cols) in enumerate(((wg_r, JS * 128, EC * 512),
                                                  (wu_r, JS * 128, EC * 512),
                                                  (wd_t, I, E))):
            ccw = min(1024, cols)
            cpr = cols // ccw
            nch_m[mi] = (rows // 128) * cpr
            for ch in range(nch_m[mi]):
                stats_jobs.append((mi, wdram, (ch // cpr) * 128, (ch % cpr) * ccw, ch, ccw))
        st_slot_use = [None, None]
        nstats = len(stats_jobs)

        def emit_stats(idx):
            mi, wdram, r0, c0, ch, ccw = stats_jobs[idx]
            sl = idx % 2
            if st_slot_use[sl]:
                S.wait("sync", "ac", st_slot_use[sl])
            t_l = S.load(stst[sl][:, 0:ccw], wdram[r0:r0 + 128, c0:c0 + ccw], f"ss{sl}")
            S.waitp("scalar", t_l)
            st_slot_use[sl] = S.s(lambda e, s=stst[sl][:, 0:ccw], m=mi, p=ch, w=ccw: e.activation(
                out=stdmp[:, 0:w], in_=s, func=AF.Abs, accum_out=parts[m][:, p:p + 1]), inc=True)
            return st_slot_use[sl]

        wt2_use = [None, None]
        x2_dv = [None, None]
        x2st_st = [None, None]
        si = 0
        njobs_per = max(1, (nstats + FS2 * TC - 1) // (FS2 * TC))
        last_stats = None
        for fs in range(FS2):
            sl = fs % 2
            if wt2_use[sl]:
                S.wait("sync", "pe", wt2_use[sl])
            t_w = S.load(wt2[sl], wo_r[fs * 128:(fs + 1) * 128, :], f"wo{sl}")
            S.waitp("tensor", t_w)
            for tt in range(TC):
                bk = tt % 2
                pA = ps[0] if bk == 0 else ps[1]
                if x2_dv[bk]:
                    S.wait("tensor", "dv", x2_dv[bk])
                for ec in range(EC):
                    t_mm = S.pe(lambda e, c=ec, t=tt, p=pA, w=wt2[sl]: e.matmul(
                        out=p[:, 0:N2], lhsT=hF[:, c * T + t * 128: c * T + t * 128 + 128],
                        rhs=w[:, c * N2:(c + 1) * N2], start=(c == 0), stop=(c == EC - 1),
                        skip_group_check=True), inc=(ec == EC - 1))
                wt2_use[sl] = t_mm
                ssl = tt % 2
                if x2_dv[bk]:
                    S.wait("sync", "dv", x2_dv[bk])
                t_xl = S.load(xinst[ssl], x_in[tt * 128:(tt + 1) * 128, fs * N2:(fs + 1) * N2], f"xi{ssl}")
                if x2st_st[ssl]:
                    S.waitp("vector", x2st_st[ssl])
                S.wait("vector", "pe", t_mm)
                S.waitp("vector", t_xl)
                if fs == 0 and tt == 0:
                    S.wait("vector", "ac", t_bo)
                    S.wait("vector", "ac", t_ga)
                S.v(lambda e, p=pA, d=x2st[ssl], f=fs: e.tensor_tensor(
                    out=d, in0=p[:, 0:N2], in1=bcA[:, f * N2:(f + 1) * N2], op=ALU.add))
                S.v(lambda e, d=x2st[ssl], f=fs: e.tensor_tensor(
                    out=d, in0=d, in1=bcB[:, f * N2:(f + 1) * N2], op=ALU.mult))
                x2_dv[bk] = S.v(lambda e, d=x2st[ssl], x=xinst[ssl]: e.tensor_tensor(
                    out=d, in0=d, in1=x, op=ALU.add), inc=True)
                S.wait("gpsimd", "dv", x2_dv[bk])
                x2st_st[ssl] = S.store(
                    x2_sc[tt * 128:(tt + 1) * 128, fs * N2:(fs + 1) * N2], x2st[ssl], f"xs{ssl}")
                for _ in range(njobs_per):
                    if si < nstats:
                        last_stats = emit_stats(si); si += 1
        while si < nstats:
            last_stats = emit_stats(si); si += 1
        x2_done = ["xs0", "xs1"]

        S.wait("vector", "ac", last_stats)
        t_rc = None
        for mi, (s_sw, s_hf) in enumerate(((s_swg, s_hfg), (s_swu, s_hfu), (s_swd, s_hfd))):
            t_cs = S.v(lambda e, m=mi: e.reduce_sum(
                out=s_zr, in_=parts[m][:, 0:nch_m[m]], axis=AX.X), inc=True)
            S.wait("tensor", "dv", t_cs)
            S.waitp("tensor", S.ps6)
            t_mm = S.pe(lambda e: e.matmul(out=ps[6][:, 0:1], lhsT=ones_blk,
                                           rhs=s_zr, start=True, stop=True), inc=True)
            S.wait("vector", "pe", t_mm)
            S.v(lambda e, d=s_sw: e.tensor_scalar(
                out=d, in0=ps[6][:, 0:1], scalar1=1.0 / (I * E),
                scalar2=None, op0=ALU.mult))
            t_mn = S.v(lambda e, d=s_sw: e.tensor_scalar(
                out=d, in0=d, scalar1=1e-5, scalar2=None, op0=ALU.max), inc=True)
            S.ps6 = ("dv", t_mn)
            S.v(lambda e, d=s_hf, s0=s_sw: e.tensor_scalar(
                out=d, in0=s0, scalar1=0.5, scalar2=None, op0=ALU.mult))
            t_rc = S.v(lambda e, d=s_sw: e.reciprocal(out=d, in_=d), inc=True)
        stats_done = t_rc

        p3b = Bump(PH)
        TH2 = T // 2
        qxFh = [p3b.b(EC * TH2), p3b.b(EC * TH2)]
        O_QXF_H1 = PH + (EC * TH2) // 2
        O_AFTER_QXF = p3b.off
        x2t = [p3b.f(E), p3b.f(E)]; h2t = [p3b.f(E), p3b.f(E)]
        LW3 = min(1024, E // 2)
        lad_y = p3b.f(LW3); lad_r = p3b.f(LW3); lad_b = p3b.f(LW3)
        lad_y2 = p3b.f(LW3); lad_r2 = p3b.f(LW3); lad_b2 = p3b.f(LW3)
        sqdump = p3b.f(E)
        bcC = p3b.f(E); bcD = p3b.f(E)
        wdst = [p3b.f(E), p3b.f(E)]
        ternb = [p3b.b(E), p3b.b(E)]
        scrd = p3b.b(E)
        fence_sync()
        t_cb = bcast(bcC, modrow(4), E, "add1", st_guard=mod_st_done)
        t_sb = bcast(bcD, modrow(3), E, "none")
        S.wait_total("sync", x2_done)

        wd_use = [None, None]
        tb_use = [None, None]
        tern_st = [None, None]

        def emit_tern(ic):
            sl = ic % 2
            if wd_use[sl]:
                S.wait("sync", "ac", wd_use[sl])
            t_l = S.load(wdst[sl], wd_t[ic * 128:(ic + 1) * 128, :], f"wd{sl}")
            S.waitp("scalar", t_l)
            if ic == 0:
                S.wait("scalar", "dv", stats_done)
            if tern_st[sl]:
                S.waitp("scalar", tern_st[sl])
            for g in tb_use:
                if g:
                    S.wait("scalar", "dv", g)
            S.s(lambda e, s=wdst[sl], d=ternb[sl]: e.activation(
                out=d, in_=s, func=AF.Sign, scale=s_swd, bias=c_mhalf))
            wd_use[sl] = S.s(lambda e, s=wdst[sl]: e.activation(
                out=scrd, in_=s, func=AF.Sign, scale=s_swd, bias=c_phalf), inc=True)
            S.wait("vector", "ac", wd_use[sl])
            tb_use[sl] = S.v(lambda e, d=ternb[sl]: e.tensor_tensor(
                out=d, in0=d, in1=scrd, op=ALU.add), inc=True)
            S.wait("gpsimd", "dv", tb_use[sl])
            tern_st[sl] = S.store(tern_sc[ic * 128:(ic + 1) * 128, :], ternb[sl], f"ts{sl}")

        qx_tr_use = [None, None]
        last_qx_evac = None
        tr_last = [None, None]
        x2t_use = [None, None]
        sq_done = [None, None]
        ld_done = [None, None]
        ic_done = 0

        def p3b_stage1(tt):
            p = tt % 2
            x2s = x2t[p]
            if x2t_use[p]:
                S.wait("sync", "dv", x2t_use[p])
            t_x2 = S.load(x2s, x2_sc[tt * 128:(tt + 1) * 128, :], f"x2t{p}")
            ld_done[p] = t_x2
            S.waitp("scalar", t_x2)
            sq_done[p] = S.s(lambda e, x=x2s, p2=p: e.activation(
                out=sqdump, in_=x, func=AF.Square, accum_out=sp_ss[p2]), inc=True)

        def p3b_stage2(tt):
            p = tt % 2
            x2s = x2t[p]
            h2s = h2t[p]
            S.waitp("vector", ld_done[p])
            S.v(lambda e, x=x2s, p2=p: e.reduce_sum(out=sp_s1[p2], in_=x, axis=AX.X))
            S.v(lambda e, p2=p: e.tensor_scalar(out=sp_m[p2], in0=sp_s1[p2],
                scalar1=1.0 / E, scalar2=None, op0=ALU.mult))
            S.v(lambda e, p2=p: e.tensor_tensor(out=sp_mm2[p2], in0=sp_m[p2],
                in1=sp_m[p2], op=ALU.mult))
            S.wait("vector", "ac", sq_done[p])
            t_var = S.v(lambda e, p2=p: e.scalar_tensor_tensor(
                out=sp_var[p2], in0=sp_ss[p2], scalar=1.0 / E, in1=sp_mm2[p2],
                op0=ALU.mult, op1=ALU.subtract), inc=True)
            S.wait("scalar", "dv", t_var)
            t_rstd = S.s(lambda e, p2=p: e.activation(
                out=sp_rstd[p2], in_=sp_var[p2], func=AF.Sqrt, bias=c_eps5), inc=True)
            S.wait("vector", "ac", t_rstd)
            S.v(lambda e, p2=p: e.reciprocal(out=sp_rstd[p2], in_=sp_rstd[p2]))
            if tt == 0:
                S.wait("vector", "ac", t_cb)
                S.wait("vector", "ac", t_sb)
            if tr_last[p]:
                S.wait("vector", "pe", tr_last[p])
            x2t_use[p] = S.v(lambda e, x=x2s, d=h2s, p2=p: e.tensor_scalar(
                out=d, in0=x, scalar1=sp_m[p2], scalar2=sp_rstd[p2],
                op0=ALU.subtract, op1=ALU.mult), inc=True)
            S.v(lambda e, d=h2s: e.tensor_tensor(out=d, in0=d, in1=bcC, op=ALU.mult))
            S.v(lambda e, d=h2s: e.tensor_tensor(out=d, in0=d, in1=bcD, op=ALU.add))
            S.v(lambda e, d=h2s, p2=p: e.reduce_max(out=sp_amax[p2], in_=d, axis=AX.X,
                                                    apply_absolute_value=True))
            S.v(lambda e, p2=p: e.tensor_scalar(
                out=sp_amax[p2], in0=sp_amax[p2], scalar1=1e-5, scalar2=None, op0=ALU.max))
            S.v(lambda e, t=tt, p2=p: e.tensor_scalar(
                out=invsx[:, t:t + 1], in0=sp_amax[p2], scalar1=1.0 / 127.0,
                scalar2=None, op0=ALU.mult))
            S.v(lambda e, t=tt, p2=p: e.reciprocal(out=sp_sx[p2], in_=invsx[:, t:t + 1]))

        def p3b_ladder(tt):
            nonlocal last_qx_evac, ic_done
            p = tt % 2
            h2s = h2t[p]
            t_lad = ladder_sd(h2s, sp_sx[p], h2s, E,
                              [lad_y, lad_y2], [lad_r, lad_r2], [lad_b, lad_b2], LW3)
            S.wait("tensor", "dv", t_lad)
            for ec in range(EC):
                bsl = ec % 2
                if qx_tr_use[bsl]:
                    S.wait("tensor", "ac", qx_tr_use[bsl])
                t_tr = S.pe(lambda e, c=ec, b=bsl, d=h2s: e.transpose(
                    out=ps[1 + b][:, 0:128], in_=d[:, c * 128:(c + 1) * 128],
                    identity=idf), inc=True)
                tr_last[p] = t_tr
                S.wait("scalar", "pe", t_tr)
                hh_ = tt * 128 // TH2
                tl_ = tt * 128 - hh_ * TH2
                qx_tr_use[bsl] = S.s(lambda e, c=ec, b=bsl, h3=hh_, t3=tl_: e.activation(
                    out=qxFh[h3][:, c * TH2 + t3: c * TH2 + t3 + 128],
                    in_=ps[1 + b][:, 0:128], func=AF.Copy), inc=True)
            last_qx_evac = qx_tr_use[(EC - 1) % 2]
            ntern = (IC * (tt + 1)) // TC
            while ic_done < ntern:
                emit_tern(ic_done); ic_done += 1

        p3b_stage1(0)
        for tt in range(TC):
            if tt + 1 < TC:
                p3b_stage1(tt + 1)
            p3b_stage2(tt)
            p3b_ladder(tt)
        while ic_done < IC:
            emit_tern(ic_done); ic_done += 1
        tern_done = ["ts0", "ts1"]

        t_scl = None
        for tt in range(TC):
            S.v(lambda e, t=tt: e.tensor_tensor(
                out=sclg[:, t:t + 1], in0=invsx[:, t:t + 1], in1=s_hfg, op=ALU.mult))
            t_scl = S.v(lambda e, t=tt: e.tensor_tensor(
                out=sclu[:, t:t + 1], in0=invsx[:, t:t + 1], in1=s_hfu, op=ALU.mult), inc=True)

        p4 = Bump(O_AFTER_QXF)
        O_P4TMP = p4.off
        gst = p4.f(EC * 512)
        scrg = p4.b(EC * 512)
        O_P4TMP_END = p4.off
        ternG = [p4.b(EC * 512), p4.b(EC * 512)]
        ternU = [p4.b(EC * 512), p4.b(EC * 512)]
        gs_t = [p4.f(512), p4.f(512)]
        hst = [p4.f(512), p4.f(512)]
        O_P4_END = p4.off

        fence_sync()
        tG_use = [None, None]; tU_use = [None, None]
        tU_rdy_prev = None
        gst_free = None
        hst_use = [None, None]
        psG_use = [None, None]; psU_use = [None, None]
        TH = TC // 2

        def p4_weights_quant(js):
            nonlocal gst_free, tU_rdy_prev
            sl = js % 2
            if gst_free:
                S.wait("sync", "ac", gst_free)
            if tG_use[sl]:
                S.wait("sync", "pe", tG_use[sl])
            t_g = S.load(gst, wg_r[js * 128:(js + 1) * 128, :], "gst")
            S.waitp("scalar", t_g)
            if js == 0:
                S.wait("scalar", "dv", stats_done)
            if tU_rdy_prev:
                S.wait("scalar", "dv", tU_rdy_prev)
            S.s(lambda e, d=ternG[sl]: e.activation(
                out=d, in_=gst, func=AF.Sign, scale=s_swg, bias=c_mhalf))
            t_sg = S.s(lambda e: e.activation(
                out=scrg, in_=gst, func=AF.Sign, scale=s_swg, bias=c_phalf), inc=True)
            gst_free = t_sg
            S.wait("vector", "ac", t_sg)
            tG_rdy = S.v(lambda e, d=ternG[sl]: e.tensor_tensor(
                out=d, in0=d, in1=scrg, op=ALU.add), inc=True)
            S.wait("gpsimd", "dv", tG_rdy)
            S.store(tg_sc[js * 128:(js + 1) * 128, :], ternG[sl], f"tw{sl}")
            if tU_use[sl]:
                S.wait("sync", "pe", tU_use[sl])
            S.wait("sync", "ac", t_sg)
            t_u = S.load(gst, wu_r[js * 128:(js + 1) * 128, :], "gst")
            S.waitp("scalar", t_u)
            S.wait("scalar", "dv", tG_rdy)
            S.s(lambda e, d=ternU[sl]: e.activation(
                out=d, in_=gst, func=AF.Sign, scale=s_swu, bias=c_mhalf))
            t_su = S.s(lambda e: e.activation(
                out=scrg, in_=gst, func=AF.Sign, scale=s_swu, bias=c_phalf), inc=True)
            gst_free = t_su
            S.wait("vector", "ac", t_su)
            tU_rdy = S.v(lambda e, d=ternU[sl]: e.tensor_tensor(
                out=d, in0=d, in1=scrg, op=ALU.add), inc=True)
            S.wait("gpsimd", "dv", tU_rdy)
            S.store(tu_sc[js * 128:(js + 1) * 128, :], ternU[sl], f"tw{sl}")
            tU_rdy_prev = tU_rdy
            return tU_rdy

        def p4_weights_load(js):
            sl = js % 2
            if tG_use[sl]:
                S.wait("sync", "pe", tG_use[sl])
            S.load(ternG[sl], tg_sc[js * 128:(js + 1) * 128, :], f"tl{sl}")
            if tU_use[sl]:
                S.wait("sync", "pe", tU_use[sl])
            t_u = S.load(ternU[sl], tu_sc[js * 128:(js + 1) * 128, :], f"tl{sl}")
            return t_u

        def p4_mms(js, half, gate_pair):
            sl = js % 2
            if gate_pair[0] == "dv":
                S.wait("tensor", "dv", gate_pair[1])
            else:
                S.waitp("tensor", gate_pair)
            for tt in range(half * TH, (half + 1) * TH):
                bg = tt % 2
                pG = ps[0] if bg == 0 else ps[1]
                pU = ps[2] if bg == 0 else ps[3]
                tl_ = tt * 128 - half * TH2
                S.waitp("tensor", psG_use[bg])
                for ec in range(EC):
                    t_gmm = S.pe(lambda e, c=ec, t3=tl_, p=pG, d=ternG[sl], h3=half: e.matmul(
                        out=p[:, 0:512], lhsT=qxFh[h3][:, c * TH2 + t3: c * TH2 + t3 + 128],
                        rhs=d[:, c * 512:(c + 1) * 512], start=(c == 0), stop=(c == EC - 1),
                        skip_group_check=True), inc=(ec == EC - 1))
                S.waitp("tensor", psU_use[bg])
                for ec in range(EC):
                    t_umm = S.pe(lambda e, c=ec, t3=tl_, p=pU, d=ternU[sl], h3=half: e.matmul(
                        out=p[:, 0:512], lhsT=qxFh[h3][:, c * TH2 + t3: c * TH2 + t3 + 128],
                        rhs=d[:, c * 512:(c + 1) * 512], start=(c == 0), stop=(c == EC - 1),
                        skip_group_check=True), inc=(ec == EC - 1))
                tG_use[sl] = t_gmm; tU_use[sl] = t_umm
                ssl = tt % 2
                S.wait("scalar", "pe", t_gmm)
                if js == 0 and tt == half * TH:
                    S.wait("scalar", "dv", t_scl)
                if psU_use[bg]:
                    S.wait("scalar", "dv", psU_use[bg][1])
                t_si = S.s(lambda e, p=pG, d=gs_t[ssl], t=tt: e.activation(
                    out=d, in_=p[:, 0:512], func=AF.Silu, scale=sclg[:, t:t + 1]), inc=True)
                psG_use[bg] = ("ac", t_si)
                if hst_use[ssl]:
                    S.waitp("vector", hst_use[ssl])
                S.wait("vector", "pe", t_umm)
                S.wait("vector", "ac", t_si)
                t_h = S.v(lambda e, p=pU, d=hst[ssl], g=gs_t[ssl], t=tt: e.scalar_tensor_tensor(
                    out=d, in0=p[:, 0:512], scalar=sclu[:, t:t + 1], in1=g,
                    op0=ALU.mult, op1=ALU.mult), inc=True)
                psU_use[bg] = ("dv", t_h)
                S.wait("gpsimd", "dv", t_h)
                hst_use[ssl] = S.store(
                    h_sc[tt * 128:(tt + 1) * 128, js * 512:(js + 1) * 512], hst[ssl],
                    f"hs{half}{ssl}")

        for js in range(JS):
            if js == 0:
                S.wait("tensor", "ac", last_qx_evac)
            t_rdy = p4_weights_quant(js)
            p4_mms(js, 0, ("dv", t_rdy))
        h0_done = ["hs00", "hs01"]

        class Zones:
            def __init__(self, zones):
                self.bumps = [(Bump(a), b) for a, b in zones]
            def f(self, n):
                for bu, lim in self.bumps:
                    if bu.off + n <= lim:
                        return bu.f(n)
                raise AssertionError(("no zone fits", n))
        LW5 = min(1024, I // 2)
        STRIPN = I // STRIP_I
        BIGCASE = (O_QXF_H1 - PH) >= I
        if BIGCASE:
            zn = Zones([(PH, O_QXF_H1), (O_P4TMP, O_P4TMP_END), (O_P4_END, ARENA)])
            lnwB = zn.f(I)
        else:
            p5d_small = Bump(PH)
            h2t2 = [p5d_small.f(I), p5d_small.f(I)]
            qh2F = p5d_small.b(IC * 256)
            ternD = [p5d_small.b(E), p5d_small.b(E)]
            x2l = p5d_small.f(E)
            xout = [p5d_small.f(E), p5d_small.f(E)]
            gbB = p5d_small.f(E)
            dlB = [p5d_small.f(LW5) for _ in range(6)]
            dmpB = p5d_small.f(STRIP_I)
            sqpB = p5d_small.f(2 * STRIPN)
            zn = Zones([(max(O_P4_END, p5d_small.off), ARENA)])
            lnwB = zn.f(I)
        h2q = zn.f(I)
        dlA = [zn.f(LW5) for _ in range(6)]
        dmpA = zn.f(STRIP_I)
        sqpA = zn.f(2 * STRIPN)

        t_lnw = bcast(lnwB, lambda o, n: lnw_row[0:1, o:o + n], I, None)
        S.wait_total("sync", h0_done)
        S.wait_total("sync", tern_done)
        bufguard = {}
        ternD_use = [None, None]
        tr2_use = [None, None]
        qh2_evac = None
        prev_epi_dv = None
        xout_st = [None, None]
        ld5_done = {}
        sq5_done = {}

        def p5_stage1(b6, sub, buf, sqp, key, dmp):
            tt = 2 * b6 + sub
            g = bufguard.setdefault(key, {})
            if g.get("pe"):
                S.wait("sync", "pe", g["pe"])
            if g.get("dv"):
                S.wait("sync", "dv", g["dv"])
            if g.get("spill"):
                S.wait_total("sync", [g["spill"]])
            t_h = S.load(buf, h_sc[tt * 128:(tt + 1) * 128, :], key)
            ld5_done[key] = t_h
            S.waitp("scalar", t_h)
            if b6 == 0 and sub == 0:
                S.wait("scalar", "ac", t_lnw)
            t_sq = None
            for si in range(STRIPN):
                t_sq = S.s(lambda e, o=si * STRIP_I, p=si, s2=sub, h=buf, d=dmp:
                           e.activation(
                    out=d, in_=h[:, o:o + STRIP_I], func=AF.Square,
                    accum_out=sqp[:, s2 * STRIPN + p: s2 * STRIPN + p + 1]), inc=True)
            sq5_done[key] = t_sq

        def p5_stage2lad(b6, sub, buf, dls, sqp, key, spill):
            tt = 2 * b6 + sub
            g = bufguard.setdefault(key, {})
            S.waitp("vector", ld5_done[key])
            S.wait("vector", "ac", sq5_done[key])
            S.v(lambda e, s2=sub: e.reduce_sum(
                out=sp_ms[s2], in_=sqp[:, s2 * STRIPN:(s2 + 1) * STRIPN], axis=AX.X))
            S.v(lambda e, h=buf: e.tensor_tensor(out=h, in0=h, in1=lnwB, op=ALU.mult))
            S.v(lambda e, h=buf, s2=sub: e.reduce_max(
                out=sp_rm[s2], in_=h, axis=AX.X, apply_absolute_value=True))
            t_ms = S.v(lambda e, s2=sub: e.tensor_scalar(
                out=sp_ms[s2], in0=sp_ms[s2], scalar1=1.0 / I, scalar2=None,
                op0=ALU.mult), inc=True)
            S.wait("scalar", "dv", t_ms)
            t_r2 = S.s(lambda e, s2=sub: e.activation(
                out=sp_rstd2[s2], in_=sp_ms[s2], func=AF.Sqrt, bias=c_eps6), inc=True)
            S.wait("vector", "ac", t_r2)
            S.v(lambda e, s2=sub: e.reciprocal(out=sp_rstd2[s2], in_=sp_rstd2[s2]))
            S.v(lambda e, s2=sub: e.tensor_tensor(
                out=sp_rm[s2], in0=sp_rm[s2], in1=sp_rstd2[s2], op=ALU.mult))
            S.v(lambda e, s2=sub: e.tensor_scalar(
                out=sp_rm[s2], in0=sp_rm[s2], scalar1=1e-5, scalar2=None, op0=ALU.max))
            S.v(lambda e, s2=sub, p=b6 % 4: e.tensor_scalar(
                out=s_invs2[p][s2], in0=sp_rm[s2], scalar1=1.0 / 127.0,
                scalar2=None, op0=ALU.mult))
            S.v(lambda e, s2=sub, p=b6 % 4: e.reciprocal(out=sp_sx2[s2], in_=s_invs2[p][s2]))
            S.v(lambda e, s2=sub: e.tensor_tensor(
                out=sp_sx2[s2], in0=sp_sx2[s2], in1=sp_rstd2[s2], op=ALU.mult))
            t_lad = ladder_sd(buf, sp_sx2[sub], buf, I,
                              [dls[0], dls[1]], [dls[2], dls[3]], [dls[4], dls[5]], LW5)
            g["dv"] = t_lad
            if spill:
                S.wait("gpsimd", "dv", t_lad)
                g["spill"] = S.store(h_sc[tt * 128:(tt + 1) * 128, :], buf, "kqA")[0]
            return t_lad

        NB_H0 = NB6 // 2
        quant_jobs = [(b6, sub) for b6 in range(NB_H0) for sub in range(2)]
        qi = 0
        tw_done = ["tw0", "tw1"]
        S.wait_total("sync", tw_done)
        for js in range(JS):
            t_u = p4_weights_load(js)
            p4_mms(js, 1, t_u)
            tgt = (js + 1) * len(quant_jobs) // JS
            while qi < tgt:
                b6q, subq = quant_jobs[qi]
                p5_stage1(b6q, subq, h2q, sqpA, "h2q", dmpA)
                p5_stage2lad(b6q, subq, h2q, dlA, sqpA, "h2q", spill=True)
                qi += 1
        while qi < len(quant_jobs):
            b6q, subq = quant_jobs[qi]
            p5_stage1(b6q, subq, h2q, sqpA, "h2q", dmpA)
            p5_stage2lad(b6q, subq, h2q, dlA, sqpA, "h2q", spill=True)
            qi += 1
        h1_done = ["hs10", "hs11"]
        S.wait_total("sync", h1_done)

        if BIGCASE:
            p5d = Bump(PH + I)
            h2t2 = [p5d.f(I), p5d.f(I)]
            qh2F = p5d.b(IC * 256)
            ternD = [p5d.b(E), p5d.b(E)]
            x2l = p5d.f(E)
            xout = [p5d.f(E), p5d.f(E)]
            gbB = p5d.f(E)
            dlB = [p5d.f(LW5) for _ in range(6)]
            dmpB = p5d.f(STRIP_I)
            sqpB = p5d.f(2 * STRIPN)
        fence_sync()
        t_gb = bcast(gbB, modrow(5), E, "signclamp", st_guard=mod_st_done)

        def emit_transposes(b6, t_lad, reload):
            nonlocal qh2_evac
            if reload:
                t_rs = []
                for sub in range(2):
                    tt = 2 * b6 + sub
                    key = f"h2r{sub}"
                    g = bufguard.setdefault(key, {})
                    if g.get("pe"):
                        S.wait("sync", "pe", g["pe"])
                    S.wait_total("sync", ["kqA"])
                    t_rs.append(S.load(h2t2[sub], h_sc[tt * 128:(tt + 1) * 128, :], key))
                for t_r in t_rs:
                    S.waitp("tensor", t_r)
            else:
                S.wait("tensor", "dv", t_lad)
            if prev_epi_dv:
                S.wait("tensor", "dv", prev_epi_dv)
            for sub in range(2):
                last_tr = None
                for ic in range(IC):
                    bsl = ic % 2
                    if tr2_use[bsl]:
                        S.wait("tensor", "ac", tr2_use[bsl])
                    t_tr = S.pe(lambda e, c=ic, b=bsl, s2=sub: e.transpose(
                        out=ps[1 + b][:, 0:128],
                        in_=h2t2[s2][:, c * 128:c * 128 + 128],
                        identity=idf), inc=True)
                    S.wait("scalar", "pe", t_tr)
                    tr2_use[bsl] = S.s(lambda e, c=ic, b=bsl, s2=sub: e.activation(
                        out=qh2F[:, c * 256 + s2 * 128: c * 256 + s2 * 128 + 128],
                        in_=ps[1 + b][:, 0:128], func=AF.Copy), inc=True)
                    last_tr = t_tr
                bufguard.setdefault(f"h2r{sub}", {})["pe"] = last_tr
            qh2_evac = tr2_use[(IC - 1) % 2]

        def emit_down(b6):
            nonlocal prev_epi_dv, ternD_use
            S.wait("tensor", "ac", qh2_evac)
            if prev_epi_dv:
                S.wait("tensor", "dv", prev_epi_dv)
            mm_fin = None
            for ic in range(IC):
                sl = ic % 2
                if ternD_use[sl]:
                    S.wait("sync", "pe", ternD_use[sl])
                t_td = S.load(ternD[sl], tern_sc[ic * 128:(ic + 1) * 128, :], f"td{sl}")
                S.waitp("tensor", t_td)
                lastmm = None
                for tsub in range(2):
                    for ft in range(NFT):
                        bank = tsub * NFT + ft
                        lastmm = S.pe(lambda e, c=ic, ts=tsub, f=ft, b=bank, d=ternD[sl]:
                                      e.matmul(out=ps[b][:, 0:NF],
                                               lhsT=qh2F[:, c * 256 + ts * 128: c * 256 + ts * 128 + 128],
                                               rhs=d[:, f * NF:(f + 1) * NF],
                                               start=(c == 0), stop=(c == IC - 1),
                                               skip_group_check=True),
                                      inc=(tsub == 1 and ft == NFT - 1))
                ternD_use[sl] = lastmm
                mm_fin = lastmm
            S.wait("vector", "pe", mm_fin)
            if b6 == 0:
                S.wait("vector", "ac", t_gb)
            for sub in range(2):
                tt = 2 * b6 + sub
                if prev_epi_dv:
                    S.wait("sync", "dv", prev_epi_dv)
                t_x2 = S.load(x2l, x2_sc[tt * 128:(tt + 1) * 128, :], "x2l")
                S.waitp("vector", t_x2)
                if xout_st[sub]:
                    S.waitp("vector", xout_st[sub])
                S.v(lambda e, s2=sub, p=b6 % 4: e.tensor_tensor(
                    out=s_scld[s2], in0=s_invs2[p][s2], in1=s_hfd, op=ALU.mult))
                t_epi = None
                for ft in range(NFT):
                    bank = sub * NFT + ft
                    d_ap = xout[sub][:, ft * NF:(ft + 1) * NF]
                    S.v(lambda e, d=d_ap, b=bank, s2=sub: e.tensor_scalar(
                        out=d, in0=ps[b][:, 0:NF], scalar1=s_scld[s2], scalar2=None,
                        op0=ALU.mult))
                    S.v(lambda e, d=d_ap, f=ft: e.tensor_tensor(
                        out=d, in0=d, in1=gbB[:, f * NF:(f + 1) * NF], op=ALU.mult))
                    t_epi = S.v(lambda e, d=d_ap, f=ft: e.tensor_tensor(
                        out=d, in0=d, in1=x2l[:, f * NF:(f + 1) * NF], op=ALU.add), inc=True)
                S.wait("gpsimd", "dv", t_epi)
                xout_st[sub] = S.store(y_out[tt * 128:(tt + 1) * 128, :], xout[sub], f"yo{sub}")
                prev_epi_dv = t_epi

        for b6 in range(NB_H0):
            emit_transposes(b6, None, reload=True)
            emit_down(b6)

        pending = None
        for b6 in range(NB_H0, NB6):
            p5_stage1(b6, 0, h2t2[0], sqpB, "h2r0", dmpB)
            p5_stage2lad(b6, 0, h2t2[0], dlB, sqpB, "h2r0", spill=False)
            p5_stage1(b6, 1, h2t2[1], sqpB, "h2r1", dmpB)
            t_lad1 = p5_stage2lad(b6, 1, h2t2[1], dlB, sqpB, "h2r1", spill=False)
            if pending is not None:
                emit_down(pending)
            emit_transposes(b6, t_lad1, reload=False)
            pending = b6
        emit_down(pending)

        S.wait_total("sync", ["yo0", "yo1"])

        for nm in ("pe", "dv", "ac"):
            S.semh[nm] = es.enter_context(nc.semaphore(nm))
        for nm in sorted(S.sems):
            S.semh[nm] = es.enter_context(nc.semaphore(nm))

        block = es.enter_context(nc.Block())

        @block.sync
        def _(sync):
            for fn in S.ops["sync"]:
                fn(sync)

        @block.gpsimd
        def _(gpsimd):
            for fn in S.ops["gpsimd"]:
                fn(gpsimd)

        @block.tensor
        def _(tensor):
            for fn in S.ops["tensor"]:
                fn(tensor)

        @block.vector
        def _(vector):
            for fn in S.ops["vector"]:
                fn(vector)

        @block.scalar
        def _(scalar):
            for fn in S.ops["scalar"]:
                fn(scalar)

    return nc


def make_tables(T):
    D = 128
    j = np.arange(D // 2, dtype=np.float64)
    thetas = 10000.0 ** (-2.0 * j / D)
    m = np.arange(1, T + 1, dtype=np.float64)
    ang = np.concatenate([thetas, thetas])[:, None] * m[None, :]
    c = 128.0 ** (-0.25)
    cosv = np.cos(ang) * c
    sinv = np.sin(ang) * c
    sgn = np.ones((D, 1)); sgn[:D // 2] = -1.0
    sinv = sinv * sgn
    chi = cosv.astype(np.float32)
    clo = (cosv - chi.astype(np.float64)).astype(np.float32)
    shi = sinv.astype(np.float32)
    slo = (sinv - shi.astype(np.float64)).astype(np.float32)
    return chi, clo, shi, slo


def make_masks():
    i = np.arange(128)[:, None]
    jq = np.arange(512)[None, :]
    return np.concatenate([
        np.where(128 * rr + i <= jq, 0.0, NEGBIG).astype(np.float32)
        for rr in range(4)], axis=1)


def host_inputs(T, E, I, Wq, bq, Wk, bk, Wv, bv, Wo, bo, wg, wu, wd, lnw, wmod):
    EC = E // 128
    FT = E // 128
    NV = min(512, E); FSV = E // NV
    N2 = min(256, E); FS2 = E // N2
    JS = I // 512
    asc = np.ascontiguousarray
    f = np.float32
    d = {}
    d["wq_t"] = asc(np.asarray(Wq, f).reshape(FT, 128, EC, 128).transpose(0, 3, 2, 1).reshape(E, E))
    d["wk_t"] = asc(np.asarray(Wk, f).reshape(FT, 128, EC, 128).transpose(0, 3, 2, 1).reshape(E, E))
    d["wv_r"] = asc(np.asarray(Wv, f).T.reshape(EC, 128, FSV, NV).transpose(2, 1, 0, 3).reshape(FSV * 128, EC * NV))
    d["wo_r"] = asc(np.asarray(Wo, f).T.reshape(EC, 128, FS2, N2).transpose(2, 1, 0, 3).reshape(FS2 * 128, EC * N2))
    d["bq_c"] = asc(np.asarray(bq, f).reshape(EC, 128).T)
    d["bk_c"] = asc(np.asarray(bk, f).reshape(EC, 128).T)
    d["bv_row"] = asc(np.asarray(bv, f).reshape(1, E))
    d["bo_row"] = asc(np.asarray(bo, f).reshape(1, E))
    d["wmod"] = asc(np.asarray(wmod, f))
    d["wg_r"] = asc(np.asarray(wg, f).T.reshape(EC, 128, JS, 512).transpose(2, 1, 0, 3).reshape(JS * 128, EC * 512))
    d["wu_r"] = asc(np.asarray(wu, f).T.reshape(EC, 128, JS, 512).transpose(2, 1, 0, 3).reshape(JS * 128, EC * 512))
    d["wd_t"] = asc(np.asarray(wd, f).T)
    d["lnw_row"] = asc(np.asarray(lnw, f).reshape(1, I))
    chi, clo, shi, slo = make_tables(T)
    d["chi_in"] = chi; d["clo_in"] = clo; d["shi_in"] = shi; d["slo_in"] = slo
    d["mask_in"] = make_masks()
    d["id_in"] = np.eye(128, dtype=np.float32)
    return d


def kernel(L0_attn_x, L0_attn_Wq, L0_attn_bq, L0_attn_Wk, L0_attn_bk,
           L0_attn_Wv, L0_attn_bv, L0_attn_Wo, L0_attn_bo,
           L0_ffn_w_gate, L0_ffn_w_up, L0_ffn_w_down, L0_ffn_ln_weight,
           L0_c, L0_wmod):
    T, E, I = 2048, 2048, 8192
    xs = np.asarray(L0_attn_x, np.float32)
    cs = np.asarray(L0_c, np.float32)
    B = xs.shape[0]
    EC = E // 128
    nc = build_program(T, E, I)
    shared = host_inputs(T, E, I, L0_attn_Wq, L0_attn_bq, L0_attn_Wk, L0_attn_bk,
                         L0_attn_Wv, L0_attn_bv, L0_attn_Wo, L0_attn_bo,
                         L0_ffn_w_gate, L0_ffn_w_up, L0_ffn_w_down,
                         L0_ffn_ln_weight, L0_wmod)
    in_maps = []
    for b in range(B):
        m = dict(shared)
        m["x_in"] = np.ascontiguousarray(xs[b])
        m["c_col"] = np.ascontiguousarray(cs[b].reshape(EC, 128).T)
        in_maps.append(m)
    res = run_bass_kernel_spmd(nc, in_maps, core_ids=list(range(B)))
    out = np.stack([np.asarray(res.results[b]["y_out"]) for b in range(B)], axis=0)
    return out.astype(np.float32)


# version 09
# speedup vs baseline: 1.0702x; 1.0702x over previous
import numpy as np
from contextlib import ExitStack
import concourse.bass as bass
import concourse.mybir as mybir
from concourse.bass_utils import run_bass_kernel_spmd

F32 = mybir.dt.float32
BF16 = mybir.dt.bfloat16
AF = mybir.ActivationFunctionType
ALU = mybir.AluOpType
AX = mybir.AxisListType

EXP_BIAS = -16.0
NEGBIG = -1.0e38
ARENA = 53152


class Sched:
    ENGS = ("sync", "gpsimd", "tensor", "vector", "scalar")

    def __init__(self):
        self.ops = {e: [] for e in self.ENGS}
        self.cnt = {"pe": 0, "dv": 0, "ac": 0}
        self.sems = {}
        self.semh = {}
        self.ps6 = None

    def _add(self, eng, fn):
        self.ops[eng].append(fn)

    def op(self, eng, build, sem=None, inc=0):
        tgt = None
        if sem is not None:
            self.cnt[sem] += inc
            tgt = self.cnt[sem]

        def fn(e, b=build, s=sem, i=inc, self_=self):
            inst = b(e)
            if s is not None:
                inst.then_inc(self_.semh[s], i)
        self._add(eng, fn)
        return tgt

    def wait(self, eng, sem, val):
        if not val:
            return
        def fn(e, s=sem, v=val, self_=self):
            e.wait_ge(self_.semh[s], v)
        self._add(eng, fn)

    def waitp(self, eng, pair):
        if pair:
            self.wait(eng, pair[0], pair[1])

    def wait_total(self, eng, names):
        for n in names:
            if self.sems.get(n):
                self.wait(eng, n, self.sems[n])

    def v(self, build, inc=False):
        self._add("vector", lambda e: e.drain())
        return self.op("vector", build, "dv" if inc else None, 1 if inc else 0)

    def s(self, build, inc=False):
        self._add("scalar", lambda e: e.drain())
        return self.op("scalar", build, "ac" if inc else None, 1 if inc else 0)

    def pe(self, build, inc=False):
        return self.op("tensor", build, "pe" if inc else None, 1 if inc else 0)

    def _dma(self, eng, out, in_, sem):
        self.sems[sem] = self.sems.get(sem, 0) + 16
        tgt = self.sems[sem]

        def fn(e, o=out, i=in_, s=sem, self_=self):
            e.dma_start(out=o, in_=i).then_inc(self_.semh[s], 16)
        self._add(eng, fn)
        return (sem, tgt)

    def load(self, out, in_, sem):
        return self._dma("sync", out, in_, sem)

    def store(self, out, in_, sem):
        return self._dma("gpsimd", out, in_, sem)


def build_program(T, E, I):
    H = E // 128
    EC = E // 128
    TC = T // 128
    QT = T // 512
    NV = min(512, E); FSV = E // NV
    N2 = min(256, E); FS2 = E // N2
    JS = I // 512
    IC = I // 128
    NB6 = T // 256
    NF = min(512, E); NFT = E // NF
    STRIP_E = min(1024, E)
    STRIP_I = min(1024, I)

    nc = bass.Bass("TRN2", target_bir_lowering=False)
    dt = lambda n, s, d=F32, k="ExternalInput": nc.dram_tensor(n, s, d, kind=k).ap()

    x_in = dt("x_in", [T, E])
    wq_t = dt("wq_t", [E, E]); wk_t = dt("wk_t", [E, E])
    wv_r = dt("wv_r", [FSV * 128, EC * NV])
    wo_r = dt("wo_r", [FS2 * 128, EC * N2])
    bq_c = dt("bq_c", [128, EC]); bk_c = dt("bk_c", [128, EC])
    bv_row = dt("bv_row", [1, E]); bo_row = dt("bo_row", [1, E])
    wmod = dt("wmod", [E, 6 * E])
    c_col = dt("c_col", [128, EC])
    wg_r = dt("wg_r", [JS * 128, EC * 512])
    wu_r = dt("wu_r", [JS * 128, EC * 512])
    wd_t = dt("wd_t", [I, E])
    lnw_row = dt("lnw_row", [1, I])
    chi_in = dt("chi_in", [128, T]); clo_in = dt("clo_in", [128, T])
    shi_in = dt("shi_in", [128, T]); slo_in = dt("slo_in", [128, T])
    mask_in = dt("mask_in", [128, 4 * 512])
    id_in = dt("id_in", [128, 128])

    q_sc = dt("q_sc", [H * 128, T], F32, "Internal")
    k_sc = dt("k_sc", [H * 128, T], F32, "Internal")
    v_sc = dt("v_sc", [H * T, 128], F32, "Internal")
    x2_sc = dt("x2_sc", [T, E], F32, "Internal")
    h_sc = dt("h_sc", [T, I], F32, "Internal")
    tern_sc = dt("tern_sc", [I, E], BF16, "Internal")
    mod_sc = dt("mod_sc", [1, 6 * E], F32, "Internal")
    y_out = dt("y_out", [T, E], F32, "ExternalOutput")

    S = Sched()
    es = ExitStack()
    with es:
        AR = es.enter_context(nc.sbuf_tensor("arena", [128, ARENA], F32))
        ARB = AR.bitcast(BF16)

        def f32(off, n):
            assert off + n <= ARENA, ("f32 overflow", off, n)
            return AR[:, off:off + n]

        def b16(off_f32, n_b16):
            assert 2 * off_f32 + n_b16 <= 2 * ARENA, ("b16 overflow", off_f32, n_b16)
            return ARB[:, 2 * off_f32:2 * off_f32 + n_b16]

        class Bump:
            def __init__(self, start):
                self.off = start
            def f(self, n):
                o = self.off; self.off += n
                assert self.off <= ARENA, ("bump overflow", self.off)
                return f32(o, n)
            def b(self, n_b16):
                n = (n_b16 + 1) // 2
                o = self.off; self.off += n
                assert self.off <= ARENA, ("bump overflow b16", self.off)
                return b16(o, n_b16)

        st_ = Bump(0)
        idf = st_.f(128)
        idb = st_.b(128)
        rowst_full = st_.f(512); rowst = rowst_full[0:1, :]
        rowst2 = st_.f(512)[0:1, :]
        rowst3 = st_.f(512)[0:1, :]
        ones_blk = st_.f(128)
        ones_col = ones_blk[:, 0:1]
        ones_row = ones_blk[0:1, :]
        scal = st_.f(64)
        sm = lambda i: scal[:, i:i + 1]
        (s_s1, s_ss, s_m, s_mm2, s_var, s_rstd, s_amax, s_sx, s_zr, s_swg,
         s_hfg, s_swu, s_hfu, s_swd, s_hfd, s_ms, s_rstd2, s_rm) = [sm(i) for i in range(18)]
        s_invs2 = [[sm(18), sm(19)], [sm(28), sm(29)]]
        s_scld = [sm(20), sm(21)]
        s_t11 = sm(22)
        c_eps5, c_eps6, c_mhalf, c_phalf, c_expb = sm(23), sm(24), sm(25), sm(26), sm(27)
        sp_s1 = [sm(30), sm(31)]; sp_ss = [sm(32), sm(33)]
        sp_m = [sm(34), sm(35)]; sp_mm2 = [sm(36), sm(37)]
        sp_var = [sm(38), sm(39)]; sp_rstd = [sm(40), sm(41)]
        sp_amax = [sm(42), sm(43)]; sp_sx = [sm(44), sm(45)]
        sp_ms = [sm(46), sm(47)]; sp_rstd2 = [sm(48), sm(49)]
        sp_rm = [sm(50), sm(51)]; sp_sx2 = [sm(52), sm(53)]
        c_m128 = sm(54)
        bq_sb = st_.f(EC); bk_sb = st_.f(EC)
        cc_sb = st_.f(EC); scc_sb = st_.f(EC)
        invsx = st_.f(TC); sclg = st_.f(TC); sclu = st_.f(TC)
        parts = [st_.f(128) for _ in range(3)]
        PH = st_.off

        ps = [es.enter_context(nc.psum_tensor(f"ps{i}", [128, 512], F32)) for i in range(8)]
        psb = ps[7].bitcast(BF16)


        def ladder_sd(src_ap, scale_ap, dst_ap, width, ytiles, rtiles, stiles, sw):
            strips = [(o, min(sw, width - o)) for o in range(0, width, sw)]
            NG = len(ytiles)
            t_last = None
            for g0 in range(0, len(strips), NG):
                grp = strips[g0:g0 + NG]
                ydv = []
                for i, (o, w) in enumerate(grp):
                    ydv.append(S.v(lambda e, d=ytiles[i][:, 0:w], s0=src_ap[:, o:o + w],
                                   sc=scale_ap: e.tensor_scalar(
                        out=d, in0=s0, scalar1=sc, scalar2=128.5,
                        op0=ALU.mult, op1=ALU.add), inc=True))
                rdv = list(ydv)
                rsrc = [ytiles[i] for i in range(len(grp))]
                for li, t2 in enumerate((128.0, 64.0, 32.0, 16.0, 8.0, 4.0, 2.0, 1.0)):
                    sac = []
                    for i, (o, w) in enumerate(grp):
                        S.wait("scalar", "dv", rdv[i])
                        sac.append(S.s(lambda e, d=stiles[i][:, 0:w],
                                       s0=rsrc[i][:, 0:w]: e.activation(
                            out=d, in_=s0, func=AF.Sign, bias=c_m128), inc=True))
                    for i, (o, w) in enumerate(grp):
                        S.wait("vector", "ac", sac[i])
                        rdv[i] = S.v(lambda e, d=rtiles[i][:, 0:w], s0=stiles[i][:, 0:w],
                                     s1=rsrc[i][:, 0:w], tv=t2 / 2: e.scalar_tensor_tensor(
                            out=d, in0=s0, scalar=-tv, in1=s1,
                            op0=ALU.mult, op1=ALU.add), inc=True)
                    rsrc = [rtiles[i] for i in range(len(grp))]
                for i, (o, w) in enumerate(grp):
                    t_last = S.v(lambda e, d=dst_ap[:, o:o + w], s0=ytiles[i][:, 0:w],
                                 s1=rtiles[i][:, 0:w]: e.scalar_tensor_tensor(
                        out=d, in0=s0, scalar=-0.5, in1=s1,
                        op0=ALU.add, op1=ALU.subtract), inc=True)
            return t_last

        def fence_sync():
            S.wait("sync", "pe", S.cnt["pe"])
            S.wait("sync", "dv", S.cnt["dv"])
            S.wait("sync", "ac", S.cnt["ac"])
            S.wait_total("sync", sorted(S.sems))

        def bcast(dst, src_row_ap_fn, width, transform, st_guard=None):
            last = None
            seg = min(512, width)
            rw = rowst_full[0:1, 0:seg]
            rw2 = rowst2[0:1, 0:seg]
            rw3 = rowst3[0:1, 0:seg]
            for sl0 in range(width // seg):
                if st_guard:
                    S.wait_total("sync", st_guard)
                t_l = S.load(rw, src_row_ap_fn(sl0 * seg, seg), "row")
                t_r = None
                if transform == "add1":
                    S.waitp("vector", t_l)
                    t_r = S.v(lambda e: e.tensor_scalar(
                        out=rw, in0=rw, scalar1=1.0, scalar2=None, op0=ALU.add), inc=True)
                elif transform == "signclamp":
                    S.waitp("vector", t_l)
                    S.v(lambda e: e.tensor_scalar(
                        out=rw2, in0=rw, scalar1=0.0, scalar2=None, op0=ALU.is_ge))
                    S.v(lambda e: e.tensor_scalar(
                        out=rw2, in0=rw2, scalar1=2.0, scalar2=-1.0,
                        op0=ALU.mult, op1=ALU.add))
                    S.v(lambda e: e.tensor_scalar(
                        out=rw3, in0=rw, scalar1=-1.0, scalar2=None, op0=ALU.mult))
                    S.v(lambda e: e.tensor_tensor(out=rw, in0=rw, in1=rw3, op=ALU.max))
                    S.v(lambda e: e.tensor_scalar(
                        out=rw, in0=rw, scalar1=0.25, scalar2=None, op0=ALU.max))
                    t_r = S.v(lambda e: e.tensor_tensor(
                        out=rw, in0=rw, in1=rw2, op=ALU.mult), inc=True)
                if t_r:
                    S.wait("tensor", "dv", t_r)
                else:
                    S.waitp("tensor", t_l)
                S.waitp("tensor", S.ps6)
                t_mm = S.pe(lambda e: e.matmul(out=ps[6][:, 0:seg], lhsT=ones_row,
                                               rhs=rw, start=True, stop=True), inc=True)
                S.wait("scalar", "pe", t_mm)
                t_e = S.s(lambda e, d=dst[:, sl0 * seg:(sl0 + 1) * seg]: e.activation(
                    out=d, in_=ps[6][:, 0:seg], func=AF.Copy), inc=True)
                S.ps6 = ("ac", t_e)
                S.wait("sync", "pe", t_mm)
                last = t_e
            return last

        S.load(idf, id_in[:, :], "ini")
        S.load(bq_sb, bq_c[:, :], "ini"); S.load(bk_sb, bk_c[:, :], "ini")
        t_cc = S.load(cc_sb, c_col[:, :], "ini")
        S.waitp("vector", t_cc)
        S.v(lambda e: e.memset(ones_blk, 1.0))
        S.v(lambda e: e.memset(c_eps5, 1e-5))
        S.v(lambda e: e.memset(c_eps6, 1e-6))
        S.v(lambda e: e.memset(c_mhalf, -0.5))
        S.v(lambda e: e.memset(c_phalf, 0.5))
        S.v(lambda e: e.memset(c_expb, EXP_BIAS))
        S.v(lambda e: e.memset(c_m128, -128.0))
        S.v(lambda e: e.tensor_copy(out=idb, in_=idf), inc=True)
        S.waitp("scalar", t_cc)
        t_silu = S.s(lambda e: e.activation(out=scc_sb, in_=cc_sb, func=AF.Silu), inc=True)

        p0 = Bump(PH)
        wch = [p0.f(6 * E), p0.f(6 * E)]
        acc = p0.f(6 * E)
        modst = [f32(PH, 512)[0:1, :], f32(PH + 512, 512)[0:1, :]]
        use_t = [None, None]
        for ec in range(EC):
            sl = ec % 2
            if use_t[sl]:
                S.wait("sync", "dv", use_t[sl])
            t_w = S.load(wch[sl], wmod[ec * 128:(ec + 1) * 128, :], f"wch{sl}")
            S.waitp("vector", t_w)
            if ec == 0:
                S.wait("vector", "ac", t_silu)
                use_t[sl] = S.v(lambda e, w=wch[sl]: e.tensor_scalar(
                    out=acc, in0=w, scalar1=scc_sb[:, 0:1], scalar2=None,
                    op0=ALU.mult), inc=True)
            else:
                use_t[sl] = S.v(lambda e, w=wch[sl], c=ec: e.scalar_tensor_tensor(
                    out=acc, in0=w, scalar=scc_sb[:, c:c + 1], in1=acc,
                    op0=ALU.mult, op1=ALU.add), inc=True)
        t_acc = use_t[(EC - 1) % 2]
        S.wait("tensor", "dv", t_acc)
        mst_use = [None, None]
        t_cp = None
        for nt in range(6 * E // 512):
            sl = nt % 2
            if t_cp:
                S.wait("tensor", "dv", t_cp)
            t_mm = S.pe(lambda e, n=nt: e.matmul(
                out=ps[0][:, 0:512], lhsT=ones_blk,
                rhs=acc[:, n * 512:(n + 1) * 512], start=True, stop=True), inc=True)
            if mst_use[sl]:
                S.waitp("vector", mst_use[sl])
            S.wait("vector", "pe", t_mm)
            t_cp = S.v(lambda e, d=modst[sl]: e.tensor_copy(
                out=d, in_=ps[0][0:1, 0:512]), inc=True)
            S.wait("gpsimd", "dv", t_cp)
            mst_use[sl] = S.store(mod_sc[0:1, nt * 512:(nt + 1) * 512], modst[sl], f"mo{sl}")
        mod_st_done = ["mo0", "mo1"]

        def modrow(g):
            return lambda o, n, gg=g: mod_sc[0:1, gg * E + o: gg * E + o + n]

        p0b = Bump(PH)
        hF = p0b.f(EC * T)
        O_AFTER_HF = p0b.off
        bc0 = p0b.f(E); bc1 = p0b.f(E)
        xt = [p0b.f(E), p0b.f(E)]
        h1t = p0b.f(E)
        t_b0 = bcast(bc0, modrow(1), E, "add1", st_guard=mod_st_done)
        t_b1 = bcast(bc1, modrow(0), E, None, st_guard=mod_st_done)
        xt_use = [None, None]
        tr_use = [None, None]
        last_hF_evac = None
        for tt in range(TC):
            sl = tt % 2
            if xt_use[sl]:
                S.wait("sync", "dv", xt_use[sl])
            t_x = S.load(xt[sl], x_in[tt * 128:(tt + 1) * 128, :], f"xt{sl}")
            S.waitp("vector", t_x)
            S.v(lambda e, x=xt[sl]: e.reduce_sum(out=s_s1, in_=x, axis=AX.X))
            S.waitp("scalar", t_x)
            t_sq = S.s(lambda e, x=xt[sl]: e.activation(
                out=h1t, in_=x, func=AF.Square, accum_out=s_ss), inc=True)
            S.v(lambda e: e.tensor_scalar(out=s_m, in0=s_s1, scalar1=1.0 / E,
                                          scalar2=None, op0=ALU.mult))
            S.v(lambda e: e.tensor_tensor(out=s_mm2, in0=s_m, in1=s_m, op=ALU.mult))
            S.wait("vector", "ac", t_sq)
            t_var = S.v(lambda e: e.scalar_tensor_tensor(
                out=s_var, in0=s_ss, scalar=1.0 / E, in1=s_mm2,
                op0=ALU.mult, op1=ALU.subtract), inc=True)
            S.wait("scalar", "dv", t_var)
            t_rstd = S.s(lambda e: e.activation(
                out=s_rstd, in_=s_var, func=AF.Sqrt, bias=c_eps5), inc=True)
            S.wait("vector", "ac", t_rstd)
            S.v(lambda e: e.reciprocal(out=s_rstd, in_=s_rstd))
            if tt == 0:
                S.wait("vector", "ac", t_b0)
                S.wait("vector", "ac", t_b1)
            xt_use[sl] = S.v(lambda e, x=xt[sl]: e.tensor_scalar(
                out=h1t, in0=x, scalar1=s_m, scalar2=s_rstd,
                op0=ALU.subtract, op1=ALU.mult), inc=True)
            S.v(lambda e: e.tensor_tensor(out=h1t, in0=h1t, in1=bc0, op=ALU.mult))
            t_h1 = S.v(lambda e: e.tensor_tensor(out=h1t, in0=h1t, in1=bc1, op=ALU.add), inc=True)
            S.wait("tensor", "dv", t_h1)
            for ec in range(EC):
                bk = ec % 2
                if tr_use[bk]:
                    S.wait("tensor", "ac", tr_use[bk])
                t_tr = S.pe(lambda e, c=ec, b=bk: e.transpose(
                    out=ps[1 + b][:, 0:128], in_=h1t[:, c * 128:(c + 1) * 128],
                    identity=idf), inc=True)
                S.wait("scalar", "pe", t_tr)
                tr_use[bk] = S.s(lambda e, c=ec, b=bk, t=tt: e.activation(
                    out=hF[:, c * T + t * 128: c * T + t * 128 + 128],
                    in_=ps[1 + b][:, 0:128], func=AF.Copy), inc=True)
            last_hF_evac = tr_use[(EC - 1) % 2]

        p1 = Bump(O_AFTER_HF)
        chi = p1.f(T); clo = p1.f(T); shi = p1.f(T); slo = p1.f(T)
        wt = [p1.f(E), p1.f(E)]
        qkst = [p1.f(512), p1.f(512)]
        swpst = [p1.f(512), p1.f(512)]
        rs1 = [p1.f(512), p1.f(512)]
        rs2 = p1.f(512)
        fence_sync()
        S.load(chi, chi_in[:, :], "tab"); S.load(clo, clo_in[:, :], "tab")
        S.load(shi, shi_in[:, :], "tab")
        t_tab = S.load(slo, slo_in[:, :], "tab")
        S.waitp("vector", t_tab)
        S.wait("tensor", "ac", last_hF_evac)
        wt_use = [None, None]
        st_use = [None, None]
        sw_use = [None, None]
        rs_store = [None, None]
        psA_use = [None, None]
        for (wdram, bias_sb, dst_sc) in ((wq_t, bq_sb, q_sc), (wk_t, bk_sb, k_sc)):
            for ft in range(EC):
                sl = ft % 2
                if wt_use[sl]:
                    S.wait("sync", "pe", wt_use[sl])
                t_w = S.load(wt[sl], wdram[ft * 128:(ft + 1) * 128, :], f"wt{sl}")
                S.waitp("tensor", t_w)
                for tq in range(QT):
                    bk = tq % 2
                    pA = ps[0] if bk == 0 else ps[3]
                    S.waitp("tensor", psA_use[bk])
                    for ec in range(EC):
                        t_mm = S.pe(lambda e, c=ec, q=tq, w=wt[sl], p=pA: e.matmul(
                            out=p[:, 0:512], lhsT=w[:, c * 128:(c + 1) * 128],
                            rhs=hF[:, c * T + q * 512: c * T + (q + 1) * 512],
                            start=(c == 0), stop=(c == EC - 1),
                            skip_group_check=True), inc=(ec == EC - 1))
                    wt_use[sl] = t_mm
                    ssl = (ft * QT + tq) % 2
                    if st_use[ssl]:
                        S.wait("scalar", "dv", st_use[ssl])
                    S.wait("scalar", "pe", t_mm)
                    t_ev = S.s(lambda e, f=ft, d=qkst[ssl], p=pA, b=bias_sb: e.activation(
                        out=d, in_=p[:, 0:512], func=AF.Identity, bias=b[:, f:f + 1]), inc=True)
                    psA_use[bk] = ("ac", t_ev)
                    if sw_use[ssl]:
                        S.wait("gpsimd", "dv", sw_use[ssl])
                    S.wait("gpsimd", "ac", t_ev)
                    S.store(swpst[ssl][0:64, :], qkst[ssl][64:128, :], f"sw{ssl}")
                    t_sw = S.store(swpst[ssl][64:128, :], qkst[ssl][0:64, :], f"sw{ssl}")
                    if rs_store[ssl]:
                        S.waitp("vector", rs_store[ssl])
                    S.waitp("vector", t_sw)
                    c0, c1 = tq * 512, (tq + 1) * 512
                    r1 = rs1[ssl]
                    S.v(lambda e, s=qkst[ssl], a=c0, b2=c1, r=r1: e.tensor_tensor(
                        out=r, in0=s, in1=chi[:, a:b2], op=ALU.mult))
                    S.v(lambda e, s=qkst[ssl], a=c0, b2=c1: e.tensor_tensor(
                        out=rs2, in0=s, in1=clo[:, a:b2], op=ALU.mult))
                    st_use[ssl] = S.v(lambda e, r=r1: e.tensor_tensor(
                        out=r, in0=r, in1=rs2, op=ALU.add), inc=True)
                    S.v(lambda e, s=swpst[ssl], a=c0, b2=c1: e.tensor_tensor(
                        out=rs2, in0=s, in1=shi[:, a:b2], op=ALU.mult))
                    S.v(lambda e, r=r1: e.tensor_tensor(out=r, in0=r, in1=rs2, op=ALU.add))
                    sw_use[ssl] = S.v(lambda e, s=swpst[ssl], a=c0, b2=c1: e.tensor_tensor(
                        out=rs2, in0=s, in1=slo[:, a:b2], op=ALU.mult), inc=True)
                    t_rot = S.v(lambda e, r=r1: e.tensor_tensor(
                        out=r, in0=r, in1=rs2, op=ALU.add), inc=True)
                    S.wait("gpsimd", "dv", t_rot)
                    rs_store[ssl] = S.store(
                        dst_sc[ft * 128:(ft + 1) * 128, tq * 512:(tq + 1) * 512], r1, f"rq{ssl}")

        pv = Bump(O_AFTER_HF)
        vsl = pv.f(EC * NV)
        vst = [pv.f(NV), pv.f(NV)]
        bvB = pv.f(E)
        fence_sync()
        t_bv = bcast(bvB, lambda o, n: bv_row[0:1, o:o + n], E, None)
        vst_use = [None, None]
        last_vmm = None
        for fs in range(FSV):
            if last_vmm:
                S.wait("sync", "pe", last_vmm)
            t_v = S.load(vsl, wv_r[fs * 128:(fs + 1) * 128, :], "vsl")
            S.waitp("tensor", t_v)
            for tt in range(TC):
                bk = tt % 2
                pA = ps[0] if bk == 0 else ps[3]
                S.waitp("tensor", psA_use[bk])
                for ec in range(EC):
                    t_mm = S.pe(lambda e, c=ec, t=tt, p=pA: e.matmul(
                        out=p[:, 0:NV], lhsT=hF[:, c * T + t * 128: c * T + t * 128 + 128],
                        rhs=vsl[:, c * NV:(c + 1) * NV], start=(c == 0), stop=(c == EC - 1),
                        skip_group_check=True), inc=(ec == EC - 1))
                last_vmm = t_mm
                ssl = tt % 2
                if vst_use[ssl]:
                    S.waitp("vector", vst_use[ssl])
                S.wait("vector", "pe", t_mm)
                if fs == 0 and tt == 0:
                    S.wait("vector", "ac", t_bv)
                t_vv = S.v(lambda e, d=vst[ssl], p=pA, f=fs: e.tensor_tensor(
                    out=d, in0=p[:, 0:NV], in1=bvB[:, f * NV:(f + 1) * NV],
                    op=ALU.add), inc=True)
                psA_use[bk] = ("dv", t_vv)
                S.wait("gpsimd", "dv", t_vv)
                for hh in range(NV // 128):
                    h = fs * (NV // 128) + hh
                    vst_use[ssl] = S.store(
                        v_sc[h * T + tt * 128: h * T + (tt + 1) * 128, :],
                        vst[ssl][:, hh * 128:(hh + 1) * 128], f"vs{ssl}")
        spill_done = ["rq0", "rq1", "vs0", "vs1"]

        p2 = Bump(O_AFTER_HF)
        qrot = [p2.f(T), p2.f(T)]
        krot = [p2.f(T), p2.f(T)]
        VRW = TC * 129
        vrhs = [p2.f(VRW + 7), p2.f(VRW + 7)]
        pT = [p2.f(512), p2.f(512), p2.f(512)]
        yst = [p2.f(128), p2.f(128)]
        masks_t = p2.f(4 * 512)
        masks = [masks_t[:, i * 512:(i + 1) * 512] for i in range(4)]
        fence_sync()
        t_mload = S.load(masks_t, mask_in[:, :], "msk")
        t_ones = None
        for vb in range(2):
            for c in range(TC):
                t_ones = S.v(lambda e, b=vb, cc=c: e.memset(
                    vrhs[b][:, cc * 129 + 128: cc * 129 + 129], 1.0), inc=True)
        S.wait_total("sync", spill_done)
        S.wait("sync", "dv", t_ones)

        head_use_pe = [None, None]
        ptslot_use = [None, None, None]
        yst_use = [None, None]
        ptr_use = None
        av_dv = [None] * 4
        first_mask_wait = True
        yF_done = None
        for h in range(H):
            hb = h % 2
            if head_use_pe[hb]:
                S.wait("sync", "pe", head_use_pe[hb])
            S.load(qrot[hb], q_sc[h * 128:(h + 1) * 128, :], f"hd{hb}")
            S.load(krot[hb], k_sc[h * 128:(h + 1) * 128, :], f"hd{hb}")
            for c in range(TC):
                t_v = S.load(vrhs[hb][:, c * 129:c * 129 + 128],
                             v_sc[h * T + c * 128: h * T + (c + 1) * 128, :], f"hd{hb}")
            S.waitp("tensor", t_v)
            exp_done = {}
            avlast = {}
            SBANKS = (0, 1, 7)
            for j in range(QT):
                nact = min(TC, 4 * (j + 1))
                for c0 in range(0, nact, 3):
                    batch = list(range(c0, min(c0 + 3, nact)))
                    for c in batch:
                        sbk = c % 3
                        pS = ps[SBANKS[sbk]]
                        if ptslot_use[sbk]:
                            S.wait("tensor", "ac", exp_done.get(c - 3, 0) or 0)
                        t_sc2 = S.pe(lambda e, cc=c, jj=j, p=pS, b=hb: e.matmul(
                            out=p[:, 0:512], lhsT=krot[b][:, cc * 128:(cc + 1) * 128],
                            rhs=qrot[b][:, jj * 512:(jj + 1) * 512],
                            start=True, stop=True, skip_group_check=True), inc=True)
                        diag = c >= 4 * j
                        if diag:
                            if first_mask_wait:
                                S.waitp("vector", t_mload)
                                first_mask_wait = False
                            S.wait("vector", "pe", t_sc2)
                            t_mk = S.v(lambda e, p=pS, r=c - 4 * j: e.tensor_tensor(
                                out=p[:, 0:512], in0=p[:, 0:512], in1=masks[r],
                                op=ALU.add), inc=True)
                            S.wait("scalar", "dv", t_mk)
                        else:
                            S.wait("scalar", "pe", t_sc2)
                        if ptslot_use[sbk]:
                            S.wait("scalar", "pe", ptslot_use[sbk])
                        t_exp = S.s(lambda e, p=pS, d=pT[sbk]: e.activation(
                            out=d, in_=p[:, 0:512], func=AF.Exp, bias=c_expb), inc=True)
                        exp_done[c] = t_exp
                    S.wait("tensor", "ac", exp_done[batch[-1]])
                    for c in batch:
                        sbk = c % 3
                        last = None
                        for qs in range(4):
                            qq = 4 * j + qs
                            if c > qq:
                                continue
                            if c == 0 and av_dv[qs] is not None:
                                S.wait("tensor", "dv", av_dv[qs])
                            last = S.pe(lambda e, cc=c, q=qs, b=hb, qqv=qq, sb2=sbk: e.matmul(
                                out=ps[2 + q][:, 0:129], lhsT=pT[sb2][:, q * 128:(q + 1) * 128],
                                rhs=vrhs[b][:, cc * 129: cc * 129 + 129],
                                start=(cc == 0), stop=(cc == qqv),
                                skip_group_check=True), inc=True)
                        avlast[c] = last
                        ptslot_use[sbk] = last
                for qs in range(4):
                    ssl = qs % 2
                    S.wait("vector", "pe", avlast[4 * j + qs])
                    if yst_use[ssl]:
                        S.wait("vector", "pe", yst_use[ssl])
                    S.v(lambda e, q=qs: e.reciprocal(out=s_zr, in_=ps[2 + q][:, 128:129]))
                    av_dv[qs] = S.v(lambda e, q=qs, d=yst[ssl]: e.tensor_scalar(
                        out=d, in0=ps[2 + q][:, 0:128], scalar1=s_zr, scalar2=None,
                        op0=ALU.mult), inc=True)
                    S.wait("tensor", "dv", av_dv[qs])
                    if ptr_use:
                        S.wait("tensor", "ac", ptr_use)
                    S.waitp("tensor", S.ps6)
                    yst_use[ssl] = S.pe(lambda e, d=yst[ssl]: e.transpose(
                        out=ps[6][:, 0:128], in_=d, identity=idf), inc=True)
                    S.wait("scalar", "pe", yst_use[ssl])
                    ptr_use = S.s(lambda e, hh2=h, qq2=4 * j + qs: e.activation(
                        out=hF[:, hh2 * T + qq2 * 128: hh2 * T + qq2 * 128 + 128],
                        in_=ps[6][:, 0:128], func=AF.Copy), inc=True)
                    S.ps6 = ("ac", ptr_use)
            head_use_pe[hb] = avlast[nact - 1]
        yF_done = ptr_use

        p3 = Bump(O_AFTER_HF)
        bcA = p3.f(E); bcB = p3.f(E)
        wt2 = [p3.f(EC * N2), p3.f(EC * N2)]
        xinst = [p3.f(N2), p3.f(N2)]
        x2st = [p3.f(N2), p3.f(N2)]
        stst = [p3.f(1024), p3.f(1024)]
        stdmp = p3.f(1024)
        S.wait("tensor", "ac", yF_done)
        fence_sync()
        t_bo = bcast(bcA, lambda o, n: bo_row[0:1, o:o + n], E, None)
        t_ga = bcast(bcB, modrow(2), E, "signclamp", st_guard=mod_st_done)

        stats_jobs = []
        nch_m = [0, 0, 0]
        for mi, (wdram, rows, cols) in enumerate(((wg_r, JS * 128, EC * 512),
                                                  (wu_r, JS * 128, EC * 512),
                                                  (wd_t, I, E))):
            ccw = min(1024, cols)
            cpr = cols // ccw
            nch_m[mi] = (rows // 128) * cpr
            for ch in range(nch_m[mi]):
                stats_jobs.append((mi, wdram, (ch // cpr) * 128, (ch % cpr) * ccw, ch, ccw))
        st_slot_use = [None, None]
        nstats = len(stats_jobs)

        def emit_stats(idx):
            mi, wdram, r0, c0, ch, ccw = stats_jobs[idx]
            sl = idx % 2
            if st_slot_use[sl]:
                S.wait("sync", "ac", st_slot_use[sl])
            t_l = S.load(stst[sl][:, 0:ccw], wdram[r0:r0 + 128, c0:c0 + ccw], f"ss{sl}")
            S.waitp("scalar", t_l)
            st_slot_use[sl] = S.s(lambda e, s=stst[sl][:, 0:ccw], m=mi, p=ch, w=ccw: e.activation(
                out=stdmp[:, 0:w], in_=s, func=AF.Abs, accum_out=parts[m][:, p:p + 1]), inc=True)
            return st_slot_use[sl]

        wt2_use = [None, None]
        x2_dv = [None, None]
        x2st_st = [None, None]
        si = 0
        njobs_per = max(1, (nstats + FS2 * TC - 1) // (FS2 * TC))
        last_stats = None
        for fs in range(FS2):
            sl = fs % 2
            if wt2_use[sl]:
                S.wait("sync", "pe", wt2_use[sl])
            t_w = S.load(wt2[sl], wo_r[fs * 128:(fs + 1) * 128, :], f"wo{sl}")
            S.waitp("tensor", t_w)
            for tt in range(TC):
                bk = tt % 2
                pA = ps[0] if bk == 0 else ps[1]
                if x2_dv[bk]:
                    S.wait("tensor", "dv", x2_dv[bk])
                for ec in range(EC):
                    t_mm = S.pe(lambda e, c=ec, t=tt, p=pA, w=wt2[sl]: e.matmul(
                        out=p[:, 0:N2], lhsT=hF[:, c * T + t * 128: c * T + t * 128 + 128],
                        rhs=w[:, c * N2:(c + 1) * N2], start=(c == 0), stop=(c == EC - 1),
                        skip_group_check=True), inc=(ec == EC - 1))
                wt2_use[sl] = t_mm
                ssl = tt % 2
                if x2_dv[bk]:
                    S.wait("sync", "dv", x2_dv[bk])
                t_xl = S.load(xinst[ssl], x_in[tt * 128:(tt + 1) * 128, fs * N2:(fs + 1) * N2], f"xi{ssl}")
                if x2st_st[ssl]:
                    S.waitp("vector", x2st_st[ssl])
                S.wait("vector", "pe", t_mm)
                S.waitp("vector", t_xl)
                if fs == 0 and tt == 0:
                    S.wait("vector", "ac", t_bo)
                    S.wait("vector", "ac", t_ga)
                S.v(lambda e, p=pA, d=x2st[ssl], f=fs: e.tensor_tensor(
                    out=d, in0=p[:, 0:N2], in1=bcA[:, f * N2:(f + 1) * N2], op=ALU.add))
                S.v(lambda e, d=x2st[ssl], f=fs: e.tensor_tensor(
                    out=d, in0=d, in1=bcB[:, f * N2:(f + 1) * N2], op=ALU.mult))
                x2_dv[bk] = S.v(lambda e, d=x2st[ssl], x=xinst[ssl]: e.tensor_tensor(
                    out=d, in0=d, in1=x, op=ALU.add), inc=True)
                S.wait("gpsimd", "dv", x2_dv[bk])
                x2st_st[ssl] = S.store(
                    x2_sc[tt * 128:(tt + 1) * 128, fs * N2:(fs + 1) * N2], x2st[ssl], f"xs{ssl}")
                for _ in range(njobs_per):
                    if si < nstats:
                        last_stats = emit_stats(si); si += 1
        while si < nstats:
            last_stats = emit_stats(si); si += 1
        x2_done = ["xs0", "xs1"]

        S.wait("vector", "ac", last_stats)
        t_rc = None
        for mi, (s_sw, s_hf) in enumerate(((s_swg, s_hfg), (s_swu, s_hfu), (s_swd, s_hfd))):
            t_cs = S.v(lambda e, m=mi: e.reduce_sum(
                out=s_zr, in_=parts[m][:, 0:nch_m[m]], axis=AX.X), inc=True)
            S.wait("tensor", "dv", t_cs)
            S.waitp("tensor", S.ps6)
            t_mm = S.pe(lambda e: e.matmul(out=ps[6][:, 0:1], lhsT=ones_blk,
                                           rhs=s_zr, start=True, stop=True), inc=True)
            S.wait("vector", "pe", t_mm)
            S.v(lambda e, d=s_sw: e.tensor_scalar(
                out=d, in0=ps[6][:, 0:1], scalar1=1.0 / (I * E),
                scalar2=None, op0=ALU.mult))
            t_mn = S.v(lambda e, d=s_sw: e.tensor_scalar(
                out=d, in0=d, scalar1=1e-5, scalar2=None, op0=ALU.max), inc=True)
            S.ps6 = ("dv", t_mn)
            S.v(lambda e, d=s_hf, s0=s_sw: e.tensor_scalar(
                out=d, in0=s0, scalar1=0.5, scalar2=None, op0=ALU.mult))
            t_rc = S.v(lambda e, d=s_sw: e.reciprocal(out=d, in_=d), inc=True)
        stats_done = t_rc

        p3b = Bump(PH)
        qxF = p3b.b(EC * T)
        O_AFTER_QXF = p3b.off
        x2t = [p3b.f(E), p3b.f(E)]; h2t = [p3b.f(E), p3b.f(E)]
        LW3 = min(1024, E // 2)
        lad_y = p3b.f(LW3); lad_r = p3b.f(LW3); lad_b = p3b.f(LW3)
        lad_y2 = p3b.f(LW3); lad_r2 = p3b.f(LW3); lad_b2 = p3b.f(LW3)
        sqdump = p3b.f(E)
        bcC = p3b.f(E); bcD = p3b.f(E)
        wdst = [p3b.f(E), p3b.f(E)]
        ternb = [p3b.b(E), p3b.b(E)]
        scrd = p3b.b(E)
        fence_sync()
        t_cb = bcast(bcC, modrow(4), E, "add1", st_guard=mod_st_done)
        t_sb = bcast(bcD, modrow(3), E, "none")
        S.wait_total("sync", x2_done)

        wd_use = [None, None]
        tb_use = [None, None]
        tern_st = [None, None]

        def emit_tern(ic):
            sl = ic % 2
            if wd_use[sl]:
                S.wait("sync", "ac", wd_use[sl])
            t_l = S.load(wdst[sl], wd_t[ic * 128:(ic + 1) * 128, :], f"wd{sl}")
            S.waitp("scalar", t_l)
            if ic == 0:
                S.wait("scalar", "dv", stats_done)
            if tern_st[sl]:
                S.waitp("scalar", tern_st[sl])
            for g in tb_use:
                if g:
                    S.wait("scalar", "dv", g)
            S.s(lambda e, s=wdst[sl], d=ternb[sl]: e.activation(
                out=d, in_=s, func=AF.Sign, scale=s_swd, bias=c_mhalf))
            wd_use[sl] = S.s(lambda e, s=wdst[sl]: e.activation(
                out=scrd, in_=s, func=AF.Sign, scale=s_swd, bias=c_phalf), inc=True)
            S.wait("vector", "ac", wd_use[sl])
            tb_use[sl] = S.v(lambda e, d=ternb[sl]: e.tensor_tensor(
                out=d, in0=d, in1=scrd, op=ALU.add), inc=True)
            S.wait("gpsimd", "dv", tb_use[sl])
            tern_st[sl] = S.store(tern_sc[ic * 128:(ic + 1) * 128, :], ternb[sl], f"ts{sl}")

        qx_tr_use = [None, None]
        last_qx_evac = None
        tr_last = [None, None]
        x2t_use = [None, None]
        sq_done = [None, None]
        ld_done = [None, None]
        ic_done = 0

        def p3b_stage1(tt):
            p = tt % 2
            x2s = x2t[p]
            if x2t_use[p]:
                S.wait("sync", "dv", x2t_use[p])
            t_x2 = S.load(x2s, x2_sc[tt * 128:(tt + 1) * 128, :], f"x2t{p}")
            ld_done[p] = t_x2
            S.waitp("scalar", t_x2)
            sq_done[p] = S.s(lambda e, x=x2s, p2=p: e.activation(
                out=sqdump, in_=x, func=AF.Square, accum_out=sp_ss[p2]), inc=True)

        def p3b_stage2(tt):
            p = tt % 2
            x2s = x2t[p]
            h2s = h2t[p]
            S.waitp("vector", ld_done[p])
            S.v(lambda e, x=x2s, p2=p: e.reduce_sum(out=sp_s1[p2], in_=x, axis=AX.X))
            S.v(lambda e, p2=p: e.tensor_scalar(out=sp_m[p2], in0=sp_s1[p2],
                scalar1=1.0 / E, scalar2=None, op0=ALU.mult))
            S.v(lambda e, p2=p: e.tensor_tensor(out=sp_mm2[p2], in0=sp_m[p2],
                in1=sp_m[p2], op=ALU.mult))
            S.wait("vector", "ac", sq_done[p])
            t_var = S.v(lambda e, p2=p: e.scalar_tensor_tensor(
                out=sp_var[p2], in0=sp_ss[p2], scalar=1.0 / E, in1=sp_mm2[p2],
                op0=ALU.mult, op1=ALU.subtract), inc=True)
            S.wait("scalar", "dv", t_var)
            t_rstd = S.s(lambda e, p2=p: e.activation(
                out=sp_rstd[p2], in_=sp_var[p2], func=AF.Sqrt, bias=c_eps5), inc=True)
            S.wait("vector", "ac", t_rstd)
            S.v(lambda e, p2=p: e.reciprocal(out=sp_rstd[p2], in_=sp_rstd[p2]))
            if tt == 0:
                S.wait("vector", "ac", t_cb)
                S.wait("vector", "ac", t_sb)
            if tr_last[p]:
                S.wait("vector", "pe", tr_last[p])
            x2t_use[p] = S.v(lambda e, x=x2s, d=h2s, p2=p: e.tensor_scalar(
                out=d, in0=x, scalar1=sp_m[p2], scalar2=sp_rstd[p2],
                op0=ALU.subtract, op1=ALU.mult), inc=True)
            S.v(lambda e, d=h2s: e.tensor_tensor(out=d, in0=d, in1=bcC, op=ALU.mult))
            S.v(lambda e, d=h2s: e.tensor_tensor(out=d, in0=d, in1=bcD, op=ALU.add))
            S.v(lambda e, d=h2s, p2=p: e.reduce_max(out=sp_amax[p2], in_=d, axis=AX.X,
                                                    apply_absolute_value=True))
            S.v(lambda e, p2=p: e.tensor_scalar(
                out=sp_amax[p2], in0=sp_amax[p2], scalar1=1e-5, scalar2=None, op0=ALU.max))
            S.v(lambda e, t=tt, p2=p: e.tensor_scalar(
                out=invsx[:, t:t + 1], in0=sp_amax[p2], scalar1=1.0 / 127.0,
                scalar2=None, op0=ALU.mult))
            S.v(lambda e, t=tt, p2=p: e.reciprocal(out=sp_sx[p2], in_=invsx[:, t:t + 1]))

        def p3b_ladder(tt):
            nonlocal last_qx_evac, ic_done
            p = tt % 2
            h2s = h2t[p]
            t_lad = ladder_sd(h2s, sp_sx[p], h2s, E,
                              [lad_y, lad_y2], [lad_r, lad_r2], [lad_b, lad_b2], LW3)
            S.wait("tensor", "dv", t_lad)
            for ec in range(EC):
                bsl = ec % 2
                if qx_tr_use[bsl]:
                    S.wait("tensor", "ac", qx_tr_use[bsl])
                t_tr = S.pe(lambda e, c=ec, b=bsl, d=h2s: e.transpose(
                    out=ps[1 + b][:, 0:128], in_=d[:, c * 128:(c + 1) * 128],
                    identity=idf), inc=True)
                tr_last[p] = t_tr
                S.wait("scalar", "pe", t_tr)
                qx_tr_use[bsl] = S.s(lambda e, c=ec, b=bsl, t=tt: e.activation(
                    out=qxF[:, c * T + t * 128: c * T + t * 128 + 128],
                    in_=ps[1 + b][:, 0:128], func=AF.Copy), inc=True)
            last_qx_evac = qx_tr_use[(EC - 1) % 2]
            ntern = (IC * (tt + 1)) // TC
            while ic_done < ntern:
                emit_tern(ic_done); ic_done += 1

        p3b_stage1(0)
        for tt in range(TC):
            if tt + 1 < TC:
                p3b_stage1(tt + 1)
            p3b_stage2(tt)
            p3b_ladder(tt)
        while ic_done < IC:
            emit_tern(ic_done); ic_done += 1
        tern_done = ["ts0", "ts1"]

        t_scl = None
        for tt in range(TC):
            S.v(lambda e, t=tt: e.tensor_tensor(
                out=sclg[:, t:t + 1], in0=invsx[:, t:t + 1], in1=s_hfg, op=ALU.mult))
            t_scl = S.v(lambda e, t=tt: e.tensor_tensor(
                out=sclu[:, t:t + 1], in0=invsx[:, t:t + 1], in1=s_hfu, op=ALU.mult), inc=True)

        p4 = Bump(O_AFTER_QXF)
        gst = p4.f(EC * 512)
        ternG = [p4.b(EC * 512), p4.b(EC * 512)]
        ternU = [p4.b(EC * 512), p4.b(EC * 512)]
        scrg = p4.b(EC * 512)
        gs_t = [p4.f(512), p4.f(512)]
        hst = [p4.f(512), p4.f(512)]

        fence_sync()
        tG_use = [None, None]; tU_use = [None, None]
        tU_rdy_prev = None
        gst_free = None
        hst_use = [None, None]
        psG_use = [None, None]; psU_use = [None, None]
        for js in range(JS):
            sl = js % 2
            if gst_free:
                S.wait("sync", "ac", gst_free)
            if tG_use[sl]:
                S.wait("sync", "pe", tG_use[sl])
            t_g = S.load(gst, wg_r[js * 128:(js + 1) * 128, :], "gst")
            S.waitp("scalar", t_g)
            if js == 0:
                S.wait("scalar", "dv", stats_done)
            if tU_rdy_prev:
                S.wait("scalar", "dv", tU_rdy_prev)
            S.s(lambda e, d=ternG[sl]: e.activation(
                out=d, in_=gst, func=AF.Sign, scale=s_swg, bias=c_mhalf))
            t_sg = S.s(lambda e: e.activation(
                out=scrg, in_=gst, func=AF.Sign, scale=s_swg, bias=c_phalf), inc=True)
            gst_free = t_sg
            S.wait("vector", "ac", t_sg)
            tG_rdy = S.v(lambda e, d=ternG[sl]: e.tensor_tensor(
                out=d, in0=d, in1=scrg, op=ALU.add), inc=True)
            if tU_use[sl]:
                S.wait("sync", "pe", tU_use[sl])
            S.wait("sync", "ac", t_sg)
            t_u = S.load(gst, wu_r[js * 128:(js + 1) * 128, :], "gst")
            S.waitp("scalar", t_u)
            S.wait("scalar", "dv", tG_rdy)
            S.s(lambda e, d=ternU[sl]: e.activation(
                out=d, in_=gst, func=AF.Sign, scale=s_swu, bias=c_mhalf))
            t_su = S.s(lambda e: e.activation(
                out=scrg, in_=gst, func=AF.Sign, scale=s_swu, bias=c_phalf), inc=True)
            gst_free = t_su
            S.wait("vector", "ac", t_su)
            tU_rdy = S.v(lambda e, d=ternU[sl]: e.tensor_tensor(
                out=d, in0=d, in1=scrg, op=ALU.add), inc=True)
            tU_rdy_prev = tU_rdy
            if js == 0:
                S.wait("tensor", "ac", last_qx_evac)
            S.wait("tensor", "dv", tU_rdy)
            for tt in range(TC):
                bg = tt % 2
                pG = ps[0] if bg == 0 else ps[1]
                pU = ps[2] if bg == 0 else ps[3]
                S.waitp("tensor", psG_use[bg])
                for ec in range(EC):
                    t_gmm = S.pe(lambda e, c=ec, t=tt, p=pG, d=ternG[sl]: e.matmul(
                        out=p[:, 0:512], lhsT=qxF[:, c * T + t * 128: c * T + t * 128 + 128],
                        rhs=d[:, c * 512:(c + 1) * 512], start=(c == 0), stop=(c == EC - 1),
                        skip_group_check=True), inc=(ec == EC - 1))
                S.waitp("tensor", psU_use[bg])
                for ec in range(EC):
                    t_umm = S.pe(lambda e, c=ec, t=tt, p=pU, d=ternU[sl]: e.matmul(
                        out=p[:, 0:512], lhsT=qxF[:, c * T + t * 128: c * T + t * 128 + 128],
                        rhs=d[:, c * 512:(c + 1) * 512], start=(c == 0), stop=(c == EC - 1),
                        skip_group_check=True), inc=(ec == EC - 1))
                tG_use[sl] = t_gmm; tU_use[sl] = t_umm
                ssl = tt % 2
                S.wait("scalar", "pe", t_gmm)
                if js == 0 and tt == 0:
                    S.wait("scalar", "dv", t_scl)
                if psU_use[bg]:
                    S.wait("scalar", "dv", psU_use[bg][1])
                t_si = S.s(lambda e, p=pG, d=gs_t[ssl], t=tt: e.activation(
                    out=d, in_=p[:, 0:512], func=AF.Silu, scale=sclg[:, t:t + 1]), inc=True)
                psG_use[bg] = ("ac", t_si)
                if hst_use[ssl]:
                    S.waitp("vector", hst_use[ssl])
                S.wait("vector", "pe", t_umm)
                S.wait("vector", "ac", t_si)
                t_h = S.v(lambda e, p=pU, d=hst[ssl], g=gs_t[ssl], t=tt: e.scalar_tensor_tensor(
                    out=d, in0=p[:, 0:512], scalar=sclu[:, t:t + 1], in1=g,
                    op0=ALU.mult, op1=ALU.mult), inc=True)
                psU_use[bg] = ("dv", t_h)
                S.wait("gpsimd", "dv", t_h)
                hst_use[ssl] = S.store(
                    h_sc[tt * 128:(tt + 1) * 128, js * 512:(js + 1) * 512], hst[ssl], f"hs{ssl}")
        h_done = ["hs0", "hs1"]

        p5 = Bump(PH)
        lnwB = p5.f(I)
        h2t2 = [p5.f(I), p5.f(I)]
        qh2F = p5.b(IC * 256)
        ternD = [p5.b(E), p5.b(E)]
        x2l = p5.f(E)
        xout = p5.f(E)
        gbB = p5.f(E)
        LW5 = min(1024, I // 2)
        dl_y = [p5.f(LW5) for _ in range(3)]
        dl_r = [p5.f(LW5) for _ in range(3)]
        dl_b = [p5.f(LW5) for _ in range(3)]
        dl_dump = p5.f(512)
        sqparts = p5.f(2 * (I // 512))

        fence_sync()
        t_lnw = bcast(lnwB, lambda o, n: lnw_row[0:1, o:o + n], I, None)
        t_gb = bcast(gbB, modrow(5), E, "signclamp", st_guard=mod_st_done)
        S.wait_total("sync", h_done)
        S.wait_total("sync", tern_done)

        h2buf_use = [None, None]
        ternD_use = [None, None]
        tr2_use = [None, None]
        qh2_evac = None
        prev_epi_dv = None
        xout_st = [None, None]

        ld5_done = [None, None]
        sq5_done = [None, None]
        h2_guard_p = [None, None]

        def p5_stage1(b6, sub):
            tt = 2 * b6 + sub
            hb2 = h2t2[sub]
            if h2buf_use[sub]:
                S.wait("sync", "pe", h2buf_use[sub])
            if h2_guard_p[sub]:
                S.wait("sync", "dv", h2_guard_p[sub])
            t_h = S.load(hb2, h_sc[tt * 128:(tt + 1) * 128, :], f"h2l{sub}")
            ld5_done[sub] = t_h
            S.waitp("scalar", t_h)
            if b6 == 0 and sub == 0:
                S.wait("scalar", "ac", t_lnw)
            t_sq = None
            nstrip = I // 512
            for si in range(nstrip):
                t_sq = S.s(lambda e, o=si * 512, p=si, s2=sub, h=hb2: e.activation(
                    out=dl_dump, in_=h[:, o:o + 512], func=AF.Square,
                    accum_out=sqparts[:, s2 * nstrip + p: s2 * nstrip + p + 1]), inc=True)
            sq5_done[sub] = t_sq

        def p5_stage2lad(b6, sub):
            tt = 2 * b6 + sub
            hb2 = h2t2[sub]
            nstrip = I // 512
            S.waitp("vector", ld5_done[sub])
            S.wait("vector", "ac", sq5_done[sub])
            S.v(lambda e, s2=sub: e.reduce_sum(
                out=sp_ms[s2], in_=sqparts[:, s2 * nstrip:(s2 + 1) * nstrip], axis=AX.X))
            S.v(lambda e, h=hb2: e.tensor_tensor(out=h, in0=h, in1=lnwB, op=ALU.mult))
            S.v(lambda e, h=hb2, s2=sub: e.reduce_max(
                out=sp_rm[s2], in_=h, axis=AX.X, apply_absolute_value=True))
            t_ms = S.v(lambda e, s2=sub: e.tensor_scalar(
                out=sp_ms[s2], in0=sp_ms[s2], scalar1=1.0 / I, scalar2=None,
                op0=ALU.mult), inc=True)
            S.wait("scalar", "dv", t_ms)
            t_r2 = S.s(lambda e, s2=sub: e.activation(
                out=sp_rstd2[s2], in_=sp_ms[s2], func=AF.Sqrt, bias=c_eps6), inc=True)
            S.wait("vector", "ac", t_r2)
            S.v(lambda e, s2=sub: e.reciprocal(out=sp_rstd2[s2], in_=sp_rstd2[s2]))
            S.v(lambda e, s2=sub: e.tensor_tensor(
                out=sp_rm[s2], in0=sp_rm[s2], in1=sp_rstd2[s2], op=ALU.mult))
            S.v(lambda e, s2=sub: e.tensor_scalar(
                out=sp_rm[s2], in0=sp_rm[s2], scalar1=1e-5, scalar2=None, op0=ALU.max))
            S.v(lambda e, s2=sub, p=b6 % 2: e.tensor_scalar(
                out=s_invs2[p][s2], in0=sp_rm[s2], scalar1=1.0 / 127.0,
                scalar2=None, op0=ALU.mult))
            S.v(lambda e, s2=sub, p=b6 % 2: e.reciprocal(out=sp_sx2[s2], in_=s_invs2[p][s2]))
            S.v(lambda e, s2=sub: e.tensor_tensor(
                out=sp_sx2[s2], in0=sp_sx2[s2], in1=sp_rstd2[s2], op=ALU.mult))
            t_lad = ladder_sd(hb2, sp_sx2[sub], hb2, I, dl_y, dl_r, dl_b, LW5)
            h2_guard_p[sub] = t_lad
            return t_lad

        def emit_transposes(b6, t_lad):
            nonlocal qh2_evac, h2buf_use
            S.wait("tensor", "dv", t_lad)
            if prev_epi_dv:
                S.wait("tensor", "dv", prev_epi_dv)
            for sub in range(2):
                last_tr = None
                for ic in range(IC):
                    bsl = ic % 2
                    if tr2_use[bsl]:
                        S.wait("tensor", "ac", tr2_use[bsl])
                    t_tr = S.pe(lambda e, c=ic, b=bsl, s2=sub: e.transpose(
                        out=ps[1 + b][:, 0:128],
                        in_=h2t2[s2][:, c * 128:c * 128 + 128],
                        identity=idf), inc=True)
                    S.wait("scalar", "pe", t_tr)
                    tr2_use[bsl] = S.s(lambda e, c=ic, b=bsl, s2=sub: e.activation(
                        out=qh2F[:, c * 256 + s2 * 128: c * 256 + s2 * 128 + 128],
                        in_=ps[1 + b][:, 0:128], func=AF.Copy), inc=True)
                    last_tr = t_tr
                h2buf_use[sub] = last_tr
            qh2_evac = tr2_use[(IC - 1) % 2]

        def emit_down(b6):
            nonlocal prev_epi_dv, ternD_use
            S.wait("tensor", "ac", qh2_evac)
            if prev_epi_dv:
                S.wait("tensor", "dv", prev_epi_dv)
            mm_fin = None
            for ic in range(IC):
                sl = ic % 2
                if ternD_use[sl]:
                    S.wait("sync", "pe", ternD_use[sl])
                t_td = S.load(ternD[sl], tern_sc[ic * 128:(ic + 1) * 128, :], f"td{sl}")
                S.waitp("tensor", t_td)
                lastmm = None
                for tsub in range(2):
                    for ft in range(NFT):
                        bank = tsub * NFT + ft
                        lastmm = S.pe(lambda e, c=ic, ts=tsub, f=ft, b=bank, d=ternD[sl]:
                                      e.matmul(out=ps[b][:, 0:NF],
                                               lhsT=qh2F[:, c * 256 + ts * 128: c * 256 + ts * 128 + 128],
                                               rhs=d[:, f * NF:(f + 1) * NF],
                                               start=(c == 0), stop=(c == IC - 1),
                                               skip_group_check=True),
                                      inc=(tsub == 1 and ft == NFT - 1))
                ternD_use[sl] = lastmm
                mm_fin = lastmm
            S.wait("vector", "pe", mm_fin)
            if b6 == 0:
                S.wait("vector", "ac", t_gb)
            for sub in range(2):
                tt = 2 * b6 + sub
                if prev_epi_dv:
                    S.wait("sync", "dv", prev_epi_dv)
                t_x2 = S.load(x2l, x2_sc[tt * 128:(tt + 1) * 128, :], "x2l")
                S.waitp("vector", t_x2)
                if xout_st[0]:
                    S.waitp("vector", xout_st[0])
                S.v(lambda e, s2=sub, p=b6 % 2: e.tensor_tensor(
                    out=s_scld[s2], in0=s_invs2[p][s2], in1=s_hfd, op=ALU.mult))
                t_epi = None
                for ft in range(NFT):
                    bank = sub * NFT + ft
                    d_ap = xout[:, ft * NF:(ft + 1) * NF]
                    S.v(lambda e, d=d_ap, b=bank, s2=sub: e.tensor_scalar(
                        out=d, in0=ps[b][:, 0:NF], scalar1=s_scld[s2], scalar2=None,
                        op0=ALU.mult))
                    S.v(lambda e, d=d_ap, f=ft: e.tensor_tensor(
                        out=d, in0=d, in1=gbB[:, f * NF:(f + 1) * NF], op=ALU.mult))
                    t_epi = S.v(lambda e, d=d_ap, f=ft: e.tensor_tensor(
                        out=d, in0=d, in1=x2l[:, f * NF:(f + 1) * NF], op=ALU.add), inc=True)
                S.wait("gpsimd", "dv", t_epi)
                xout_st[0] = S.store(y_out[tt * 128:(tt + 1) * 128, :], xout, "yo0")
                prev_epi_dv = t_epi

        pending = None
        for b6 in range(NB6):
            p5_stage1(b6, 0)
            p5_stage2lad(b6, 0)
            p5_stage1(b6, 1)
            t_lad1 = p5_stage2lad(b6, 1)
            if pending is not None:
                emit_down(pending)
            emit_transposes(b6, t_lad1)
            pending = b6
        emit_down(pending)

        S.wait_total("sync", ["yo0", "yo1"])

        for nm in ("pe", "dv", "ac"):
            S.semh[nm] = es.enter_context(nc.semaphore(nm))
        for nm in sorted(S.sems):
            S.semh[nm] = es.enter_context(nc.semaphore(nm))

        block = es.enter_context(nc.Block())

        @block.sync
        def _(sync):
            for fn in S.ops["sync"]:
                fn(sync)

        @block.gpsimd
        def _(gpsimd):
            for fn in S.ops["gpsimd"]:
                fn(gpsimd)

        @block.tensor
        def _(tensor):
            for fn in S.ops["tensor"]:
                fn(tensor)

        @block.vector
        def _(vector):
            for fn in S.ops["vector"]:
                fn(vector)

        @block.scalar
        def _(scalar):
            for fn in S.ops["scalar"]:
                fn(scalar)

    return nc


def make_tables(T):
    D = 128
    j = np.arange(D // 2, dtype=np.float64)
    thetas = 10000.0 ** (-2.0 * j / D)
    m = np.arange(1, T + 1, dtype=np.float64)
    ang = np.concatenate([thetas, thetas])[:, None] * m[None, :]
    c = 128.0 ** (-0.25)
    cosv = np.cos(ang) * c
    sinv = np.sin(ang) * c
    sgn = np.ones((D, 1)); sgn[:D // 2] = -1.0
    sinv = sinv * sgn
    chi = cosv.astype(np.float32)
    clo = (cosv - chi.astype(np.float64)).astype(np.float32)
    shi = sinv.astype(np.float32)
    slo = (sinv - shi.astype(np.float64)).astype(np.float32)
    return chi, clo, shi, slo


def make_masks():
    i = np.arange(128)[:, None]
    jq = np.arange(512)[None, :]
    return np.concatenate([
        np.where(128 * rr + i <= jq, 0.0, NEGBIG).astype(np.float32)
        for rr in range(4)], axis=1)


def host_inputs(T, E, I, Wq, bq, Wk, bk, Wv, bv, Wo, bo, wg, wu, wd, lnw, wmod):
    EC = E // 128
    FT = E // 128
    NV = min(512, E); FSV = E // NV
    N2 = min(256, E); FS2 = E // N2
    JS = I // 512
    asc = np.ascontiguousarray
    f = np.float32
    d = {}
    d["wq_t"] = asc(np.asarray(Wq, f).reshape(FT, 128, EC, 128).transpose(0, 3, 2, 1).reshape(E, E))
    d["wk_t"] = asc(np.asarray(Wk, f).reshape(FT, 128, EC, 128).transpose(0, 3, 2, 1).reshape(E, E))
    d["wv_r"] = asc(np.asarray(Wv, f).T.reshape(EC, 128, FSV, NV).transpose(2, 1, 0, 3).reshape(FSV * 128, EC * NV))
    d["wo_r"] = asc(np.asarray(Wo, f).T.reshape(EC, 128, FS2, N2).transpose(2, 1, 0, 3).reshape(FS2 * 128, EC * N2))
    d["bq_c"] = asc(np.asarray(bq, f).reshape(EC, 128).T)
    d["bk_c"] = asc(np.asarray(bk, f).reshape(EC, 128).T)
    d["bv_row"] = asc(np.asarray(bv, f).reshape(1, E))
    d["bo_row"] = asc(np.asarray(bo, f).reshape(1, E))
    d["wmod"] = asc(np.asarray(wmod, f))
    d["wg_r"] = asc(np.asarray(wg, f).T.reshape(EC, 128, JS, 512).transpose(2, 1, 0, 3).reshape(JS * 128, EC * 512))
    d["wu_r"] = asc(np.asarray(wu, f).T.reshape(EC, 128, JS, 512).transpose(2, 1, 0, 3).reshape(JS * 128, EC * 512))
    d["wd_t"] = asc(np.asarray(wd, f).T)
    d["lnw_row"] = asc(np.asarray(lnw, f).reshape(1, I))
    chi, clo, shi, slo = make_tables(T)
    d["chi_in"] = chi; d["clo_in"] = clo; d["shi_in"] = shi; d["slo_in"] = slo
    d["mask_in"] = make_masks()
    d["id_in"] = np.eye(128, dtype=np.float32)
    return d


def kernel(L0_attn_x, L0_attn_Wq, L0_attn_bq, L0_attn_Wk, L0_attn_bk,
           L0_attn_Wv, L0_attn_bv, L0_attn_Wo, L0_attn_bo,
           L0_ffn_w_gate, L0_ffn_w_up, L0_ffn_w_down, L0_ffn_ln_weight,
           L0_c, L0_wmod):
    T, E, I = 2048, 2048, 8192
    xs = np.asarray(L0_attn_x, np.float32)
    cs = np.asarray(L0_c, np.float32)
    B = xs.shape[0]
    EC = E // 128
    nc = build_program(T, E, I)
    shared = host_inputs(T, E, I, L0_attn_Wq, L0_attn_bq, L0_attn_Wk, L0_attn_bk,
                         L0_attn_Wv, L0_attn_bv, L0_attn_Wo, L0_attn_bo,
                         L0_ffn_w_gate, L0_ffn_w_up, L0_ffn_w_down,
                         L0_ffn_ln_weight, L0_wmod)
    in_maps = []
    for b in range(B):
        m = dict(shared)
        m["x_in"] = np.ascontiguousarray(xs[b])
        m["c_col"] = np.ascontiguousarray(cs[b].reshape(EC, 128).T)
        in_maps.append(m)
    res = run_bass_kernel_spmd(nc, in_maps, core_ids=list(range(B)))
    out = np.stack([np.asarray(res.results[b]["y_out"]) for b in range(B)], axis=0)
    return out.astype(np.float32)


# version 10
# speedup vs baseline: 1.0986x; 1.0266x over previous
import numpy as np
from contextlib import ExitStack
import concourse.bass as bass
import concourse.mybir as mybir
from concourse.bass_utils import run_bass_kernel_spmd

F32 = mybir.dt.float32
BF16 = mybir.dt.bfloat16
AF = mybir.ActivationFunctionType
ALU = mybir.AluOpType
AX = mybir.AxisListType

EXP_BIAS = -16.0
NEGBIG = -1.0e38
ARENA = 53152


class Sched:
    ENGS = ("sync", "gpsimd", "tensor", "vector", "scalar")

    def __init__(self):
        self.ops = {e: [] for e in self.ENGS}
        self.cnt = {"pe": 0, "dv": 0, "ac": 0}
        self.sems = {}
        self.semh = {}
        self.ps6 = None

    def _add(self, eng, fn):
        self.ops[eng].append(fn)

    def op(self, eng, build, sem=None, inc=0):
        tgt = None
        if sem is not None:
            self.cnt[sem] += inc
            tgt = self.cnt[sem]

        def fn(e, b=build, s=sem, i=inc, self_=self):
            inst = b(e)
            if s is not None:
                inst.then_inc(self_.semh[s], i)
        self._add(eng, fn)
        return tgt

    def wait(self, eng, sem, val):
        if not val:
            return
        def fn(e, s=sem, v=val, self_=self):
            e.wait_ge(self_.semh[s], v)
        self._add(eng, fn)

    def waitp(self, eng, pair):
        if pair:
            self.wait(eng, pair[0], pair[1])

    def wait_total(self, eng, names):
        for n in names:
            if self.sems.get(n):
                self.wait(eng, n, self.sems[n])

    def v(self, build, inc=False):
        self._add("vector", lambda e: e.drain())
        return self.op("vector", build, "dv" if inc else None, 1 if inc else 0)

    def s(self, build, inc=False):
        self._add("scalar", lambda e: e.drain())
        return self.op("scalar", build, "ac" if inc else None, 1 if inc else 0)

    def pe(self, build, inc=False):
        return self.op("tensor", build, "pe" if inc else None, 1 if inc else 0)

    def _dma(self, eng, out, in_, sem):
        self.sems[sem] = self.sems.get(sem, 0) + 16
        tgt = self.sems[sem]

        def fn(e, o=out, i=in_, s=sem, self_=self):
            e.dma_start(out=o, in_=i).then_inc(self_.semh[s], 16)
        self._add(eng, fn)
        return (sem, tgt)

    def load(self, out, in_, sem):
        return self._dma("sync", out, in_, sem)

    def store(self, out, in_, sem):
        return self._dma("gpsimd", out, in_, sem)


def build_program(T, E, I):
    H = E // 128
    EC = E // 128
    TC = T // 128
    QT = T // 512
    NV = min(512, E); FSV = E // NV
    N2 = min(256, E); FS2 = E // N2
    JS = I // 512
    IC = I // 128
    NB6 = T // 256
    NF = min(512, E); NFT = E // NF
    STRIP_E = min(1024, E)
    STRIP_I = min(1024, I)

    nc = bass.Bass("TRN2", target_bir_lowering=False)
    dt = lambda n, s, d=F32, k="ExternalInput": nc.dram_tensor(n, s, d, kind=k).ap()

    x_in = dt("x_in", [T, E])
    wq_t = dt("wq_t", [E, E]); wk_t = dt("wk_t", [E, E])
    wv_r = dt("wv_r", [FSV * 128, EC * NV])
    wo_r = dt("wo_r", [FS2 * 128, EC * N2])
    bq_c = dt("bq_c", [128, EC]); bk_c = dt("bk_c", [128, EC])
    bv_row = dt("bv_row", [1, E]); bo_row = dt("bo_row", [1, E])
    wmod = dt("wmod", [E, 6 * E])
    c_col = dt("c_col", [128, EC])
    wg_r = dt("wg_r", [JS * 128, EC * 512])
    wu_r = dt("wu_r", [JS * 128, EC * 512])
    wd_t = dt("wd_t", [I, E])
    lnw_row = dt("lnw_row", [1, I])
    chi_in = dt("chi_in", [128, T]); clo_in = dt("clo_in", [128, T])
    shi_in = dt("shi_in", [128, T]); slo_in = dt("slo_in", [128, T])
    mask_in = dt("mask_in", [128, 4 * 512])
    id_in = dt("id_in", [128, 128])

    q_sc = dt("q_sc", [H * 128, T], F32, "Internal")
    k_sc = dt("k_sc", [H * 128, T], F32, "Internal")
    v_sc = dt("v_sc", [H * T, 128], F32, "Internal")
    x2_sc = dt("x2_sc", [T, E], F32, "Internal")
    h_sc = dt("h_sc", [T, I], F32, "Internal")
    tern_sc = dt("tern_sc", [I, E], BF16, "Internal")
    mod_sc = dt("mod_sc", [1, 6 * E], F32, "Internal")
    y_out = dt("y_out", [T, E], F32, "ExternalOutput")

    S = Sched()
    es = ExitStack()
    with es:
        AR = es.enter_context(nc.sbuf_tensor("arena", [128, ARENA], F32))
        ARB = AR.bitcast(BF16)

        def f32(off, n):
            assert off + n <= ARENA, ("f32 overflow", off, n)
            return AR[:, off:off + n]

        def b16(off_f32, n_b16):
            assert 2 * off_f32 + n_b16 <= 2 * ARENA, ("b16 overflow", off_f32, n_b16)
            return ARB[:, 2 * off_f32:2 * off_f32 + n_b16]

        class Bump:
            def __init__(self, start):
                self.off = start
            def f(self, n):
                o = self.off; self.off += n
                assert self.off <= ARENA, ("bump overflow", self.off)
                return f32(o, n)
            def b(self, n_b16):
                n = (n_b16 + 1) // 2
                o = self.off; self.off += n
                assert self.off <= ARENA, ("bump overflow b16", self.off)
                return b16(o, n_b16)

        st_ = Bump(0)
        idf = st_.f(128)
        idb = st_.b(128)
        rowst_full = st_.f(512); rowst = rowst_full[0:1, :]
        rowst2 = st_.f(512)[0:1, :]
        rowst3 = st_.f(512)[0:1, :]
        ones_blk = st_.f(128)
        ones_col = ones_blk[:, 0:1]
        ones_row = ones_blk[0:1, :]
        scal = st_.f(64)
        sm = lambda i: scal[:, i:i + 1]
        (s_s1, s_ss, s_m, s_mm2, s_var, s_rstd, s_amax, s_sx, s_zr, s_swg,
         s_hfg, s_swu, s_hfu, s_swd, s_hfd, s_ms, s_rstd2, s_rm) = [sm(i) for i in range(18)]
        s_invs2 = [[sm(18), sm(19)], [sm(28), sm(29)]]
        s_scld = [sm(20), sm(21)]
        s_t11 = sm(22)
        c_eps5, c_eps6, c_mhalf, c_phalf, c_expb = sm(23), sm(24), sm(25), sm(26), sm(27)
        sp_s1 = [sm(30), sm(31)]; sp_ss = [sm(32), sm(33)]
        sp_m = [sm(34), sm(35)]; sp_mm2 = [sm(36), sm(37)]
        sp_var = [sm(38), sm(39)]; sp_rstd = [sm(40), sm(41)]
        sp_amax = [sm(42), sm(43)]; sp_sx = [sm(44), sm(45)]
        sp_ms = [sm(46), sm(47)]; sp_rstd2 = [sm(48), sm(49)]
        sp_rm = [sm(50), sm(51)]; sp_sx2 = [sm(52), sm(53)]
        c_m128 = sm(54)
        bq_sb = st_.f(EC); bk_sb = st_.f(EC)
        cc_sb = st_.f(EC); scc_sb = st_.f(EC)
        invsx = st_.f(TC); sclg = st_.f(TC); sclu = st_.f(TC)
        parts = [st_.f(128) for _ in range(3)]
        PH = st_.off

        ps = [es.enter_context(nc.psum_tensor(f"ps{i}", [128, 512], F32)) for i in range(8)]
        psb = ps[7].bitcast(BF16)


        def ladder_sd(src_ap, scale_ap, dst_ap, width, ytiles, rtiles, stiles, sw):
            strips = [(o, min(sw, width - o)) for o in range(0, width, sw)]
            NG = len(ytiles)
            t_last = None
            for g0 in range(0, len(strips), NG):
                grp = strips[g0:g0 + NG]
                ydv = []
                for i, (o, w) in enumerate(grp):
                    ydv.append(S.v(lambda e, d=ytiles[i][:, 0:w], s0=src_ap[:, o:o + w],
                                   sc=scale_ap: e.tensor_scalar(
                        out=d, in0=s0, scalar1=sc, scalar2=128.5,
                        op0=ALU.mult, op1=ALU.add), inc=True))
                rdv = list(ydv)
                rsrc = [ytiles[i] for i in range(len(grp))]
                for li, t2 in enumerate((128.0, 64.0, 32.0, 16.0, 8.0, 4.0, 2.0, 1.0)):
                    sac = []
                    for i, (o, w) in enumerate(grp):
                        S.wait("scalar", "dv", rdv[i])
                        sac.append(S.s(lambda e, d=stiles[i][:, 0:w],
                                       s0=rsrc[i][:, 0:w]: e.activation(
                            out=d, in_=s0, func=AF.Sign, bias=c_m128), inc=True))
                    for i, (o, w) in enumerate(grp):
                        S.wait("vector", "ac", sac[i])
                        rdv[i] = S.v(lambda e, d=rtiles[i][:, 0:w], s0=stiles[i][:, 0:w],
                                     s1=rsrc[i][:, 0:w], tv=t2 / 2: e.scalar_tensor_tensor(
                            out=d, in0=s0, scalar=-tv, in1=s1,
                            op0=ALU.mult, op1=ALU.add), inc=True)
                    rsrc = [rtiles[i] for i in range(len(grp))]
                for i, (o, w) in enumerate(grp):
                    t_last = S.v(lambda e, d=dst_ap[:, o:o + w], s0=ytiles[i][:, 0:w],
                                 s1=rtiles[i][:, 0:w]: e.scalar_tensor_tensor(
                        out=d, in0=s0, scalar=-0.5, in1=s1,
                        op0=ALU.add, op1=ALU.subtract), inc=True)
            return t_last

        def fence_sync():
            S.wait("sync", "pe", S.cnt["pe"])
            S.wait("sync", "dv", S.cnt["dv"])
            S.wait("sync", "ac", S.cnt["ac"])
            S.wait_total("sync", sorted(S.sems))

        def bcast(dst, src_row_ap_fn, width, transform, st_guard=None):
            last = None
            seg = min(512, width)
            rw = rowst_full[0:1, 0:seg]
            rw2 = rowst2[0:1, 0:seg]
            rw3 = rowst3[0:1, 0:seg]
            for sl0 in range(width // seg):
                if st_guard:
                    S.wait_total("sync", st_guard)
                t_l = S.load(rw, src_row_ap_fn(sl0 * seg, seg), "row")
                t_r = None
                if transform == "add1":
                    S.waitp("vector", t_l)
                    t_r = S.v(lambda e: e.tensor_scalar(
                        out=rw, in0=rw, scalar1=1.0, scalar2=None, op0=ALU.add), inc=True)
                elif transform == "signclamp":
                    S.waitp("vector", t_l)
                    S.v(lambda e: e.tensor_scalar(
                        out=rw2, in0=rw, scalar1=0.0, scalar2=None, op0=ALU.is_ge))
                    S.v(lambda e: e.tensor_scalar(
                        out=rw2, in0=rw2, scalar1=2.0, scalar2=-1.0,
                        op0=ALU.mult, op1=ALU.add))
                    S.v(lambda e: e.tensor_scalar(
                        out=rw3, in0=rw, scalar1=-1.0, scalar2=None, op0=ALU.mult))
                    S.v(lambda e: e.tensor_tensor(out=rw, in0=rw, in1=rw3, op=ALU.max))
                    S.v(lambda e: e.tensor_scalar(
                        out=rw, in0=rw, scalar1=0.25, scalar2=None, op0=ALU.max))
                    t_r = S.v(lambda e: e.tensor_tensor(
                        out=rw, in0=rw, in1=rw2, op=ALU.mult), inc=True)
                if t_r:
                    S.wait("tensor", "dv", t_r)
                else:
                    S.waitp("tensor", t_l)
                S.waitp("tensor", S.ps6)
                t_mm = S.pe(lambda e: e.matmul(out=ps[6][:, 0:seg], lhsT=ones_row,
                                               rhs=rw, start=True, stop=True), inc=True)
                S.wait("scalar", "pe", t_mm)
                t_e = S.s(lambda e, d=dst[:, sl0 * seg:(sl0 + 1) * seg]: e.activation(
                    out=d, in_=ps[6][:, 0:seg], func=AF.Copy), inc=True)
                S.ps6 = ("ac", t_e)
                S.wait("sync", "pe", t_mm)
                last = t_e
            return last

        S.load(idf, id_in[:, :], "ini")
        S.load(bq_sb, bq_c[:, :], "ini"); S.load(bk_sb, bk_c[:, :], "ini")
        t_cc = S.load(cc_sb, c_col[:, :], "ini")
        S.waitp("vector", t_cc)
        S.v(lambda e: e.memset(ones_blk, 1.0))
        S.v(lambda e: e.memset(c_eps5, 1e-5))
        S.v(lambda e: e.memset(c_eps6, 1e-6))
        S.v(lambda e: e.memset(c_mhalf, -0.5))
        S.v(lambda e: e.memset(c_phalf, 0.5))
        S.v(lambda e: e.memset(c_expb, EXP_BIAS))
        S.v(lambda e: e.memset(c_m128, -128.0))
        S.v(lambda e: e.tensor_copy(out=idb, in_=idf), inc=True)
        S.waitp("scalar", t_cc)
        t_silu = S.s(lambda e: e.activation(out=scc_sb, in_=cc_sb, func=AF.Silu), inc=True)

        p0 = Bump(PH)
        wch = [p0.f(6 * E), p0.f(6 * E)]
        acc = p0.f(6 * E)
        modst = [f32(PH, 512)[0:1, :], f32(PH + 512, 512)[0:1, :]]
        use_t = [None, None]
        for ec in range(EC):
            sl = ec % 2
            if use_t[sl]:
                S.wait("sync", "dv", use_t[sl])
            t_w = S.load(wch[sl], wmod[ec * 128:(ec + 1) * 128, :], f"wch{sl}")
            S.waitp("vector", t_w)
            if ec == 0:
                S.wait("vector", "ac", t_silu)
                use_t[sl] = S.v(lambda e, w=wch[sl]: e.tensor_scalar(
                    out=acc, in0=w, scalar1=scc_sb[:, 0:1], scalar2=None,
                    op0=ALU.mult), inc=True)
            else:
                use_t[sl] = S.v(lambda e, w=wch[sl], c=ec: e.scalar_tensor_tensor(
                    out=acc, in0=w, scalar=scc_sb[:, c:c + 1], in1=acc,
                    op0=ALU.mult, op1=ALU.add), inc=True)
        t_acc = use_t[(EC - 1) % 2]
        S.wait("tensor", "dv", t_acc)
        mst_use = [None, None]
        t_cp = None
        for nt in range(6 * E // 512):
            sl = nt % 2
            if t_cp:
                S.wait("tensor", "dv", t_cp)
            t_mm = S.pe(lambda e, n=nt: e.matmul(
                out=ps[0][:, 0:512], lhsT=ones_blk,
                rhs=acc[:, n * 512:(n + 1) * 512], start=True, stop=True), inc=True)
            if mst_use[sl]:
                S.waitp("vector", mst_use[sl])
            S.wait("vector", "pe", t_mm)
            t_cp = S.v(lambda e, d=modst[sl]: e.tensor_copy(
                out=d, in_=ps[0][0:1, 0:512]), inc=True)
            S.wait("gpsimd", "dv", t_cp)
            mst_use[sl] = S.store(mod_sc[0:1, nt * 512:(nt + 1) * 512], modst[sl], f"mo{sl}")
        mod_st_done = ["mo0", "mo1"]

        def modrow(g):
            return lambda o, n, gg=g: mod_sc[0:1, gg * E + o: gg * E + o + n]

        p0b = Bump(PH)
        hF = p0b.f(EC * T)
        O_AFTER_HF = p0b.off
        bc0 = p0b.f(E); bc1 = p0b.f(E)
        xt = [p0b.f(E), p0b.f(E)]
        h1t = p0b.f(E)
        t_b0 = bcast(bc0, modrow(1), E, "add1", st_guard=mod_st_done)
        t_b1 = bcast(bc1, modrow(0), E, None, st_guard=mod_st_done)
        xt_use = [None, None]
        tr_use = [None, None]
        last_hF_evac = None
        for tt in range(TC):
            sl = tt % 2
            if xt_use[sl]:
                S.wait("sync", "dv", xt_use[sl])
            t_x = S.load(xt[sl], x_in[tt * 128:(tt + 1) * 128, :], f"xt{sl}")
            S.waitp("vector", t_x)
            S.v(lambda e, x=xt[sl]: e.reduce_sum(out=s_s1, in_=x, axis=AX.X))
            S.waitp("scalar", t_x)
            t_sq = S.s(lambda e, x=xt[sl]: e.activation(
                out=h1t, in_=x, func=AF.Square, accum_out=s_ss), inc=True)
            S.v(lambda e: e.tensor_scalar(out=s_m, in0=s_s1, scalar1=1.0 / E,
                                          scalar2=None, op0=ALU.mult))
            S.v(lambda e: e.tensor_tensor(out=s_mm2, in0=s_m, in1=s_m, op=ALU.mult))
            S.wait("vector", "ac", t_sq)
            t_var = S.v(lambda e: e.scalar_tensor_tensor(
                out=s_var, in0=s_ss, scalar=1.0 / E, in1=s_mm2,
                op0=ALU.mult, op1=ALU.subtract), inc=True)
            S.wait("scalar", "dv", t_var)
            t_rstd = S.s(lambda e: e.activation(
                out=s_rstd, in_=s_var, func=AF.Sqrt, bias=c_eps5), inc=True)
            S.wait("vector", "ac", t_rstd)
            S.v(lambda e: e.reciprocal(out=s_rstd, in_=s_rstd))
            if tt == 0:
                S.wait("vector", "ac", t_b0)
                S.wait("vector", "ac", t_b1)
            xt_use[sl] = S.v(lambda e, x=xt[sl]: e.tensor_scalar(
                out=h1t, in0=x, scalar1=s_m, scalar2=s_rstd,
                op0=ALU.subtract, op1=ALU.mult), inc=True)
            S.v(lambda e: e.tensor_tensor(out=h1t, in0=h1t, in1=bc0, op=ALU.mult))
            t_h1 = S.v(lambda e: e.tensor_tensor(out=h1t, in0=h1t, in1=bc1, op=ALU.add), inc=True)
            S.wait("tensor", "dv", t_h1)
            for ec in range(EC):
                bk = ec % 2
                if tr_use[bk]:
                    S.wait("tensor", "ac", tr_use[bk])
                t_tr = S.pe(lambda e, c=ec, b=bk: e.transpose(
                    out=ps[1 + b][:, 0:128], in_=h1t[:, c * 128:(c + 1) * 128],
                    identity=idf), inc=True)
                S.wait("scalar", "pe", t_tr)
                tr_use[bk] = S.s(lambda e, c=ec, b=bk, t=tt: e.activation(
                    out=hF[:, c * T + t * 128: c * T + t * 128 + 128],
                    in_=ps[1 + b][:, 0:128], func=AF.Copy), inc=True)
            last_hF_evac = tr_use[(EC - 1) % 2]

        p1 = Bump(O_AFTER_HF)
        chi = p1.f(T); clo = p1.f(T); shi = p1.f(T); slo = p1.f(T)
        wt = [p1.f(E), p1.f(E)]
        qkst = [p1.f(512), p1.f(512)]
        swpst = [p1.f(512), p1.f(512)]
        rs1 = [p1.f(512), p1.f(512)]
        rs2 = p1.f(512)
        fence_sync()
        S.load(chi, chi_in[:, :], "tab"); S.load(clo, clo_in[:, :], "tab")
        S.load(shi, shi_in[:, :], "tab")
        t_tab = S.load(slo, slo_in[:, :], "tab")
        S.waitp("vector", t_tab)
        S.wait("tensor", "ac", last_hF_evac)
        wt_use = [None, None]
        st_use = [None, None]
        sw_use = [None, None]
        rs_store = [None, None]
        psA_use = [None, None]
        for (wdram, bias_sb, dst_sc) in ((wq_t, bq_sb, q_sc), (wk_t, bk_sb, k_sc)):
            for ft in range(EC):
                sl = ft % 2
                if wt_use[sl]:
                    S.wait("sync", "pe", wt_use[sl])
                t_w = S.load(wt[sl], wdram[ft * 128:(ft + 1) * 128, :], f"wt{sl}")
                S.waitp("tensor", t_w)
                for tq in range(QT):
                    bk = tq % 2
                    pA = ps[0] if bk == 0 else ps[3]
                    S.waitp("tensor", psA_use[bk])
                    for ec in range(EC):
                        t_mm = S.pe(lambda e, c=ec, q=tq, w=wt[sl], p=pA: e.matmul(
                            out=p[:, 0:512], lhsT=w[:, c * 128:(c + 1) * 128],
                            rhs=hF[:, c * T + q * 512: c * T + (q + 1) * 512],
                            start=(c == 0), stop=(c == EC - 1),
                            skip_group_check=True), inc=(ec == EC - 1))
                    wt_use[sl] = t_mm
                    ssl = (ft * QT + tq) % 2
                    if st_use[ssl]:
                        S.wait("scalar", "dv", st_use[ssl])
                    S.wait("scalar", "pe", t_mm)
                    t_ev = S.s(lambda e, f=ft, d=qkst[ssl], p=pA, b=bias_sb: e.activation(
                        out=d, in_=p[:, 0:512], func=AF.Identity, bias=b[:, f:f + 1]), inc=True)
                    psA_use[bk] = ("ac", t_ev)
                    if sw_use[ssl]:
                        S.wait("gpsimd", "dv", sw_use[ssl])
                    S.wait("gpsimd", "ac", t_ev)
                    S.store(swpst[ssl][0:64, :], qkst[ssl][64:128, :], f"sw{ssl}")
                    t_sw = S.store(swpst[ssl][64:128, :], qkst[ssl][0:64, :], f"sw{ssl}")
                    if rs_store[ssl]:
                        S.waitp("vector", rs_store[ssl])
                    S.waitp("vector", t_sw)
                    c0, c1 = tq * 512, (tq + 1) * 512
                    r1 = rs1[ssl]
                    S.v(lambda e, s=qkst[ssl], a=c0, b2=c1, r=r1: e.tensor_tensor(
                        out=r, in0=s, in1=chi[:, a:b2], op=ALU.mult))
                    S.v(lambda e, s=qkst[ssl], a=c0, b2=c1: e.tensor_tensor(
                        out=rs2, in0=s, in1=clo[:, a:b2], op=ALU.mult))
                    st_use[ssl] = S.v(lambda e, r=r1: e.tensor_tensor(
                        out=r, in0=r, in1=rs2, op=ALU.add), inc=True)
                    S.v(lambda e, s=swpst[ssl], a=c0, b2=c1: e.tensor_tensor(
                        out=rs2, in0=s, in1=shi[:, a:b2], op=ALU.mult))
                    S.v(lambda e, r=r1: e.tensor_tensor(out=r, in0=r, in1=rs2, op=ALU.add))
                    sw_use[ssl] = S.v(lambda e, s=swpst[ssl], a=c0, b2=c1: e.tensor_tensor(
                        out=rs2, in0=s, in1=slo[:, a:b2], op=ALU.mult), inc=True)
                    t_rot = S.v(lambda e, r=r1: e.tensor_tensor(
                        out=r, in0=r, in1=rs2, op=ALU.add), inc=True)
                    S.wait("gpsimd", "dv", t_rot)
                    rs_store[ssl] = S.store(
                        dst_sc[ft * 128:(ft + 1) * 128, tq * 512:(tq + 1) * 512], r1, f"rq{ssl}")

        pv = Bump(O_AFTER_HF)
        vsl = pv.f(EC * NV)
        vst = [pv.f(NV), pv.f(NV)]
        bvB = pv.f(E)
        fence_sync()
        t_bv = bcast(bvB, lambda o, n: bv_row[0:1, o:o + n], E, None)
        vst_use = [None, None]
        last_vmm = None
        for fs in range(FSV):
            if last_vmm:
                S.wait("sync", "pe", last_vmm)
            t_v = S.load(vsl, wv_r[fs * 128:(fs + 1) * 128, :], "vsl")
            S.waitp("tensor", t_v)
            for tt in range(TC):
                bk = tt % 2
                pA = ps[0] if bk == 0 else ps[3]
                S.waitp("tensor", psA_use[bk])
                for ec in range(EC):
                    t_mm = S.pe(lambda e, c=ec, t=tt, p=pA: e.matmul(
                        out=p[:, 0:NV], lhsT=hF[:, c * T + t * 128: c * T + t * 128 + 128],
                        rhs=vsl[:, c * NV:(c + 1) * NV], start=(c == 0), stop=(c == EC - 1),
                        skip_group_check=True), inc=(ec == EC - 1))
                last_vmm = t_mm
                ssl = tt % 2
                if vst_use[ssl]:
                    S.waitp("vector", vst_use[ssl])
                S.wait("vector", "pe", t_mm)
                if fs == 0 and tt == 0:
                    S.wait("vector", "ac", t_bv)
                t_vv = S.v(lambda e, d=vst[ssl], p=pA, f=fs: e.tensor_tensor(
                    out=d, in0=p[:, 0:NV], in1=bvB[:, f * NV:(f + 1) * NV],
                    op=ALU.add), inc=True)
                psA_use[bk] = ("dv", t_vv)
                S.wait("gpsimd", "dv", t_vv)
                for hh in range(NV // 128):
                    h = fs * (NV // 128) + hh
                    vst_use[ssl] = S.store(
                        v_sc[h * T + tt * 128: h * T + (tt + 1) * 128, :],
                        vst[ssl][:, hh * 128:(hh + 1) * 128], f"vs{ssl}")
        spill_done = ["rq0", "rq1", "vs0", "vs1"]

        p2 = Bump(O_AFTER_HF)
        qrot = [p2.f(T), p2.f(T)]
        krot = [p2.f(T), p2.f(T)]
        VRW = TC * 129
        vrhs = [p2.f(VRW + 7), p2.f(VRW + 7)]
        pT = [p2.f(512), p2.f(512), p2.f(512)]
        yst = [p2.f(128), p2.f(128)]
        masks_t = p2.f(4 * 512)
        masks = [masks_t[:, i * 512:(i + 1) * 512] for i in range(4)]
        fence_sync()
        t_mload = S.load(masks_t, mask_in[:, :], "msk")
        t_ones = None
        for vb in range(2):
            for c in range(TC):
                t_ones = S.v(lambda e, b=vb, cc=c: e.memset(
                    vrhs[b][:, cc * 129 + 128: cc * 129 + 129], 1.0), inc=True)
        S.wait_total("sync", spill_done)
        S.wait("sync", "dv", t_ones)

        head_use_pe = [None, None]
        ptslot_use = [None, None, None]
        yst_use = [None, None]
        ptr_use = None
        av_dv = [None] * 4
        first_mask_wait = True
        yF_done = None
        for h in range(H):
            hb = h % 2
            if head_use_pe[hb]:
                S.wait("sync", "pe", head_use_pe[hb])
            S.load(qrot[hb], q_sc[h * 128:(h + 1) * 128, :], f"hd{hb}")
            S.load(krot[hb], k_sc[h * 128:(h + 1) * 128, :], f"hd{hb}")
            for c in range(TC):
                t_v = S.load(vrhs[hb][:, c * 129:c * 129 + 128],
                             v_sc[h * T + c * 128: h * T + (c + 1) * 128, :], f"hd{hb}")
            S.waitp("tensor", t_v)
            exp_done = {}
            avlast = {}
            SBANKS = (0, 1, 7)
            for j in range(QT):
                nact = min(TC, 4 * (j + 1))
                for c0 in range(0, nact, 3):
                    batch = list(range(c0, min(c0 + 3, nact)))
                    for c in batch:
                        sbk = c % 3
                        pS = ps[SBANKS[sbk]]
                        if ptslot_use[sbk]:
                            S.wait("tensor", "ac", exp_done.get(c - 3, 0) or 0)
                        t_sc2 = S.pe(lambda e, cc=c, jj=j, p=pS, b=hb: e.matmul(
                            out=p[:, 0:512], lhsT=krot[b][:, cc * 128:(cc + 1) * 128],
                            rhs=qrot[b][:, jj * 512:(jj + 1) * 512],
                            start=True, stop=True, skip_group_check=True), inc=True)
                        diag = c >= 4 * j
                        if diag:
                            if first_mask_wait:
                                S.waitp("vector", t_mload)
                                first_mask_wait = False
                            S.wait("vector", "pe", t_sc2)
                            t_mk = S.v(lambda e, p=pS, r=c - 4 * j: e.tensor_tensor(
                                out=p[:, 0:512], in0=p[:, 0:512], in1=masks[r],
                                op=ALU.add), inc=True)
                            S.wait("scalar", "dv", t_mk)
                        else:
                            S.wait("scalar", "pe", t_sc2)
                        if ptslot_use[sbk]:
                            S.wait("scalar", "pe", ptslot_use[sbk])
                        t_exp = S.s(lambda e, p=pS, d=pT[sbk]: e.activation(
                            out=d, in_=p[:, 0:512], func=AF.Exp, bias=c_expb), inc=True)
                        exp_done[c] = t_exp
                    S.wait("tensor", "ac", exp_done[batch[-1]])
                    for c in batch:
                        sbk = c % 3
                        last = None
                        for qs in range(4):
                            qq = 4 * j + qs
                            if c > qq:
                                continue
                            if c == 0 and av_dv[qs] is not None:
                                S.wait("tensor", "dv", av_dv[qs])
                            last = S.pe(lambda e, cc=c, q=qs, b=hb, qqv=qq, sb2=sbk: e.matmul(
                                out=ps[2 + q][:, 0:129], lhsT=pT[sb2][:, q * 128:(q + 1) * 128],
                                rhs=vrhs[b][:, cc * 129: cc * 129 + 129],
                                start=(cc == 0), stop=(cc == qqv),
                                skip_group_check=True), inc=True)
                        avlast[c] = last
                        ptslot_use[sbk] = last
                for qs in range(4):
                    ssl = qs % 2
                    S.wait("vector", "pe", avlast[4 * j + qs])
                    if yst_use[ssl]:
                        S.wait("vector", "pe", yst_use[ssl])
                    S.v(lambda e, q=qs: e.reciprocal(out=s_zr, in_=ps[2 + q][:, 128:129]))
                    av_dv[qs] = S.v(lambda e, q=qs, d=yst[ssl]: e.tensor_scalar(
                        out=d, in0=ps[2 + q][:, 0:128], scalar1=s_zr, scalar2=None,
                        op0=ALU.mult), inc=True)
                    S.wait("tensor", "dv", av_dv[qs])
                    if ptr_use:
                        S.wait("tensor", "ac", ptr_use)
                    S.waitp("tensor", S.ps6)
                    yst_use[ssl] = S.pe(lambda e, d=yst[ssl]: e.transpose(
                        out=ps[6][:, 0:128], in_=d, identity=idf), inc=True)
                    S.wait("scalar", "pe", yst_use[ssl])
                    ptr_use = S.s(lambda e, hh2=h, qq2=4 * j + qs: e.activation(
                        out=hF[:, hh2 * T + qq2 * 128: hh2 * T + qq2 * 128 + 128],
                        in_=ps[6][:, 0:128], func=AF.Copy), inc=True)
                    S.ps6 = ("ac", ptr_use)
            head_use_pe[hb] = avlast[nact - 1]
        yF_done = ptr_use

        p3 = Bump(O_AFTER_HF)
        bcA = p3.f(E); bcB = p3.f(E)
        wt2 = [p3.f(EC * N2), p3.f(EC * N2)]
        xinst = [p3.f(N2), p3.f(N2)]
        x2st = [p3.f(N2), p3.f(N2)]
        stst = [p3.f(1024), p3.f(1024)]
        stdmp = p3.f(1024)
        S.wait("tensor", "ac", yF_done)
        fence_sync()
        t_bo = bcast(bcA, lambda o, n: bo_row[0:1, o:o + n], E, None)
        t_ga = bcast(bcB, modrow(2), E, "signclamp", st_guard=mod_st_done)

        stats_jobs = []
        nch_m = [0, 0, 0]
        for mi, (wdram, rows, cols) in enumerate(((wg_r, JS * 128, EC * 512),
                                                  (wu_r, JS * 128, EC * 512),
                                                  (wd_t, I, E))):
            ccw = min(1024, cols)
            cpr = cols // ccw
            nch_m[mi] = (rows // 128) * cpr
            for ch in range(nch_m[mi]):
                stats_jobs.append((mi, wdram, (ch // cpr) * 128, (ch % cpr) * ccw, ch, ccw))
        st_slot_use = [None, None]
        nstats = len(stats_jobs)

        def emit_stats(idx):
            mi, wdram, r0, c0, ch, ccw = stats_jobs[idx]
            sl = idx % 2
            if st_slot_use[sl]:
                S.wait("sync", "ac", st_slot_use[sl])
            t_l = S.load(stst[sl][:, 0:ccw], wdram[r0:r0 + 128, c0:c0 + ccw], f"ss{sl}")
            S.waitp("scalar", t_l)
            st_slot_use[sl] = S.s(lambda e, s=stst[sl][:, 0:ccw], m=mi, p=ch, w=ccw: e.activation(
                out=stdmp[:, 0:w], in_=s, func=AF.Abs, accum_out=parts[m][:, p:p + 1]), inc=True)
            return st_slot_use[sl]

        wt2_use = [None, None]
        x2_dv = [None, None]
        x2st_st = [None, None]
        si = 0
        njobs_per = max(1, (nstats + FS2 * TC - 1) // (FS2 * TC))
        last_stats = None
        for fs in range(FS2):
            sl = fs % 2
            if wt2_use[sl]:
                S.wait("sync", "pe", wt2_use[sl])
            t_w = S.load(wt2[sl], wo_r[fs * 128:(fs + 1) * 128, :], f"wo{sl}")
            S.waitp("tensor", t_w)
            for tt in range(TC):
                bk = tt % 2
                pA = ps[0] if bk == 0 else ps[1]
                if x2_dv[bk]:
                    S.wait("tensor", "dv", x2_dv[bk])
                for ec in range(EC):
                    t_mm = S.pe(lambda e, c=ec, t=tt, p=pA, w=wt2[sl]: e.matmul(
                        out=p[:, 0:N2], lhsT=hF[:, c * T + t * 128: c * T + t * 128 + 128],
                        rhs=w[:, c * N2:(c + 1) * N2], start=(c == 0), stop=(c == EC - 1),
                        skip_group_check=True), inc=(ec == EC - 1))
                wt2_use[sl] = t_mm
                ssl = tt % 2
                if x2_dv[bk]:
                    S.wait("sync", "dv", x2_dv[bk])
                t_xl = S.load(xinst[ssl], x_in[tt * 128:(tt + 1) * 128, fs * N2:(fs + 1) * N2], f"xi{ssl}")
                if x2st_st[ssl]:
                    S.waitp("vector", x2st_st[ssl])
                S.wait("vector", "pe", t_mm)
                S.waitp("vector", t_xl)
                if fs == 0 and tt == 0:
                    S.wait("vector", "ac", t_bo)
                    S.wait("vector", "ac", t_ga)
                S.v(lambda e, p=pA, d=x2st[ssl], f=fs: e.tensor_tensor(
                    out=d, in0=p[:, 0:N2], in1=bcA[:, f * N2:(f + 1) * N2], op=ALU.add))
                S.v(lambda e, d=x2st[ssl], f=fs: e.tensor_tensor(
                    out=d, in0=d, in1=bcB[:, f * N2:(f + 1) * N2], op=ALU.mult))
                x2_dv[bk] = S.v(lambda e, d=x2st[ssl], x=xinst[ssl]: e.tensor_tensor(
                    out=d, in0=d, in1=x, op=ALU.add), inc=True)
                S.wait("gpsimd", "dv", x2_dv[bk])
                x2st_st[ssl] = S.store(
                    x2_sc[tt * 128:(tt + 1) * 128, fs * N2:(fs + 1) * N2], x2st[ssl], f"xs{ssl}")
                for _ in range(njobs_per):
                    if si < nstats:
                        last_stats = emit_stats(si); si += 1
        while si < nstats:
            last_stats = emit_stats(si); si += 1
        x2_done = ["xs0", "xs1"]

        S.wait("vector", "ac", last_stats)
        t_rc = None
        for mi, (s_sw, s_hf) in enumerate(((s_swg, s_hfg), (s_swu, s_hfu), (s_swd, s_hfd))):
            t_cs = S.v(lambda e, m=mi: e.reduce_sum(
                out=s_zr, in_=parts[m][:, 0:nch_m[m]], axis=AX.X), inc=True)
            S.wait("tensor", "dv", t_cs)
            S.waitp("tensor", S.ps6)
            t_mm = S.pe(lambda e: e.matmul(out=ps[6][:, 0:1], lhsT=ones_blk,
                                           rhs=s_zr, start=True, stop=True), inc=True)
            S.wait("vector", "pe", t_mm)
            S.v(lambda e, d=s_sw: e.tensor_scalar(
                out=d, in0=ps[6][:, 0:1], scalar1=1.0 / (I * E),
                scalar2=None, op0=ALU.mult))
            t_mn = S.v(lambda e, d=s_sw: e.tensor_scalar(
                out=d, in0=d, scalar1=1e-5, scalar2=None, op0=ALU.max), inc=True)
            S.ps6 = ("dv", t_mn)
            S.v(lambda e, d=s_hf, s0=s_sw: e.tensor_scalar(
                out=d, in0=s0, scalar1=0.5, scalar2=None, op0=ALU.mult))
            t_rc = S.v(lambda e, d=s_sw: e.reciprocal(out=d, in_=d), inc=True)
        stats_done = t_rc

        p3b = Bump(PH)
        qxF = p3b.b(EC * T)
        O_AFTER_QXF = p3b.off
        x2t = [p3b.f(E), p3b.f(E)]; h2t = [p3b.f(E), p3b.f(E)]
        LW3 = min(1024, E // 2)
        lad_y = p3b.f(LW3); lad_r = p3b.f(LW3); lad_b = p3b.f(LW3)
        lad_y2 = p3b.f(LW3); lad_r2 = p3b.f(LW3); lad_b2 = p3b.f(LW3)
        sqdump = p3b.f(E)
        bcC = p3b.f(E); bcD = p3b.f(E)
        wdst = [p3b.f(E), p3b.f(E)]
        ternb = [p3b.b(E), p3b.b(E)]
        scrd = p3b.b(E)
        fence_sync()
        t_cb = bcast(bcC, modrow(4), E, "add1", st_guard=mod_st_done)
        t_sb = bcast(bcD, modrow(3), E, "none")
        S.wait_total("sync", x2_done)

        wd_use = [None, None]
        tb_use = [None, None]
        tern_st = [None, None]

        def emit_tern(ic):
            sl = ic % 2
            if wd_use[sl]:
                S.wait("sync", "ac", wd_use[sl])
            t_l = S.load(wdst[sl], wd_t[ic * 128:(ic + 1) * 128, :], f"wd{sl}")
            S.waitp("scalar", t_l)
            if ic == 0:
                S.wait("scalar", "dv", stats_done)
            if tern_st[sl]:
                S.waitp("scalar", tern_st[sl])
            for g in tb_use:
                if g:
                    S.wait("scalar", "dv", g)
            S.s(lambda e, s=wdst[sl], d=ternb[sl]: e.activation(
                out=d, in_=s, func=AF.Sign, scale=s_swd, bias=c_mhalf))
            wd_use[sl] = S.s(lambda e, s=wdst[sl]: e.activation(
                out=scrd, in_=s, func=AF.Sign, scale=s_swd, bias=c_phalf), inc=True)
            S.wait("vector", "ac", wd_use[sl])
            tb_use[sl] = S.v(lambda e, d=ternb[sl]: e.tensor_tensor(
                out=d, in0=d, in1=scrd, op=ALU.add), inc=True)
            S.wait("gpsimd", "dv", tb_use[sl])
            tern_st[sl] = S.store(tern_sc[ic * 128:(ic + 1) * 128, :], ternb[sl], f"ts{sl}")

        qx_tr_use = [None, None]
        last_qx_evac = None
        tr_last = [None, None]
        x2t_use = [None, None]
        sq_done = [None, None]
        ld_done = [None, None]
        ic_done = 0

        def p3b_stage1(tt):
            p = tt % 2
            x2s = x2t[p]
            if x2t_use[p]:
                S.wait("sync", "dv", x2t_use[p])
            t_x2 = S.load(x2s, x2_sc[tt * 128:(tt + 1) * 128, :], f"x2t{p}")
            ld_done[p] = t_x2
            S.waitp("scalar", t_x2)
            sq_done[p] = S.s(lambda e, x=x2s, p2=p: e.activation(
                out=sqdump, in_=x, func=AF.Square, accum_out=sp_ss[p2]), inc=True)

        def p3b_stage2(tt):
            p = tt % 2
            x2s = x2t[p]
            h2s = h2t[p]
            S.waitp("vector", ld_done[p])
            S.v(lambda e, x=x2s, p2=p: e.reduce_sum(out=sp_s1[p2], in_=x, axis=AX.X))
            S.v(lambda e, p2=p: e.tensor_scalar(out=sp_m[p2], in0=sp_s1[p2],
                scalar1=1.0 / E, scalar2=None, op0=ALU.mult))
            S.v(lambda e, p2=p: e.tensor_tensor(out=sp_mm2[p2], in0=sp_m[p2],
                in1=sp_m[p2], op=ALU.mult))
            S.wait("vector", "ac", sq_done[p])
            t_var = S.v(lambda e, p2=p: e.scalar_tensor_tensor(
                out=sp_var[p2], in0=sp_ss[p2], scalar=1.0 / E, in1=sp_mm2[p2],
                op0=ALU.mult, op1=ALU.subtract), inc=True)
            S.wait("scalar", "dv", t_var)
            t_rstd = S.s(lambda e, p2=p: e.activation(
                out=sp_rstd[p2], in_=sp_var[p2], func=AF.Sqrt, bias=c_eps5), inc=True)
            S.wait("vector", "ac", t_rstd)
            S.v(lambda e, p2=p: e.reciprocal(out=sp_rstd[p2], in_=sp_rstd[p2]))
            if tt == 0:
                S.wait("vector", "ac", t_cb)
                S.wait("vector", "ac", t_sb)
            if tr_last[p]:
                S.wait("vector", "pe", tr_last[p])
            x2t_use[p] = S.v(lambda e, x=x2s, d=h2s, p2=p: e.tensor_scalar(
                out=d, in0=x, scalar1=sp_m[p2], scalar2=sp_rstd[p2],
                op0=ALU.subtract, op1=ALU.mult), inc=True)
            S.v(lambda e, d=h2s: e.tensor_tensor(out=d, in0=d, in1=bcC, op=ALU.mult))
            S.v(lambda e, d=h2s: e.tensor_tensor(out=d, in0=d, in1=bcD, op=ALU.add))
            S.v(lambda e, d=h2s, p2=p: e.reduce_max(out=sp_amax[p2], in_=d, axis=AX.X,
                                                    apply_absolute_value=True))
            S.v(lambda e, p2=p: e.tensor_scalar(
                out=sp_amax[p2], in0=sp_amax[p2], scalar1=1e-5, scalar2=None, op0=ALU.max))
            S.v(lambda e, t=tt, p2=p: e.tensor_scalar(
                out=invsx[:, t:t + 1], in0=sp_amax[p2], scalar1=1.0 / 127.0,
                scalar2=None, op0=ALU.mult))
            S.v(lambda e, t=tt, p2=p: e.reciprocal(out=sp_sx[p2], in_=invsx[:, t:t + 1]))

        def p3b_ladder(tt):
            nonlocal last_qx_evac, ic_done
            p = tt % 2
            h2s = h2t[p]
            t_lad = ladder_sd(h2s, sp_sx[p], h2s, E,
                              [lad_y, lad_y2], [lad_r, lad_r2], [lad_b, lad_b2], LW3)
            S.wait("tensor", "dv", t_lad)
            for e0 in range(0, EC, 4):
                npk = min(4, EC - e0)
                bsl = (e0 // 4) % 2
                if qx_tr_use[bsl]:
                    S.wait("tensor", "ac", qx_tr_use[bsl])
                t_tr = None
                for k in range(npk):
                    t_tr = S.pe(lambda e, c=e0 + k, b=bsl, d=h2s, kk=k: e.transpose(
                        out=ps[1 + b][:, kk * 128:(kk + 1) * 128],
                        in_=d[:, c * 128:(c + 1) * 128],
                        identity=idf), inc=True)
                tr_last[p] = t_tr
                S.wait("scalar", "pe", t_tr)
                qx_tr_use[bsl] = S.s(lambda e, c=e0, b=bsl, t=tt, w=npk * 128: e.activation(
                    out=qxF[:, t * E + c * 128: t * E + c * 128 + w],
                    in_=ps[1 + b][:, 0:w], func=AF.Copy), inc=True)
            last_qx_evac = qx_tr_use[((EC - 1) // 4) % 2]
            ntern = (IC * (tt + 1)) // TC
            while ic_done < ntern:
                emit_tern(ic_done); ic_done += 1

        p3b_stage1(0)
        for tt in range(TC):
            if tt + 1 < TC:
                p3b_stage1(tt + 1)
            p3b_stage2(tt)
            p3b_ladder(tt)
        while ic_done < IC:
            emit_tern(ic_done); ic_done += 1
        tern_done = ["ts0", "ts1"]

        t_scl = None
        for tt in range(TC):
            S.v(lambda e, t=tt: e.tensor_tensor(
                out=sclg[:, t:t + 1], in0=invsx[:, t:t + 1], in1=s_hfg, op=ALU.mult))
            t_scl = S.v(lambda e, t=tt: e.tensor_tensor(
                out=sclu[:, t:t + 1], in0=invsx[:, t:t + 1], in1=s_hfu, op=ALU.mult), inc=True)

        p4 = Bump(O_AFTER_QXF)
        gst = p4.f(EC * 512)
        ternG = [p4.b(EC * 512), p4.b(EC * 512)]
        ternU = [p4.b(EC * 512), p4.b(EC * 512)]
        scrg = p4.b(EC * 512)
        gs_t = [p4.f(512), p4.f(512)]
        hst = [p4.f(512), p4.f(512)]

        fence_sync()
        tG_use = [None, None]; tU_use = [None, None]
        tU_rdy_prev = None
        gst_free = None
        hst_use = [None, None]
        psG_use = [None, None]; psU_use = [None, None]
        for js in range(JS):
            sl = js % 2
            if gst_free:
                S.wait("sync", "ac", gst_free)
            if tG_use[sl]:
                S.wait("sync", "pe", tG_use[sl])
            t_g = S.load(gst, wg_r[js * 128:(js + 1) * 128, :], "gst")
            S.waitp("scalar", t_g)
            if js == 0:
                S.wait("scalar", "dv", stats_done)
            if tU_rdy_prev:
                S.wait("scalar", "dv", tU_rdy_prev)
            S.s(lambda e, d=ternG[sl]: e.activation(
                out=d, in_=gst, func=AF.Sign, scale=s_swg, bias=c_mhalf))
            t_sg = S.s(lambda e: e.activation(
                out=scrg, in_=gst, func=AF.Sign, scale=s_swg, bias=c_phalf), inc=True)
            gst_free = t_sg
            S.wait("vector", "ac", t_sg)
            tG_rdy = S.v(lambda e, d=ternG[sl]: e.tensor_tensor(
                out=d, in0=d, in1=scrg, op=ALU.add), inc=True)
            if tU_use[sl]:
                S.wait("sync", "pe", tU_use[sl])
            S.wait("sync", "ac", t_sg)
            t_u = S.load(gst, wu_r[js * 128:(js + 1) * 128, :], "gst")
            S.waitp("scalar", t_u)
            S.wait("scalar", "dv", tG_rdy)
            S.s(lambda e, d=ternU[sl]: e.activation(
                out=d, in_=gst, func=AF.Sign, scale=s_swu, bias=c_mhalf))
            t_su = S.s(lambda e: e.activation(
                out=scrg, in_=gst, func=AF.Sign, scale=s_swu, bias=c_phalf), inc=True)
            gst_free = t_su
            S.wait("vector", "ac", t_su)
            tU_rdy = S.v(lambda e, d=ternU[sl]: e.tensor_tensor(
                out=d, in0=d, in1=scrg, op=ALU.add), inc=True)
            tU_rdy_prev = tU_rdy
            if js == 0:
                S.wait("tensor", "ac", last_qx_evac)
            S.wait("tensor", "dv", tU_rdy)
            for tt in range(TC):
                bg = tt % 2
                pG = ps[0] if bg == 0 else ps[1]
                pU = ps[2] if bg == 0 else ps[3]
                S.waitp("tensor", psG_use[bg])
                for ec in range(EC):
                    t_gmm = S.pe(lambda e, c=ec, t=tt, p=pG, d=ternG[sl]: e.matmul(
                        out=p[:, 0:512], lhsT=qxF[:, t * E + c * 128: t * E + c * 128 + 128],
                        rhs=d[:, c * 512:(c + 1) * 512], start=(c == 0), stop=(c == EC - 1),
                        skip_group_check=True), inc=(ec == EC - 1))
                S.waitp("tensor", psU_use[bg])
                for ec in range(EC):
                    t_umm = S.pe(lambda e, c=ec, t=tt, p=pU, d=ternU[sl]: e.matmul(
                        out=p[:, 0:512], lhsT=qxF[:, t * E + c * 128: t * E + c * 128 + 128],
                        rhs=d[:, c * 512:(c + 1) * 512], start=(c == 0), stop=(c == EC - 1),
                        skip_group_check=True), inc=(ec == EC - 1))
                tG_use[sl] = t_gmm; tU_use[sl] = t_umm
                ssl = tt % 2
                S.wait("scalar", "pe", t_gmm)
                if js == 0 and tt == 0:
                    S.wait("scalar", "dv", t_scl)
                if psU_use[bg]:
                    S.wait("scalar", "dv", psU_use[bg][1])
                t_si = S.s(lambda e, p=pG, d=gs_t[ssl], t=tt: e.activation(
                    out=d, in_=p[:, 0:512], func=AF.Silu, scale=sclg[:, t:t + 1]), inc=True)
                psG_use[bg] = ("ac", t_si)
                if hst_use[ssl]:
                    S.waitp("vector", hst_use[ssl])
                S.wait("vector", "pe", t_umm)
                S.wait("vector", "ac", t_si)
                t_h = S.v(lambda e, p=pU, d=hst[ssl], g=gs_t[ssl], t=tt: e.scalar_tensor_tensor(
                    out=d, in0=p[:, 0:512], scalar=sclu[:, t:t + 1], in1=g,
                    op0=ALU.mult, op1=ALU.mult), inc=True)
                psU_use[bg] = ("dv", t_h)
                S.wait("gpsimd", "dv", t_h)
                hst_use[ssl] = S.store(
                    h_sc[tt * 128:(tt + 1) * 128, js * 512:(js + 1) * 512], hst[ssl], f"hs{ssl}")
        h_done = ["hs0", "hs1"]

        p5 = Bump(PH)
        lnwB = p5.f(I)
        h2t2 = [p5.f(I), p5.f(I)]
        qh2F = p5.b(IC * 256)
        ternD = [p5.b(E), p5.b(E)]
        x2l = p5.f(E)
        xout = p5.f(E)
        gbB = p5.f(E)
        LW5 = min(1024, I // 2)
        dl_y = [p5.f(LW5) for _ in range(3)]
        dl_r = [p5.f(LW5) for _ in range(3)]
        dl_b = [p5.f(LW5) for _ in range(3)]
        dl_dump = p5.f(512)
        sqparts = p5.f(2 * (I // 512))

        fence_sync()
        t_lnw = bcast(lnwB, lambda o, n: lnw_row[0:1, o:o + n], I, None)
        t_gb = bcast(gbB, modrow(5), E, "signclamp", st_guard=mod_st_done)
        S.wait_total("sync", h_done)
        S.wait_total("sync", tern_done)

        h2buf_use = [None, None]
        ternD_use = [None, None]
        tr2_use = [None, None]
        qh2_evac = None
        prev_epi_dv = None
        xout_st = [None, None]

        ld5_done = [None, None]
        sq5_done = [None, None]
        h2_guard_p = [None, None]

        def p5_stage1(b6, sub):
            tt = 2 * b6 + sub
            hb2 = h2t2[sub]
            if h2buf_use[sub]:
                S.wait("sync", "pe", h2buf_use[sub])
            if h2_guard_p[sub]:
                S.wait("sync", "dv", h2_guard_p[sub])
            t_h = S.load(hb2, h_sc[tt * 128:(tt + 1) * 128, :], f"h2l{sub}")
            ld5_done[sub] = t_h
            S.waitp("scalar", t_h)
            if b6 == 0 and sub == 0:
                S.wait("scalar", "ac", t_lnw)
            t_sq = None
            nstrip = I // 512
            for si in range(nstrip):
                t_sq = S.s(lambda e, o=si * 512, p=si, s2=sub, h=hb2: e.activation(
                    out=dl_dump, in_=h[:, o:o + 512], func=AF.Square,
                    accum_out=sqparts[:, s2 * nstrip + p: s2 * nstrip + p + 1]), inc=True)
            sq5_done[sub] = t_sq

        def p5_stage2lad(b6, sub):
            tt = 2 * b6 + sub
            hb2 = h2t2[sub]
            nstrip = I // 512
            S.waitp("vector", ld5_done[sub])
            S.wait("vector", "ac", sq5_done[sub])
            S.v(lambda e, s2=sub: e.reduce_sum(
                out=sp_ms[s2], in_=sqparts[:, s2 * nstrip:(s2 + 1) * nstrip], axis=AX.X))
            S.v(lambda e, h=hb2: e.tensor_tensor(out=h, in0=h, in1=lnwB, op=ALU.mult))
            S.v(lambda e, h=hb2, s2=sub: e.reduce_max(
                out=sp_rm[s2], in_=h, axis=AX.X, apply_absolute_value=True))
            t_ms = S.v(lambda e, s2=sub: e.tensor_scalar(
                out=sp_ms[s2], in0=sp_ms[s2], scalar1=1.0 / I, scalar2=None,
                op0=ALU.mult), inc=True)
            S.wait("scalar", "dv", t_ms)
            t_r2 = S.s(lambda e, s2=sub: e.activation(
                out=sp_rstd2[s2], in_=sp_ms[s2], func=AF.Sqrt, bias=c_eps6), inc=True)
            S.wait("vector", "ac", t_r2)
            S.v(lambda e, s2=sub: e.reciprocal(out=sp_rstd2[s2], in_=sp_rstd2[s2]))
            S.v(lambda e, s2=sub: e.tensor_tensor(
                out=sp_rm[s2], in0=sp_rm[s2], in1=sp_rstd2[s2], op=ALU.mult))
            S.v(lambda e, s2=sub: e.tensor_scalar(
                out=sp_rm[s2], in0=sp_rm[s2], scalar1=1e-5, scalar2=None, op0=ALU.max))
            S.v(lambda e, s2=sub, p=b6 % 2: e.tensor_scalar(
                out=s_invs2[p][s2], in0=sp_rm[s2], scalar1=1.0 / 127.0,
                scalar2=None, op0=ALU.mult))
            S.v(lambda e, s2=sub, p=b6 % 2: e.reciprocal(out=sp_sx2[s2], in_=s_invs2[p][s2]))
            S.v(lambda e, s2=sub: e.tensor_tensor(
                out=sp_sx2[s2], in0=sp_sx2[s2], in1=sp_rstd2[s2], op=ALU.mult))
            t_lad = ladder_sd(hb2, sp_sx2[sub], hb2, I, dl_y, dl_r, dl_b, LW5)
            h2_guard_p[sub] = t_lad
            return t_lad

        def emit_transposes(b6, t_lad):
            nonlocal qh2_evac, h2buf_use
            S.wait("tensor", "dv", t_lad)
            if prev_epi_dv:
                S.wait("tensor", "dv", prev_epi_dv)
            for sub in range(2):
                last_tr = None
                for i0 in range(0, IC, 4):
                    npk = min(4, IC - i0)
                    bsl = (i0 // 4) % 2
                    if tr2_use[bsl]:
                        S.wait("tensor", "ac", tr2_use[bsl])
                    t_tr = None
                    for k in range(npk):
                        t_tr = S.pe(lambda e, c=i0 + k, b=bsl, s2=sub, kk=k: e.transpose(
                            out=ps[1 + b][:, kk * 128:(kk + 1) * 128],
                            in_=h2t2[s2][:, c * 128:c * 128 + 128],
                            identity=idf), inc=True)
                    S.wait("scalar", "pe", t_tr)
                    tr2_use[bsl] = S.s(lambda e, c=i0, b=bsl, s2=sub, w=npk * 128: e.activation(
                        out=qh2F[:, s2 * IC * 128 + c * 128: s2 * IC * 128 + c * 128 + w],
                        in_=ps[1 + b][:, 0:w], func=AF.Copy), inc=True)
                    last_tr = t_tr
                h2buf_use[sub] = last_tr
            qh2_evac = tr2_use[(((IC - 1) // 4) % 2)]

        def emit_down(b6):
            nonlocal prev_epi_dv, ternD_use
            S.wait("tensor", "ac", qh2_evac)
            if prev_epi_dv:
                S.wait("tensor", "dv", prev_epi_dv)
            mm_fin = None
            for ic in range(IC):
                sl = ic % 2
                if ternD_use[sl]:
                    S.wait("sync", "pe", ternD_use[sl])
                t_td = S.load(ternD[sl], tern_sc[ic * 128:(ic + 1) * 128, :], f"td{sl}")
                S.waitp("tensor", t_td)
                lastmm = None
                for tsub in range(2):
                    for ft in range(NFT):
                        bank = tsub * NFT + ft
                        lastmm = S.pe(lambda e, c=ic, ts=tsub, f=ft, b=bank, d=ternD[sl]:
                                      e.matmul(out=ps[b][:, 0:NF],
                                               lhsT=qh2F[:, ts * IC * 128 + c * 128: ts * IC * 128 + c * 128 + 128],
                                               rhs=d[:, f * NF:(f + 1) * NF],
                                               start=(c == 0), stop=(c == IC - 1),
                                               skip_group_check=True),
                                      inc=(tsub == 1 and ft == NFT - 1))
                ternD_use[sl] = lastmm
                mm_fin = lastmm
            S.wait("vector", "pe", mm_fin)
            if b6 == 0:
                S.wait("vector", "ac", t_gb)
            for sub in range(2):
                tt = 2 * b6 + sub
                if prev_epi_dv:
                    S.wait("sync", "dv", prev_epi_dv)
                t_x2 = S.load(x2l, x2_sc[tt * 128:(tt + 1) * 128, :], "x2l")
                S.waitp("vector", t_x2)
                if xout_st[0]:
                    S.waitp("vector", xout_st[0])
                S.v(lambda e, s2=sub, p=b6 % 2: e.tensor_tensor(
                    out=s_scld[s2], in0=s_invs2[p][s2], in1=s_hfd, op=ALU.mult))
                t_epi = None
                for ft in range(NFT):
                    bank = sub * NFT + ft
                    d_ap = xout[:, ft * NF:(ft + 1) * NF]
                    S.v(lambda e, d=d_ap, b=bank, s2=sub: e.tensor_scalar(
                        out=d, in0=ps[b][:, 0:NF], scalar1=s_scld[s2], scalar2=None,
                        op0=ALU.mult))
                    S.v(lambda e, d=d_ap, f=ft: e.tensor_tensor(
                        out=d, in0=d, in1=gbB[:, f * NF:(f + 1) * NF], op=ALU.mult))
                    t_epi = S.v(lambda e, d=d_ap, f=ft: e.tensor_tensor(
                        out=d, in0=d, in1=x2l[:, f * NF:(f + 1) * NF], op=ALU.add), inc=True)
                S.wait("gpsimd", "dv", t_epi)
                xout_st[0] = S.store(y_out[tt * 128:(tt + 1) * 128, :], xout, "yo0")
                prev_epi_dv = t_epi

        pending = None
        for b6 in range(NB6):
            p5_stage1(b6, 0)
            p5_stage2lad(b6, 0)
            p5_stage1(b6, 1)
            t_lad1 = p5_stage2lad(b6, 1)
            if pending is not None:
                emit_down(pending)
            emit_transposes(b6, t_lad1)
            pending = b6
        emit_down(pending)

        S.wait_total("sync", ["yo0", "yo1"])

        for nm in ("pe", "dv", "ac"):
            S.semh[nm] = es.enter_context(nc.semaphore(nm))
        for nm in sorted(S.sems):
            S.semh[nm] = es.enter_context(nc.semaphore(nm))

        block = es.enter_context(nc.Block())

        @block.sync
        def _(sync):
            for fn in S.ops["sync"]:
                fn(sync)

        @block.gpsimd
        def _(gpsimd):
            for fn in S.ops["gpsimd"]:
                fn(gpsimd)

        @block.tensor
        def _(tensor):
            for fn in S.ops["tensor"]:
                fn(tensor)

        @block.vector
        def _(vector):
            for fn in S.ops["vector"]:
                fn(vector)

        @block.scalar
        def _(scalar):
            for fn in S.ops["scalar"]:
                fn(scalar)

    return nc


def make_tables(T):
    D = 128
    j = np.arange(D // 2, dtype=np.float64)
    thetas = 10000.0 ** (-2.0 * j / D)
    m = np.arange(1, T + 1, dtype=np.float64)
    ang = np.concatenate([thetas, thetas])[:, None] * m[None, :]
    c = 128.0 ** (-0.25)
    cosv = np.cos(ang) * c
    sinv = np.sin(ang) * c
    sgn = np.ones((D, 1)); sgn[:D // 2] = -1.0
    sinv = sinv * sgn
    chi = cosv.astype(np.float32)
    clo = (cosv - chi.astype(np.float64)).astype(np.float32)
    shi = sinv.astype(np.float32)
    slo = (sinv - shi.astype(np.float64)).astype(np.float32)
    return chi, clo, shi, slo


def make_masks():
    i = np.arange(128)[:, None]
    jq = np.arange(512)[None, :]
    return np.concatenate([
        np.where(128 * rr + i <= jq, 0.0, NEGBIG).astype(np.float32)
        for rr in range(4)], axis=1)


def host_inputs(T, E, I, Wq, bq, Wk, bk, Wv, bv, Wo, bo, wg, wu, wd, lnw, wmod):
    EC = E // 128
    FT = E // 128
    NV = min(512, E); FSV = E // NV
    N2 = min(256, E); FS2 = E // N2
    JS = I // 512
    asc = np.ascontiguousarray
    f = np.float32
    d = {}
    d["wq_t"] = asc(np.asarray(Wq, f).reshape(FT, 128, EC, 128).transpose(0, 3, 2, 1).reshape(E, E))
    d["wk_t"] = asc(np.asarray(Wk, f).reshape(FT, 128, EC, 128).transpose(0, 3, 2, 1).reshape(E, E))
    d["wv_r"] = asc(np.asarray(Wv, f).T.reshape(EC, 128, FSV, NV).transpose(2, 1, 0, 3).reshape(FSV * 128, EC * NV))
    d["wo_r"] = asc(np.asarray(Wo, f).T.reshape(EC, 128, FS2, N2).transpose(2, 1, 0, 3).reshape(FS2 * 128, EC * N2))
    d["bq_c"] = asc(np.asarray(bq, f).reshape(EC, 128).T)
    d["bk_c"] = asc(np.asarray(bk, f).reshape(EC, 128).T)
    d["bv_row"] = asc(np.asarray(bv, f).reshape(1, E))
    d["bo_row"] = asc(np.asarray(bo, f).reshape(1, E))
    d["wmod"] = asc(np.asarray(wmod, f))
    d["wg_r"] = asc(np.asarray(wg, f).T.reshape(EC, 128, JS, 512).transpose(2, 1, 0, 3).reshape(JS * 128, EC * 512))
    d["wu_r"] = asc(np.asarray(wu, f).T.reshape(EC, 128, JS, 512).transpose(2, 1, 0, 3).reshape(JS * 128, EC * 512))
    d["wd_t"] = asc(np.asarray(wd, f).T)
    d["lnw_row"] = asc(np.asarray(lnw, f).reshape(1, I))
    chi, clo, shi, slo = make_tables(T)
    d["chi_in"] = chi; d["clo_in"] = clo; d["shi_in"] = shi; d["slo_in"] = slo
    d["mask_in"] = make_masks()
    d["id_in"] = np.eye(128, dtype=np.float32)
    return d


def kernel(L0_attn_x, L0_attn_Wq, L0_attn_bq, L0_attn_Wk, L0_attn_bk,
           L0_attn_Wv, L0_attn_bv, L0_attn_Wo, L0_attn_bo,
           L0_ffn_w_gate, L0_ffn_w_up, L0_ffn_w_down, L0_ffn_ln_weight,
           L0_c, L0_wmod):
    T, E, I = 2048, 2048, 8192
    xs = np.asarray(L0_attn_x, np.float32)
    cs = np.asarray(L0_c, np.float32)
    B = xs.shape[0]
    EC = E // 128
    nc = build_program(T, E, I)
    shared = host_inputs(T, E, I, L0_attn_Wq, L0_attn_bq, L0_attn_Wk, L0_attn_bk,
                         L0_attn_Wv, L0_attn_bv, L0_attn_Wo, L0_attn_bo,
                         L0_ffn_w_gate, L0_ffn_w_up, L0_ffn_w_down,
                         L0_ffn_ln_weight, L0_wmod)
    in_maps = []
    for b in range(B):
        m = dict(shared)
        m["x_in"] = np.ascontiguousarray(xs[b])
        m["c_col"] = np.ascontiguousarray(cs[b].reshape(EC, 128).T)
        in_maps.append(m)
    res = run_bass_kernel_spmd(nc, in_maps, core_ids=list(range(B)))
    out = np.stack([np.asarray(res.results[b]["y_out"]) for b in range(B)], axis=0)
    return out.astype(np.float32)


# version 11
# speedup vs baseline: 1.1263x; 1.0252x over previous
import numpy as np
from contextlib import ExitStack
import concourse.bass as bass
import concourse.mybir as mybir
from concourse.bass_utils import run_bass_kernel_spmd

F32 = mybir.dt.float32
BF16 = mybir.dt.bfloat16
AF = mybir.ActivationFunctionType
ALU = mybir.AluOpType
AX = mybir.AxisListType

EXP_BIAS = -16.0
NEGBIG = -1.0e38
ARENA = 53152


class Sched:
    ENGS = ("sync", "gpsimd", "tensor", "vector", "scalar")

    def __init__(self):
        self.ops = {e: [] for e in self.ENGS}
        self.cnt = {"pe": 0, "dv": 0, "ac": 0}
        self.sems = {}
        self.semh = {}
        self.ps6 = None

    def _add(self, eng, fn):
        self.ops[eng].append(fn)

    def op(self, eng, build, sem=None, inc=0):
        tgt = None
        if sem is not None:
            self.cnt[sem] += inc
            tgt = self.cnt[sem]

        def fn(e, b=build, s=sem, i=inc, self_=self):
            inst = b(e)
            if s is not None:
                inst.then_inc(self_.semh[s], i)
        self._add(eng, fn)
        return tgt

    def wait(self, eng, sem, val):
        if not val:
            return
        def fn(e, s=sem, v=val, self_=self):
            e.wait_ge(self_.semh[s], v)
        self._add(eng, fn)

    def waitp(self, eng, pair):
        if pair:
            self.wait(eng, pair[0], pair[1])

    def wait_total(self, eng, names):
        for n in names:
            if self.sems.get(n):
                self.wait(eng, n, self.sems[n])

    def v(self, build, inc=False):
        self._add("vector", lambda e: e.drain())
        return self.op("vector", build, "dv" if inc else None, 1 if inc else 0)

    def s(self, build, inc=False):
        self._add("scalar", lambda e: e.drain())
        return self.op("scalar", build, "ac" if inc else None, 1 if inc else 0)

    def pe(self, build, inc=False):
        return self.op("tensor", build, "pe" if inc else None, 1 if inc else 0)

    def _dma(self, eng, out, in_, sem):
        self.sems[sem] = self.sems.get(sem, 0) + 16
        tgt = self.sems[sem]

        def fn(e, o=out, i=in_, s=sem, self_=self):
            e.dma_start(out=o, in_=i).then_inc(self_.semh[s], 16)
        self._add(eng, fn)
        return (sem, tgt)

    def load(self, out, in_, sem):
        return self._dma("sync", out, in_, sem)

    def store(self, out, in_, sem):
        return self._dma("gpsimd", out, in_, sem)


def build_program(T, E, I):
    H = E // 128
    EC = E // 128
    TC = T // 128
    QT = T // 512
    NV = min(512, E); FSV = E // NV
    N2 = min(256, E); FS2 = E // N2
    JS = I // 512
    IC = I // 128
    NB6 = T // 256
    NF = min(512, E); NFT = E // NF
    STRIP_E = min(1024, E)
    STRIP_I = min(1024, I)

    nc = bass.Bass("TRN2", target_bir_lowering=False)
    dt = lambda n, s, d=F32, k="ExternalInput": nc.dram_tensor(n, s, d, kind=k).ap()

    x_in = dt("x_in", [T, E])
    wq_t = dt("wq_t", [E, E]); wk_t = dt("wk_t", [E, E])
    wv_r = dt("wv_r", [FSV * 128, EC * NV])
    wo_r = dt("wo_r", [FS2 * 128, EC * N2])
    bq_c = dt("bq_c", [128, EC]); bk_c = dt("bk_c", [128, EC])
    bv_row = dt("bv_row", [1, E]); bo_row = dt("bo_row", [1, E])
    wmod = dt("wmod", [E, 6 * E])
    c_col = dt("c_col", [128, EC])
    wg_r = dt("wg_r", [JS * 128, EC * 512])
    wu_r = dt("wu_r", [JS * 128, EC * 512])
    wd_t = dt("wd_t", [I, E])
    lnw_row = dt("lnw_row", [1, I])
    chi_in = dt("chi_in", [128, T]); clo_in = dt("clo_in", [128, T])
    shi_in = dt("shi_in", [128, T]); slo_in = dt("slo_in", [128, T])
    mask_in = dt("mask_in", [128, 4 * 512])
    id_in = dt("id_in", [128, 128])

    q_sc = dt("q_sc", [H * 128, T], F32, "Internal")
    k_sc = dt("k_sc", [H * 128, T], F32, "Internal")
    v_sc = dt("v_sc", [H * T, 128], F32, "Internal")
    x2_sc = dt("x2_sc", [T, E], F32, "Internal")
    h_sc = dt("h_sc", [T, I], F32, "Internal")
    tern_sc = dt("tern_sc", [I, E], BF16, "Internal")
    mod_sc = dt("mod_sc", [1, 6 * E], F32, "Internal")
    y_out = dt("y_out", [T, E], F32, "ExternalOutput")

    S = Sched()
    es = ExitStack()
    with es:
        AR = es.enter_context(nc.sbuf_tensor("arena", [128, ARENA], F32))
        ARB = AR.bitcast(BF16)

        def f32(off, n):
            assert off + n <= ARENA, ("f32 overflow", off, n)
            return AR[:, off:off + n]

        def b16(off_f32, n_b16):
            assert 2 * off_f32 + n_b16 <= 2 * ARENA, ("b16 overflow", off_f32, n_b16)
            return ARB[:, 2 * off_f32:2 * off_f32 + n_b16]

        class Bump:
            def __init__(self, start):
                self.off = start
            def f(self, n):
                o = self.off; self.off += n
                assert self.off <= ARENA, ("bump overflow", self.off)
                return f32(o, n)
            def b(self, n_b16):
                n = (n_b16 + 1) // 2
                o = self.off; self.off += n
                assert self.off <= ARENA, ("bump overflow b16", self.off)
                return b16(o, n_b16)

        st_ = Bump(0)
        idf = st_.f(128)
        idb = st_.b(128)
        rowst_full = st_.f(512); rowst = rowst_full[0:1, :]
        rowst2 = st_.f(512)[0:1, :]
        rowst3 = st_.f(512)[0:1, :]
        ones_blk = st_.f(128)
        ones_col = ones_blk[:, 0:1]
        ones_row = ones_blk[0:1, :]
        scal = st_.f(64)
        sm = lambda i: scal[:, i:i + 1]
        (s_s1, s_ss, s_m, s_mm2, s_var, s_rstd, s_amax, s_sx, s_zr, s_swg,
         s_hfg, s_swu, s_hfu, s_swd, s_hfd, s_ms, s_rstd2, s_rm) = [sm(i) for i in range(18)]
        s_invs2 = [[sm(18), sm(19)], [sm(28), sm(29)]]
        s_scld = [sm(20), sm(21)]
        s_t11 = sm(22)
        c_eps5, c_eps6, c_mhalf, c_phalf, c_expb = sm(23), sm(24), sm(25), sm(26), sm(27)
        sp_s1 = [sm(30), sm(31)]; sp_ss = [sm(32), sm(33)]
        sp_m = [sm(34), sm(35)]; sp_mm2 = [sm(36), sm(37)]
        sp_var = [sm(38), sm(39)]; sp_rstd = [sm(40), sm(41)]
        sp_amax = [sm(42), sm(43)]; sp_sx = [sm(44), sm(45)]
        sp_ms = [sm(46), sm(47)]; sp_rstd2 = [sm(48), sm(49)]
        sp_rm = [sm(50), sm(51)]; sp_sx2 = [sm(52), sm(53)]
        c_m128 = sm(54)
        bq_sb = st_.f(EC); bk_sb = st_.f(EC)
        cc_sb = st_.f(EC); scc_sb = st_.f(EC)
        invsx = st_.f(TC); sclg = st_.f(TC); sclu = st_.f(TC)
        parts = [st_.f(128) for _ in range(3)]
        PH = st_.off

        ps = [es.enter_context(nc.psum_tensor(f"ps{i}", [128, 512], F32)) for i in range(8)]
        psb = ps[7].bitcast(BF16)


        def ladder_sd(src_ap, scale_ap, dst_ap, width, ytiles, rtiles, stiles, sw):
            strips = [(o, min(sw, width - o)) for o in range(0, width, sw)]
            NG = len(ytiles)
            t_last = None
            for g0 in range(0, len(strips), NG):
                grp = strips[g0:g0 + NG]
                ydv = []
                for i, (o, w) in enumerate(grp):
                    ydv.append(S.v(lambda e, d=ytiles[i][:, 0:w], s0=src_ap[:, o:o + w],
                                   sc=scale_ap: e.tensor_scalar(
                        out=d, in0=s0, scalar1=sc, scalar2=128.5,
                        op0=ALU.mult, op1=ALU.add), inc=True))
                rdv = list(ydv)
                rsrc = [ytiles[i] for i in range(len(grp))]
                for li, t2 in enumerate((128.0, 64.0, 32.0, 16.0, 8.0, 4.0, 2.0, 1.0)):
                    sac = []
                    for i, (o, w) in enumerate(grp):
                        S.wait("scalar", "dv", rdv[i])
                        sac.append(S.s(lambda e, d=stiles[i][:, 0:w],
                                       s0=rsrc[i][:, 0:w]: e.activation(
                            out=d, in_=s0, func=AF.Sign, bias=c_m128), inc=True))
                    for i, (o, w) in enumerate(grp):
                        S.wait("vector", "ac", sac[i])
                        rdv[i] = S.v(lambda e, d=rtiles[i][:, 0:w], s0=stiles[i][:, 0:w],
                                     s1=rsrc[i][:, 0:w], tv=t2 / 2: e.scalar_tensor_tensor(
                            out=d, in0=s0, scalar=-tv, in1=s1,
                            op0=ALU.mult, op1=ALU.add), inc=True)
                    rsrc = [rtiles[i] for i in range(len(grp))]
                for i, (o, w) in enumerate(grp):
                    t_last = S.v(lambda e, d=dst_ap[:, o:o + w], s0=ytiles[i][:, 0:w],
                                 s1=rtiles[i][:, 0:w]: e.scalar_tensor_tensor(
                        out=d, in0=s0, scalar=-0.5, in1=s1,
                        op0=ALU.add, op1=ALU.subtract), inc=True)
            return t_last

        def fence_sync():
            S.wait("sync", "pe", S.cnt["pe"])
            S.wait("sync", "dv", S.cnt["dv"])
            S.wait("sync", "ac", S.cnt["ac"])
            S.wait_total("sync", sorted(S.sems))

        def bcast(dst, src_row_ap_fn, width, transform, st_guard=None):
            last = None
            seg = min(512, width)
            rw = rowst_full[0:1, 0:seg]
            rw2 = rowst2[0:1, 0:seg]
            rw3 = rowst3[0:1, 0:seg]
            for sl0 in range(width // seg):
                if st_guard:
                    S.wait_total("sync", st_guard)
                t_l = S.load(rw, src_row_ap_fn(sl0 * seg, seg), "row")
                t_r = None
                if transform == "add1":
                    S.waitp("vector", t_l)
                    t_r = S.v(lambda e: e.tensor_scalar(
                        out=rw, in0=rw, scalar1=1.0, scalar2=None, op0=ALU.add), inc=True)
                elif transform == "signclamp":
                    S.waitp("vector", t_l)
                    S.v(lambda e: e.tensor_scalar(
                        out=rw2, in0=rw, scalar1=0.0, scalar2=None, op0=ALU.is_ge))
                    S.v(lambda e: e.tensor_scalar(
                        out=rw2, in0=rw2, scalar1=2.0, scalar2=-1.0,
                        op0=ALU.mult, op1=ALU.add))
                    S.v(lambda e: e.tensor_scalar(
                        out=rw3, in0=rw, scalar1=-1.0, scalar2=None, op0=ALU.mult))
                    S.v(lambda e: e.tensor_tensor(out=rw, in0=rw, in1=rw3, op=ALU.max))
                    S.v(lambda e: e.tensor_scalar(
                        out=rw, in0=rw, scalar1=0.25, scalar2=None, op0=ALU.max))
                    t_r = S.v(lambda e: e.tensor_tensor(
                        out=rw, in0=rw, in1=rw2, op=ALU.mult), inc=True)
                if t_r:
                    S.wait("tensor", "dv", t_r)
                else:
                    S.waitp("tensor", t_l)
                S.waitp("tensor", S.ps6)
                t_mm = S.pe(lambda e: e.matmul(out=ps[6][:, 0:seg], lhsT=ones_row,
                                               rhs=rw, start=True, stop=True), inc=True)
                S.wait("scalar", "pe", t_mm)
                t_e = S.s(lambda e, d=dst[:, sl0 * seg:(sl0 + 1) * seg]: e.activation(
                    out=d, in_=ps[6][:, 0:seg], func=AF.Copy), inc=True)
                S.ps6 = ("ac", t_e)
                S.wait("sync", "pe", t_mm)
                last = t_e
            return last

        S.load(idf, id_in[:, :], "ini")
        S.load(bq_sb, bq_c[:, :], "ini"); S.load(bk_sb, bk_c[:, :], "ini")
        t_cc = S.load(cc_sb, c_col[:, :], "ini")
        S.waitp("vector", t_cc)
        S.v(lambda e: e.memset(ones_blk, 1.0))
        S.v(lambda e: e.memset(c_eps5, 1e-5))
        S.v(lambda e: e.memset(c_eps6, 1e-6))
        S.v(lambda e: e.memset(c_mhalf, -0.5))
        S.v(lambda e: e.memset(c_phalf, 0.5))
        S.v(lambda e: e.memset(c_expb, EXP_BIAS))
        S.v(lambda e: e.memset(c_m128, -128.0))
        S.v(lambda e: e.tensor_copy(out=idb, in_=idf), inc=True)
        S.waitp("scalar", t_cc)
        t_silu = S.s(lambda e: e.activation(out=scc_sb, in_=cc_sb, func=AF.Silu), inc=True)

        p0 = Bump(PH)
        wch = [p0.f(6 * E), p0.f(6 * E)]
        acc = p0.f(6 * E)
        modst = [f32(PH, 512)[0:1, :], f32(PH + 512, 512)[0:1, :]]
        use_t = [None, None]
        for ec in range(EC):
            sl = ec % 2
            if use_t[sl]:
                S.wait("sync", "dv", use_t[sl])
            t_w = S.load(wch[sl], wmod[ec * 128:(ec + 1) * 128, :], f"wch{sl}")
            S.waitp("vector", t_w)
            if ec == 0:
                S.wait("vector", "ac", t_silu)
                use_t[sl] = S.v(lambda e, w=wch[sl]: e.tensor_scalar(
                    out=acc, in0=w, scalar1=scc_sb[:, 0:1], scalar2=None,
                    op0=ALU.mult), inc=True)
            else:
                use_t[sl] = S.v(lambda e, w=wch[sl], c=ec: e.scalar_tensor_tensor(
                    out=acc, in0=w, scalar=scc_sb[:, c:c + 1], in1=acc,
                    op0=ALU.mult, op1=ALU.add), inc=True)
        t_acc = use_t[(EC - 1) % 2]
        S.wait("tensor", "dv", t_acc)
        mst_use = [None, None]
        t_cp = None
        for nt in range(6 * E // 512):
            sl = nt % 2
            if t_cp:
                S.wait("tensor", "dv", t_cp)
            t_mm = S.pe(lambda e, n=nt: e.matmul(
                out=ps[0][:, 0:512], lhsT=ones_blk,
                rhs=acc[:, n * 512:(n + 1) * 512], start=True, stop=True), inc=True)
            if mst_use[sl]:
                S.waitp("vector", mst_use[sl])
            S.wait("vector", "pe", t_mm)
            t_cp = S.v(lambda e, d=modst[sl]: e.tensor_copy(
                out=d, in_=ps[0][0:1, 0:512]), inc=True)
            S.wait("gpsimd", "dv", t_cp)
            mst_use[sl] = S.store(mod_sc[0:1, nt * 512:(nt + 1) * 512], modst[sl], f"mo{sl}")
        mod_st_done = ["mo0", "mo1"]

        def modrow(g):
            return lambda o, n, gg=g: mod_sc[0:1, gg * E + o: gg * E + o + n]

        p0b = Bump(PH)
        hF = p0b.f(EC * T)
        O_AFTER_HF = p0b.off
        bc0 = p0b.f(E); bc1 = p0b.f(E)
        xt = [p0b.f(E), p0b.f(E)]
        h1t = p0b.f(E)
        t_b0 = bcast(bc0, modrow(1), E, "add1", st_guard=mod_st_done)
        t_b1 = bcast(bc1, modrow(0), E, None, st_guard=mod_st_done)
        xt_use = [None, None]
        tr_use = [None, None]
        last_hF_evac = None
        for tt in range(TC):
            sl = tt % 2
            if xt_use[sl]:
                S.wait("sync", "dv", xt_use[sl])
            t_x = S.load(xt[sl], x_in[tt * 128:(tt + 1) * 128, :], f"xt{sl}")
            S.waitp("vector", t_x)
            S.v(lambda e, x=xt[sl]: e.reduce_sum(out=s_s1, in_=x, axis=AX.X))
            S.waitp("scalar", t_x)
            t_sq = S.s(lambda e, x=xt[sl]: e.activation(
                out=h1t, in_=x, func=AF.Square, accum_out=s_ss), inc=True)
            S.v(lambda e: e.tensor_scalar(out=s_m, in0=s_s1, scalar1=1.0 / E,
                                          scalar2=None, op0=ALU.mult))
            S.v(lambda e: e.tensor_tensor(out=s_mm2, in0=s_m, in1=s_m, op=ALU.mult))
            S.wait("vector", "ac", t_sq)
            t_var = S.v(lambda e: e.scalar_tensor_tensor(
                out=s_var, in0=s_ss, scalar=1.0 / E, in1=s_mm2,
                op0=ALU.mult, op1=ALU.subtract), inc=True)
            S.wait("scalar", "dv", t_var)
            t_rstd = S.s(lambda e: e.activation(
                out=s_rstd, in_=s_var, func=AF.Sqrt, bias=c_eps5), inc=True)
            S.wait("vector", "ac", t_rstd)
            S.v(lambda e: e.reciprocal(out=s_rstd, in_=s_rstd))
            if tt == 0:
                S.wait("vector", "ac", t_b0)
                S.wait("vector", "ac", t_b1)
            xt_use[sl] = S.v(lambda e, x=xt[sl]: e.tensor_scalar(
                out=h1t, in0=x, scalar1=s_m, scalar2=s_rstd,
                op0=ALU.subtract, op1=ALU.mult), inc=True)
            S.v(lambda e: e.tensor_tensor(out=h1t, in0=h1t, in1=bc0, op=ALU.mult))
            t_h1 = S.v(lambda e: e.tensor_tensor(out=h1t, in0=h1t, in1=bc1, op=ALU.add), inc=True)
            S.wait("tensor", "dv", t_h1)
            for ec in range(EC):
                bk = ec % 2
                if tr_use[bk]:
                    S.wait("tensor", "ac", tr_use[bk])
                t_tr = S.pe(lambda e, c=ec, b=bk: e.transpose(
                    out=ps[1 + b][:, 0:128], in_=h1t[:, c * 128:(c + 1) * 128],
                    identity=idf), inc=True)
                S.wait("scalar", "pe", t_tr)
                tr_use[bk] = S.s(lambda e, c=ec, b=bk, t=tt: e.activation(
                    out=hF[:, c * T + t * 128: c * T + t * 128 + 128],
                    in_=ps[1 + b][:, 0:128], func=AF.Copy), inc=True)
            last_hF_evac = tr_use[(EC - 1) % 2]

        p1 = Bump(O_AFTER_HF)
        chi = p1.f(T); clo = p1.f(T); shi = p1.f(T); slo = p1.f(T)
        wt = [p1.f(E), p1.f(E)]
        qkst = [p1.f(512), p1.f(512)]
        swpst = [p1.f(512), p1.f(512)]
        rs1 = [p1.f(512), p1.f(512)]
        rs2 = p1.f(512)
        fence_sync()
        S.load(chi, chi_in[:, :], "tab"); S.load(clo, clo_in[:, :], "tab")
        S.load(shi, shi_in[:, :], "tab")
        t_tab = S.load(slo, slo_in[:, :], "tab")
        S.waitp("vector", t_tab)
        S.wait("tensor", "ac", last_hF_evac)
        wt_use = [None, None]
        st_use = [None, None]
        sw_use = [None, None]
        rs_store = [None, None]
        psA_use = [None, None]
        for (wdram, bias_sb, dst_sc) in ((wq_t, bq_sb, q_sc), (wk_t, bk_sb, k_sc)):
            for ft in range(EC):
                sl = ft % 2
                if wt_use[sl]:
                    S.wait("sync", "pe", wt_use[sl])
                t_w = S.load(wt[sl], wdram[ft * 128:(ft + 1) * 128, :], f"wt{sl}")
                S.waitp("tensor", t_w)
                for tq in range(QT):
                    bk = tq % 2
                    pA = ps[0] if bk == 0 else ps[3]
                    S.waitp("tensor", psA_use[bk])
                    for ec in range(EC):
                        t_mm = S.pe(lambda e, c=ec, q=tq, w=wt[sl], p=pA: e.matmul(
                            out=p[:, 0:512], lhsT=w[:, c * 128:(c + 1) * 128],
                            rhs=hF[:, c * T + q * 512: c * T + (q + 1) * 512],
                            start=(c == 0), stop=(c == EC - 1),
                            skip_group_check=True), inc=(ec == EC - 1))
                    wt_use[sl] = t_mm
                    ssl = (ft * QT + tq) % 2
                    if st_use[ssl]:
                        S.wait("scalar", "dv", st_use[ssl])
                    S.wait("scalar", "pe", t_mm)
                    t_ev = S.s(lambda e, f=ft, d=qkst[ssl], p=pA, b=bias_sb: e.activation(
                        out=d, in_=p[:, 0:512], func=AF.Identity, bias=b[:, f:f + 1]), inc=True)
                    psA_use[bk] = ("ac", t_ev)
                    if sw_use[ssl]:
                        S.wait("gpsimd", "dv", sw_use[ssl])
                    S.wait("gpsimd", "ac", t_ev)
                    S.store(swpst[ssl][0:64, :], qkst[ssl][64:128, :], f"sw{ssl}")
                    t_sw = S.store(swpst[ssl][64:128, :], qkst[ssl][0:64, :], f"sw{ssl}")
                    if rs_store[ssl]:
                        S.waitp("vector", rs_store[ssl])
                    S.waitp("vector", t_sw)
                    c0, c1 = tq * 512, (tq + 1) * 512
                    r1 = rs1[ssl]
                    S.v(lambda e, s=qkst[ssl], a=c0, b2=c1, r=r1: e.tensor_tensor(
                        out=r, in0=s, in1=chi[:, a:b2], op=ALU.mult))
                    S.v(lambda e, s=qkst[ssl], a=c0, b2=c1: e.tensor_tensor(
                        out=rs2, in0=s, in1=clo[:, a:b2], op=ALU.mult))
                    st_use[ssl] = S.v(lambda e, r=r1: e.tensor_tensor(
                        out=r, in0=r, in1=rs2, op=ALU.add), inc=True)
                    S.v(lambda e, s=swpst[ssl], a=c0, b2=c1: e.tensor_tensor(
                        out=rs2, in0=s, in1=shi[:, a:b2], op=ALU.mult))
                    S.v(lambda e, r=r1: e.tensor_tensor(out=r, in0=r, in1=rs2, op=ALU.add))
                    sw_use[ssl] = S.v(lambda e, s=swpst[ssl], a=c0, b2=c1: e.tensor_tensor(
                        out=rs2, in0=s, in1=slo[:, a:b2], op=ALU.mult), inc=True)
                    t_rot = S.v(lambda e, r=r1: e.tensor_tensor(
                        out=r, in0=r, in1=rs2, op=ALU.add), inc=True)
                    S.wait("gpsimd", "dv", t_rot)
                    rs_store[ssl] = S.store(
                        dst_sc[ft * 128:(ft + 1) * 128, tq * 512:(tq + 1) * 512], r1, f"rq{ssl}")

        pv = Bump(O_AFTER_HF)
        vsl = pv.f(EC * NV)
        vst = [pv.f(NV), pv.f(NV)]
        bvB = pv.f(E)
        fence_sync()
        t_bv = bcast(bvB, lambda o, n: bv_row[0:1, o:o + n], E, None)
        vst_use = [None, None]
        last_vmm = None
        for fs in range(FSV):
            if last_vmm:
                S.wait("sync", "pe", last_vmm)
            t_v = S.load(vsl, wv_r[fs * 128:(fs + 1) * 128, :], "vsl")
            S.waitp("tensor", t_v)
            for tt in range(TC):
                bk = tt % 2
                pA = ps[0] if bk == 0 else ps[3]
                S.waitp("tensor", psA_use[bk])
                for ec in range(EC):
                    t_mm = S.pe(lambda e, c=ec, t=tt, p=pA: e.matmul(
                        out=p[:, 0:NV], lhsT=hF[:, c * T + t * 128: c * T + t * 128 + 128],
                        rhs=vsl[:, c * NV:(c + 1) * NV], start=(c == 0), stop=(c == EC - 1),
                        skip_group_check=True), inc=(ec == EC - 1))
                last_vmm = t_mm
                ssl = tt % 2
                if vst_use[ssl]:
                    S.waitp("vector", vst_use[ssl])
                S.wait("vector", "pe", t_mm)
                if fs == 0 and tt == 0:
                    S.wait("vector", "ac", t_bv)
                t_vv = S.v(lambda e, d=vst[ssl], p=pA, f=fs: e.tensor_tensor(
                    out=d, in0=p[:, 0:NV], in1=bvB[:, f * NV:(f + 1) * NV],
                    op=ALU.add), inc=True)
                psA_use[bk] = ("dv", t_vv)
                S.wait("gpsimd", "dv", t_vv)
                for hh in range(NV // 128):
                    h = fs * (NV // 128) + hh
                    vst_use[ssl] = S.store(
                        v_sc[h * T + tt * 128: h * T + (tt + 1) * 128, :],
                        vst[ssl][:, hh * 128:(hh + 1) * 128], f"vs{ssl}")
        spill_done = ["rq0", "rq1", "vs0", "vs1"]

        p2 = Bump(O_AFTER_HF)
        qrot = [p2.f(T), p2.f(T)]
        krot = [p2.f(T), p2.f(T)]
        VRW = TC * 129
        vrhs = [p2.f(VRW + 7), p2.f(VRW + 7)]
        pT = [p2.f(512), p2.f(512), p2.f(512)]
        yst = [p2.f(128), p2.f(128), p2.f(128), p2.f(128)]
        masks_t = p2.f(4 * 512)
        masks = [masks_t[:, i * 512:(i + 1) * 512] for i in range(4)]
        fence_sync()
        t_mload = S.load(masks_t, mask_in[:, :], "msk")
        t_ones = None
        for vb in range(2):
            for c in range(TC):
                t_ones = S.v(lambda e, b=vb, cc=c: e.memset(
                    vrhs[b][:, cc * 129 + 128: cc * 129 + 129], 1.0), inc=True)
        S.wait_total("sync", spill_done)
        S.wait("sync", "dv", t_ones)

        head_use_pe = [None, None]
        ptslot_use = [None, None, None]
        yst_use = [None, None, None, None]
        ptr_use = None
        av_dv = [None] * 4
        first_mask_wait = True
        yF_done = None
        for h in range(H):
            hb = h % 2
            if head_use_pe[hb]:
                S.wait("sync", "pe", head_use_pe[hb])
            S.load(qrot[hb], q_sc[h * 128:(h + 1) * 128, :], f"hd{hb}")
            S.load(krot[hb], k_sc[h * 128:(h + 1) * 128, :], f"hd{hb}")
            for c in range(TC):
                t_v = S.load(vrhs[hb][:, c * 129:c * 129 + 128],
                             v_sc[h * T + c * 128: h * T + (c + 1) * 128, :], f"hd{hb}")
            S.waitp("tensor", t_v)
            exp_done = {}
            avlast = {}
            SBANKS = (0, 1, 7)
            for j in range(QT):
                nact = min(TC, 4 * (j + 1))
                for c0 in range(0, nact, 3):
                    batch = list(range(c0, min(c0 + 3, nact)))
                    for c in batch:
                        sbk = c % 3
                        pS = ps[SBANKS[sbk]]
                        if ptslot_use[sbk]:
                            S.wait("tensor", "ac", exp_done.get(c - 3, 0) or 0)
                        t_sc2 = S.pe(lambda e, cc=c, jj=j, p=pS, b=hb: e.matmul(
                            out=p[:, 0:512], lhsT=krot[b][:, cc * 128:(cc + 1) * 128],
                            rhs=qrot[b][:, jj * 512:(jj + 1) * 512],
                            start=True, stop=True, skip_group_check=True), inc=True)
                        diag = c >= 4 * j
                        if diag:
                            if first_mask_wait:
                                S.waitp("vector", t_mload)
                                first_mask_wait = False
                            S.wait("vector", "pe", t_sc2)
                            t_mk = S.v(lambda e, p=pS, r=c - 4 * j: e.tensor_tensor(
                                out=p[:, 0:512], in0=p[:, 0:512], in1=masks[r],
                                op=ALU.add), inc=True)
                            S.wait("scalar", "dv", t_mk)
                        else:
                            S.wait("scalar", "pe", t_sc2)
                        if ptslot_use[sbk]:
                            S.wait("scalar", "pe", ptslot_use[sbk])
                        t_exp = S.s(lambda e, p=pS, d=pT[sbk]: e.activation(
                            out=d, in_=p[:, 0:512], func=AF.Exp, bias=c_expb), inc=True)
                        exp_done[c] = t_exp
                    S.wait("tensor", "ac", exp_done[batch[-1]])
                    for c in batch:
                        sbk = c % 3
                        last = None
                        for qs in range(4):
                            qq = 4 * j + qs
                            if c > qq:
                                continue
                            if c == 0 and av_dv[qs] is not None:
                                S.wait("tensor", "dv", av_dv[qs])
                            last = S.pe(lambda e, cc=c, q=qs, b=hb, qqv=qq, sb2=sbk: e.matmul(
                                out=ps[2 + q][:, 0:129], lhsT=pT[sb2][:, q * 128:(q + 1) * 128],
                                rhs=vrhs[b][:, cc * 129: cc * 129 + 129],
                                start=(cc == 0), stop=(cc == qqv),
                                skip_group_check=True), inc=True)
                        avlast[c] = last
                        ptslot_use[sbk] = last
                t_trl = None
                for qs in range(4):
                    S.wait("vector", "pe", avlast[4 * j + qs])
                    if yst_use[qs]:
                        S.wait("vector", "pe", yst_use[qs])
                    S.v(lambda e, q=qs: e.reciprocal(out=s_zr, in_=ps[2 + q][:, 128:129]))
                    av_dv[qs] = S.v(lambda e, q=qs, d=yst[qs]: e.tensor_scalar(
                        out=d, in0=ps[2 + q][:, 0:128], scalar1=s_zr, scalar2=None,
                        op0=ALU.mult), inc=True)
                    S.wait("tensor", "dv", av_dv[qs])
                    if qs == 0:
                        if ptr_use:
                            S.wait("tensor", "ac", ptr_use)
                        S.waitp("tensor", S.ps6)
                    t_trl = S.pe(lambda e, d=yst[qs], q=qs: e.transpose(
                        out=ps[6][:, q * 128:(q + 1) * 128], in_=d, identity=idf), inc=True)
                    yst_use[qs] = t_trl
                S.wait("scalar", "pe", t_trl)
                ptr_use = S.s(lambda e, hh2=h, jj2=j: e.activation(
                    out=hF[:, hh2 * T + jj2 * 512: hh2 * T + jj2 * 512 + 512],
                    in_=ps[6][:, 0:512], func=AF.Copy), inc=True)
                S.ps6 = ("ac", ptr_use)
            head_use_pe[hb] = avlast[nact - 1]
        yF_done = ptr_use

        p3 = Bump(O_AFTER_HF)
        bcA = p3.f(E); bcB = p3.f(E)
        wt2 = [p3.f(EC * N2), p3.f(EC * N2)]
        xinst = [p3.f(N2), p3.f(N2)]
        x2st = [p3.f(N2), p3.f(N2)]
        stst = [p3.f(1024), p3.f(1024)]
        stdmp = p3.f(1024)
        S.wait("tensor", "ac", yF_done)
        fence_sync()
        t_bo = bcast(bcA, lambda o, n: bo_row[0:1, o:o + n], E, None)
        t_ga = bcast(bcB, modrow(2), E, "signclamp", st_guard=mod_st_done)

        stats_jobs = []
        nch_m = [0, 0, 0]
        for mi, (wdram, rows, cols) in enumerate(((wg_r, JS * 128, EC * 512),
                                                  (wu_r, JS * 128, EC * 512),
                                                  (wd_t, I, E))):
            ccw = min(1024, cols)
            cpr = cols // ccw
            nch_m[mi] = (rows // 128) * cpr
            for ch in range(nch_m[mi]):
                stats_jobs.append((mi, wdram, (ch // cpr) * 128, (ch % cpr) * ccw, ch, ccw))
        st_slot_use = [None, None]
        nstats = len(stats_jobs)

        def emit_stats(idx):
            mi, wdram, r0, c0, ch, ccw = stats_jobs[idx]
            sl = idx % 2
            if st_slot_use[sl]:
                S.wait("sync", "ac", st_slot_use[sl])
            t_l = S.load(stst[sl][:, 0:ccw], wdram[r0:r0 + 128, c0:c0 + ccw], f"ss{sl}")
            S.waitp("scalar", t_l)
            st_slot_use[sl] = S.s(lambda e, s=stst[sl][:, 0:ccw], m=mi, p=ch, w=ccw: e.activation(
                out=stdmp[:, 0:w], in_=s, func=AF.Abs, accum_out=parts[m][:, p:p + 1]), inc=True)
            return st_slot_use[sl]

        wt2_use = [None, None]
        x2_dv = [None, None]
        x2st_st = [None, None]
        si = 0
        njobs_per = max(1, (nstats + FS2 * TC - 1) // (FS2 * TC))
        last_stats = None
        for fs in range(FS2):
            sl = fs % 2
            if wt2_use[sl]:
                S.wait("sync", "pe", wt2_use[sl])
            t_w = S.load(wt2[sl], wo_r[fs * 128:(fs + 1) * 128, :], f"wo{sl}")
            S.waitp("tensor", t_w)
            for tt in range(TC):
                bk = tt % 2
                pA = ps[0] if bk == 0 else ps[1]
                if x2_dv[bk]:
                    S.wait("tensor", "dv", x2_dv[bk])
                for ec in range(EC):
                    t_mm = S.pe(lambda e, c=ec, t=tt, p=pA, w=wt2[sl]: e.matmul(
                        out=p[:, 0:N2], lhsT=hF[:, c * T + t * 128: c * T + t * 128 + 128],
                        rhs=w[:, c * N2:(c + 1) * N2], start=(c == 0), stop=(c == EC - 1),
                        skip_group_check=True), inc=(ec == EC - 1))
                wt2_use[sl] = t_mm
                ssl = tt % 2
                if x2_dv[bk]:
                    S.wait("sync", "dv", x2_dv[bk])
                t_xl = S.load(xinst[ssl], x_in[tt * 128:(tt + 1) * 128, fs * N2:(fs + 1) * N2], f"xi{ssl}")
                if x2st_st[ssl]:
                    S.waitp("vector", x2st_st[ssl])
                S.wait("vector", "pe", t_mm)
                S.waitp("vector", t_xl)
                if fs == 0 and tt == 0:
                    S.wait("vector", "ac", t_bo)
                    S.wait("vector", "ac", t_ga)
                S.v(lambda e, p=pA, d=x2st[ssl], f=fs: e.tensor_tensor(
                    out=d, in0=p[:, 0:N2], in1=bcA[:, f * N2:(f + 1) * N2], op=ALU.add))
                S.v(lambda e, d=x2st[ssl], f=fs: e.tensor_tensor(
                    out=d, in0=d, in1=bcB[:, f * N2:(f + 1) * N2], op=ALU.mult))
                x2_dv[bk] = S.v(lambda e, d=x2st[ssl], x=xinst[ssl]: e.tensor_tensor(
                    out=d, in0=d, in1=x, op=ALU.add), inc=True)
                S.wait("gpsimd", "dv", x2_dv[bk])
                x2st_st[ssl] = S.store(
                    x2_sc[tt * 128:(tt + 1) * 128, fs * N2:(fs + 1) * N2], x2st[ssl], f"xs{ssl}")
                for _ in range(njobs_per):
                    if si < nstats:
                        last_stats = emit_stats(si); si += 1
        while si < nstats:
            last_stats = emit_stats(si); si += 1
        x2_done = ["xs0", "xs1"]

        S.wait("vector", "ac", last_stats)
        t_rc = None
        for mi, (s_sw, s_hf) in enumerate(((s_swg, s_hfg), (s_swu, s_hfu), (s_swd, s_hfd))):
            t_cs = S.v(lambda e, m=mi: e.reduce_sum(
                out=s_zr, in_=parts[m][:, 0:nch_m[m]], axis=AX.X), inc=True)
            S.wait("tensor", "dv", t_cs)
            S.waitp("tensor", S.ps6)
            t_mm = S.pe(lambda e: e.matmul(out=ps[6][:, 0:1], lhsT=ones_blk,
                                           rhs=s_zr, start=True, stop=True), inc=True)
            S.wait("vector", "pe", t_mm)
            S.v(lambda e, d=s_sw: e.tensor_scalar(
                out=d, in0=ps[6][:, 0:1], scalar1=1.0 / (I * E),
                scalar2=None, op0=ALU.mult))
            t_mn = S.v(lambda e, d=s_sw: e.tensor_scalar(
                out=d, in0=d, scalar1=1e-5, scalar2=None, op0=ALU.max), inc=True)
            S.ps6 = ("dv", t_mn)
            S.v(lambda e, d=s_hf, s0=s_sw: e.tensor_scalar(
                out=d, in0=s0, scalar1=0.5, scalar2=None, op0=ALU.mult))
            t_rc = S.v(lambda e, d=s_sw: e.reciprocal(out=d, in_=d), inc=True)
        stats_done = t_rc

        p3b = Bump(PH)
        qxF = p3b.b(EC * T)
        O_AFTER_QXF = p3b.off
        x2t = [p3b.f(E), p3b.f(E)]; h2t = [p3b.f(E), p3b.f(E)]
        LW3 = min(1024, E // 2)
        lad_y = p3b.f(LW3); lad_r = p3b.f(LW3); lad_b = p3b.f(LW3)
        lad_y2 = p3b.f(LW3); lad_r2 = p3b.f(LW3); lad_b2 = p3b.f(LW3)
        sqdump = p3b.f(E)
        bcC = p3b.f(E); bcD = p3b.f(E)
        wdst = [p3b.f(E), p3b.f(E)]
        ternb = [p3b.b(E), p3b.b(E)]
        scrd = p3b.b(E)
        fence_sync()
        t_cb = bcast(bcC, modrow(4), E, "add1", st_guard=mod_st_done)
        t_sb = bcast(bcD, modrow(3), E, "none")
        S.wait_total("sync", x2_done)

        wd_use = [None, None]
        tb_use = [None, None]
        tern_st = [None, None]

        def emit_tern(ic):
            sl = ic % 2
            if wd_use[sl]:
                S.wait("sync", "ac", wd_use[sl])
            t_l = S.load(wdst[sl], wd_t[ic * 128:(ic + 1) * 128, :], f"wd{sl}")
            S.waitp("scalar", t_l)
            if ic == 0:
                S.wait("scalar", "dv", stats_done)
            if tern_st[sl]:
                S.waitp("scalar", tern_st[sl])
            for g in tb_use:
                if g:
                    S.wait("scalar", "dv", g)
            S.s(lambda e, s=wdst[sl], d=ternb[sl]: e.activation(
                out=d, in_=s, func=AF.Sign, scale=s_swd, bias=c_mhalf))
            wd_use[sl] = S.s(lambda e, s=wdst[sl]: e.activation(
                out=scrd, in_=s, func=AF.Sign, scale=s_swd, bias=c_phalf), inc=True)
            S.wait("vector", "ac", wd_use[sl])
            tb_use[sl] = S.v(lambda e, d=ternb[sl]: e.tensor_tensor(
                out=d, in0=d, in1=scrd, op=ALU.add), inc=True)
            S.wait("gpsimd", "dv", tb_use[sl])
            tern_st[sl] = S.store(tern_sc[ic * 128:(ic + 1) * 128, :], ternb[sl], f"ts{sl}")

        qx_tr_use = [None, None]
        last_qx_evac = None
        tr_last = [None, None]
        x2t_use = [None, None]
        sq_done = [None, None]
        ld_done = [None, None]
        ic_done = 0

        def p3b_stage1(tt):
            p = tt % 2
            x2s = x2t[p]
            if x2t_use[p]:
                S.wait("sync", "dv", x2t_use[p])
            t_x2 = S.load(x2s, x2_sc[tt * 128:(tt + 1) * 128, :], f"x2t{p}")
            ld_done[p] = t_x2
            S.waitp("scalar", t_x2)
            sq_done[p] = S.s(lambda e, x=x2s, p2=p: e.activation(
                out=sqdump, in_=x, func=AF.Square, accum_out=sp_ss[p2]), inc=True)

        def p3b_stage2(tt):
            p = tt % 2
            x2s = x2t[p]
            h2s = h2t[p]
            S.waitp("vector", ld_done[p])
            S.v(lambda e, x=x2s, p2=p: e.reduce_sum(out=sp_s1[p2], in_=x, axis=AX.X))
            S.v(lambda e, p2=p: e.tensor_scalar(out=sp_m[p2], in0=sp_s1[p2],
                scalar1=1.0 / E, scalar2=None, op0=ALU.mult))
            S.v(lambda e, p2=p: e.tensor_tensor(out=sp_mm2[p2], in0=sp_m[p2],
                in1=sp_m[p2], op=ALU.mult))
            S.wait("vector", "ac", sq_done[p])
            t_var = S.v(lambda e, p2=p: e.scalar_tensor_tensor(
                out=sp_var[p2], in0=sp_ss[p2], scalar=1.0 / E, in1=sp_mm2[p2],
                op0=ALU.mult, op1=ALU.subtract), inc=True)
            S.wait("scalar", "dv", t_var)
            t_rstd = S.s(lambda e, p2=p: e.activation(
                out=sp_rstd[p2], in_=sp_var[p2], func=AF.Sqrt, bias=c_eps5), inc=True)
            S.wait("vector", "ac", t_rstd)
            S.v(lambda e, p2=p: e.reciprocal(out=sp_rstd[p2], in_=sp_rstd[p2]))
            if tt == 0:
                S.wait("vector", "ac", t_cb)
                S.wait("vector", "ac", t_sb)
            if tr_last[p]:
                S.wait("vector", "pe", tr_last[p])
            x2t_use[p] = S.v(lambda e, x=x2s, d=h2s, p2=p: e.tensor_scalar(
                out=d, in0=x, scalar1=sp_m[p2], scalar2=sp_rstd[p2],
                op0=ALU.subtract, op1=ALU.mult), inc=True)
            S.v(lambda e, d=h2s: e.tensor_tensor(out=d, in0=d, in1=bcC, op=ALU.mult))
            S.v(lambda e, d=h2s: e.tensor_tensor(out=d, in0=d, in1=bcD, op=ALU.add))
            S.v(lambda e, d=h2s, p2=p: e.reduce_max(out=sp_amax[p2], in_=d, axis=AX.X,
                                                    apply_absolute_value=True))
            S.v(lambda e, p2=p: e.tensor_scalar(
                out=sp_amax[p2], in0=sp_amax[p2], scalar1=1e-5, scalar2=None, op0=ALU.max))
            S.v(lambda e, t=tt, p2=p: e.tensor_scalar(
                out=invsx[:, t:t + 1], in0=sp_amax[p2], scalar1=1.0 / 127.0,
                scalar2=None, op0=ALU.mult))
            S.v(lambda e, t=tt, p2=p: e.reciprocal(out=sp_sx[p2], in_=invsx[:, t:t + 1]))

        def p3b_ladder(tt):
            nonlocal last_qx_evac, ic_done
            p = tt % 2
            h2s = h2t[p]
            t_lad = ladder_sd(h2s, sp_sx[p], h2s, E,
                              [lad_y, lad_y2], [lad_r, lad_r2], [lad_b, lad_b2], LW3)
            S.wait("tensor", "dv", t_lad)
            for e0 in range(0, EC, 4):
                npk = min(4, EC - e0)
                bsl = (e0 // 4) % 2
                if qx_tr_use[bsl]:
                    S.wait("tensor", "ac", qx_tr_use[bsl])
                t_tr = None
                for k in range(npk):
                    t_tr = S.pe(lambda e, c=e0 + k, b=bsl, d=h2s, kk=k: e.transpose(
                        out=ps[1 + b][:, kk * 128:(kk + 1) * 128],
                        in_=d[:, c * 128:(c + 1) * 128],
                        identity=idf), inc=True)
                tr_last[p] = t_tr
                S.wait("scalar", "pe", t_tr)
                qx_tr_use[bsl] = S.s(lambda e, c=e0, b=bsl, t=tt, w=npk * 128: e.activation(
                    out=qxF[:, t * E + c * 128: t * E + c * 128 + w],
                    in_=ps[1 + b][:, 0:w], func=AF.Copy), inc=True)
            last_qx_evac = qx_tr_use[((EC - 1) // 4) % 2]
            ntern = (IC * (tt + 1)) // TC
            while ic_done < ntern:
                emit_tern(ic_done); ic_done += 1

        p3b_stage1(0)
        for tt in range(TC):
            if tt + 1 < TC:
                p3b_stage1(tt + 1)
            p3b_stage2(tt)
            p3b_ladder(tt)
        while ic_done < IC:
            emit_tern(ic_done); ic_done += 1
        tern_done = ["ts0", "ts1"]

        t_scl = None
        for tt in range(TC):
            S.v(lambda e, t=tt: e.tensor_tensor(
                out=sclg[:, t:t + 1], in0=invsx[:, t:t + 1], in1=s_hfg, op=ALU.mult))
            t_scl = S.v(lambda e, t=tt: e.tensor_tensor(
                out=sclu[:, t:t + 1], in0=invsx[:, t:t + 1], in1=s_hfu, op=ALU.mult), inc=True)

        p4 = Bump(O_AFTER_QXF)
        gst = p4.f(EC * 512)
        ternG = [p4.b(EC * 512), p4.b(EC * 512)]
        ternU = [p4.b(EC * 512), p4.b(EC * 512)]
        scrg = p4.b(EC * 512)
        gs_t = [p4.f(512), p4.f(512)]
        hst = [p4.f(512), p4.f(512)]

        fence_sync()
        tG_use = [None, None]; tU_use = [None, None]
        tU_rdy_prev = None
        gst_free = None
        hst_use = [None, None]
        psG_use = [None, None]; psU_use = [None, None]
        for js in range(JS):
            sl = js % 2
            if gst_free:
                S.wait("sync", "ac", gst_free)
            if tG_use[sl]:
                S.wait("sync", "pe", tG_use[sl])
            t_g = S.load(gst, wg_r[js * 128:(js + 1) * 128, :], "gst")
            S.waitp("scalar", t_g)
            if js == 0:
                S.wait("scalar", "dv", stats_done)
            if tU_rdy_prev:
                S.wait("scalar", "dv", tU_rdy_prev)
            S.s(lambda e, d=ternG[sl]: e.activation(
                out=d, in_=gst, func=AF.Sign, scale=s_swg, bias=c_mhalf))
            t_sg = S.s(lambda e: e.activation(
                out=scrg, in_=gst, func=AF.Sign, scale=s_swg, bias=c_phalf), inc=True)
            gst_free = t_sg
            S.wait("vector", "ac", t_sg)
            tG_rdy = S.v(lambda e, d=ternG[sl]: e.tensor_tensor(
                out=d, in0=d, in1=scrg, op=ALU.add), inc=True)
            if tU_use[sl]:
                S.wait("sync", "pe", tU_use[sl])
            S.wait("sync", "ac", t_sg)
            t_u = S.load(gst, wu_r[js * 128:(js + 1) * 128, :], "gst")
            S.waitp("scalar", t_u)
            S.wait("scalar", "dv", tG_rdy)
            S.s(lambda e, d=ternU[sl]: e.activation(
                out=d, in_=gst, func=AF.Sign, scale=s_swu, bias=c_mhalf))
            t_su = S.s(lambda e: e.activation(
                out=scrg, in_=gst, func=AF.Sign, scale=s_swu, bias=c_phalf), inc=True)
            gst_free = t_su
            S.wait("vector", "ac", t_su)
            tU_rdy = S.v(lambda e, d=ternU[sl]: e.tensor_tensor(
                out=d, in0=d, in1=scrg, op=ALU.add), inc=True)
            tU_rdy_prev = tU_rdy
            if js == 0:
                S.wait("tensor", "ac", last_qx_evac)
            S.wait("tensor", "dv", tU_rdy)
            for tt in range(TC):
                bg = tt % 2
                pG = ps[0] if bg == 0 else ps[1]
                pU = ps[2] if bg == 0 else ps[3]
                S.waitp("tensor", psG_use[bg])
                for ec in range(EC):
                    t_gmm = S.pe(lambda e, c=ec, t=tt, p=pG, d=ternG[sl]: e.matmul(
                        out=p[:, 0:512], lhsT=qxF[:, t * E + c * 128: t * E + c * 128 + 128],
                        rhs=d[:, c * 512:(c + 1) * 512], start=(c == 0), stop=(c == EC - 1),
                        skip_group_check=True), inc=(ec == EC - 1))
                S.waitp("tensor", psU_use[bg])
                for ec in range(EC):
                    t_umm = S.pe(lambda e, c=ec, t=tt, p=pU, d=ternU[sl]: e.matmul(
                        out=p[:, 0:512], lhsT=qxF[:, t * E + c * 128: t * E + c * 128 + 128],
                        rhs=d[:, c * 512:(c + 1) * 512], start=(c == 0), stop=(c == EC - 1),
                        skip_group_check=True), inc=(ec == EC - 1))
                tG_use[sl] = t_gmm; tU_use[sl] = t_umm
                ssl = tt % 2
                S.wait("scalar", "pe", t_gmm)
                if js == 0 and tt == 0:
                    S.wait("scalar", "dv", t_scl)
                if psU_use[bg]:
                    S.wait("scalar", "dv", psU_use[bg][1])
                t_si = S.s(lambda e, p=pG, d=gs_t[ssl], t=tt: e.activation(
                    out=d, in_=p[:, 0:512], func=AF.Silu, scale=sclg[:, t:t + 1]), inc=True)
                psG_use[bg] = ("ac", t_si)
                if hst_use[ssl]:
                    S.waitp("vector", hst_use[ssl])
                S.wait("vector", "pe", t_umm)
                S.wait("vector", "ac", t_si)
                t_h = S.v(lambda e, p=pU, d=hst[ssl], g=gs_t[ssl], t=tt: e.scalar_tensor_tensor(
                    out=d, in0=p[:, 0:512], scalar=sclu[:, t:t + 1], in1=g,
                    op0=ALU.mult, op1=ALU.mult), inc=True)
                psU_use[bg] = ("dv", t_h)
                S.wait("gpsimd", "dv", t_h)
                hst_use[ssl] = S.store(
                    h_sc[tt * 128:(tt + 1) * 128, js * 512:(js + 1) * 512], hst[ssl], f"hs{ssl}")
        h_done = ["hs0", "hs1"]

        p5 = Bump(PH)
        lnwB = p5.f(I)
        h2t2 = [p5.f(I), p5.f(I)]
        qh2F = p5.b(IC * 256)
        ternD = [p5.b(E), p5.b(E)]
        x2l = p5.f(E)
        xout = p5.f(E)
        gbB = p5.f(E)
        LW5 = min(1024, I // 2)
        dl_y = [p5.f(LW5) for _ in range(3)]
        dl_r = [p5.f(LW5) for _ in range(3)]
        dl_b = [p5.f(LW5) for _ in range(3)]
        dl_dump = p5.f(512)
        sqparts = p5.f(2 * (I // 512))

        fence_sync()
        t_lnw = bcast(lnwB, lambda o, n: lnw_row[0:1, o:o + n], I, None)
        t_gb = bcast(gbB, modrow(5), E, "signclamp", st_guard=mod_st_done)
        S.wait_total("sync", h_done)
        S.wait_total("sync", tern_done)

        h2buf_use = [None, None]
        ternD_use = [None, None]
        tr2_use = [None, None]
        qh2_evac = None
        prev_epi_dv = None
        xout_st = [None, None]

        ld5_done = [None, None]
        sq5_done = [None, None]
        h2_guard_p = [None, None]

        def p5_stage1(b6, sub):
            tt = 2 * b6 + sub
            hb2 = h2t2[sub]
            if h2buf_use[sub]:
                S.wait("sync", "pe", h2buf_use[sub])
            if h2_guard_p[sub]:
                S.wait("sync", "dv", h2_guard_p[sub])
            t_h = S.load(hb2, h_sc[tt * 128:(tt + 1) * 128, :], f"h2l{sub}")
            ld5_done[sub] = t_h
            S.waitp("scalar", t_h)
            if b6 == 0 and sub == 0:
                S.wait("scalar", "ac", t_lnw)
            t_sq = None
            nstrip = I // 512
            for si in range(nstrip):
                t_sq = S.s(lambda e, o=si * 512, p=si, s2=sub, h=hb2: e.activation(
                    out=dl_dump, in_=h[:, o:o + 512], func=AF.Square,
                    accum_out=sqparts[:, s2 * nstrip + p: s2 * nstrip + p + 1]), inc=True)
            sq5_done[sub] = t_sq

        def p5_stage2lad(b6, sub):
            tt = 2 * b6 + sub
            hb2 = h2t2[sub]
            nstrip = I // 512
            S.waitp("vector", ld5_done[sub])
            S.wait("vector", "ac", sq5_done[sub])
            S.v(lambda e, s2=sub: e.reduce_sum(
                out=sp_ms[s2], in_=sqparts[:, s2 * nstrip:(s2 + 1) * nstrip], axis=AX.X))
            S.v(lambda e, h=hb2: e.tensor_tensor(out=h, in0=h, in1=lnwB, op=ALU.mult))
            S.v(lambda e, h=hb2, s2=sub: e.reduce_max(
                out=sp_rm[s2], in_=h, axis=AX.X, apply_absolute_value=True))
            t_ms = S.v(lambda e, s2=sub: e.tensor_scalar(
                out=sp_ms[s2], in0=sp_ms[s2], scalar1=1.0 / I, scalar2=None,
                op0=ALU.mult), inc=True)
            S.wait("scalar", "dv", t_ms)
            t_r2 = S.s(lambda e, s2=sub: e.activation(
                out=sp_rstd2[s2], in_=sp_ms[s2], func=AF.Sqrt, bias=c_eps6), inc=True)
            S.wait("vector", "ac", t_r2)
            S.v(lambda e, s2=sub: e.reciprocal(out=sp_rstd2[s2], in_=sp_rstd2[s2]))
            S.v(lambda e, s2=sub: e.tensor_tensor(
                out=sp_rm[s2], in0=sp_rm[s2], in1=sp_rstd2[s2], op=ALU.mult))
            S.v(lambda e, s2=sub: e.tensor_scalar(
                out=sp_rm[s2], in0=sp_rm[s2], scalar1=1e-5, scalar2=None, op0=ALU.max))
            S.v(lambda e, s2=sub, p=b6 % 2: e.tensor_scalar(
                out=s_invs2[p][s2], in0=sp_rm[s2], scalar1=1.0 / 127.0,
                scalar2=None, op0=ALU.mult))
            S.v(lambda e, s2=sub, p=b6 % 2: e.reciprocal(out=sp_sx2[s2], in_=s_invs2[p][s2]))
            S.v(lambda e, s2=sub: e.tensor_tensor(
                out=sp_sx2[s2], in0=sp_sx2[s2], in1=sp_rstd2[s2], op=ALU.mult))
            t_lad = ladder_sd(hb2, sp_sx2[sub], hb2, I, dl_y, dl_r, dl_b, LW5)
            h2_guard_p[sub] = t_lad
            return t_lad

        def emit_transposes(b6, t_lad):
            nonlocal qh2_evac, h2buf_use
            S.wait("tensor", "dv", t_lad)
            if prev_epi_dv:
                S.wait("tensor", "dv", prev_epi_dv)
            for sub in range(2):
                last_tr = None
                for i0 in range(0, IC, 4):
                    npk = min(4, IC - i0)
                    bsl = (i0 // 4) % 2
                    if tr2_use[bsl]:
                        S.wait("tensor", "ac", tr2_use[bsl])
                    t_tr = None
                    for k in range(npk):
                        t_tr = S.pe(lambda e, c=i0 + k, b=bsl, s2=sub, kk=k: e.transpose(
                            out=ps[1 + b][:, kk * 128:(kk + 1) * 128],
                            in_=h2t2[s2][:, c * 128:c * 128 + 128],
                            identity=idf), inc=True)
                    S.wait("scalar", "pe", t_tr)
                    tr2_use[bsl] = S.s(lambda e, c=i0, b=bsl, s2=sub, w=npk * 128: e.activation(
                        out=qh2F[:, s2 * IC * 128 + c * 128: s2 * IC * 128 + c * 128 + w],
                        in_=ps[1 + b][:, 0:w], func=AF.Copy), inc=True)
                    last_tr = t_tr
                h2buf_use[sub] = last_tr
            qh2_evac = tr2_use[(((IC - 1) // 4) % 2)]

        def emit_down(b6):
            nonlocal prev_epi_dv, ternD_use
            S.wait("tensor", "ac", qh2_evac)
            if prev_epi_dv:
                S.wait("tensor", "dv", prev_epi_dv)
            mm_fin = None
            for ic in range(IC):
                sl = ic % 2
                if ternD_use[sl]:
                    S.wait("sync", "pe", ternD_use[sl])
                t_td = S.load(ternD[sl], tern_sc[ic * 128:(ic + 1) * 128, :], f"td{sl}")
                S.waitp("tensor", t_td)
                lastmm = None
                for tsub in range(2):
                    for ft in range(NFT):
                        bank = tsub * NFT + ft
                        lastmm = S.pe(lambda e, c=ic, ts=tsub, f=ft, b=bank, d=ternD[sl]:
                                      e.matmul(out=ps[b][:, 0:NF],
                                               lhsT=qh2F[:, ts * IC * 128 + c * 128: ts * IC * 128 + c * 128 + 128],
                                               rhs=d[:, f * NF:(f + 1) * NF],
                                               start=(c == 0), stop=(c == IC - 1),
                                               skip_group_check=True),
                                      inc=(tsub == 1 and ft == NFT - 1))
                ternD_use[sl] = lastmm
                mm_fin = lastmm
            S.wait("vector", "pe", mm_fin)
            if b6 == 0:
                S.wait("vector", "ac", t_gb)
            for sub in range(2):
                tt = 2 * b6 + sub
                if prev_epi_dv:
                    S.wait("sync", "dv", prev_epi_dv)
                t_x2 = S.load(x2l, x2_sc[tt * 128:(tt + 1) * 128, :], "x2l")
                S.waitp("vector", t_x2)
                if xout_st[0]:
                    S.waitp("vector", xout_st[0])
                S.v(lambda e, s2=sub, p=b6 % 2: e.tensor_tensor(
                    out=s_scld[s2], in0=s_invs2[p][s2], in1=s_hfd, op=ALU.mult))
                t_epi = None
                for ft in range(NFT):
                    bank = sub * NFT + ft
                    d_ap = xout[:, ft * NF:(ft + 1) * NF]
                    S.v(lambda e, d=d_ap, b=bank, s2=sub: e.tensor_scalar(
                        out=d, in0=ps[b][:, 0:NF], scalar1=s_scld[s2], scalar2=None,
                        op0=ALU.mult))
                    S.v(lambda e, d=d_ap, f=ft: e.tensor_tensor(
                        out=d, in0=d, in1=gbB[:, f * NF:(f + 1) * NF], op=ALU.mult))
                    t_epi = S.v(lambda e, d=d_ap, f=ft: e.tensor_tensor(
                        out=d, in0=d, in1=x2l[:, f * NF:(f + 1) * NF], op=ALU.add), inc=True)
                S.wait("gpsimd", "dv", t_epi)
                xout_st[0] = S.store(y_out[tt * 128:(tt + 1) * 128, :], xout, "yo0")
                prev_epi_dv = t_epi

        pending = None
        for b6 in range(NB6):
            p5_stage1(b6, 0)
            p5_stage2lad(b6, 0)
            p5_stage1(b6, 1)
            t_lad1 = p5_stage2lad(b6, 1)
            if pending is not None:
                emit_down(pending)
            emit_transposes(b6, t_lad1)
            pending = b6
        emit_down(pending)

        S.wait_total("sync", ["yo0", "yo1"])

        for nm in ("pe", "dv", "ac"):
            S.semh[nm] = es.enter_context(nc.semaphore(nm))
        for nm in sorted(S.sems):
            S.semh[nm] = es.enter_context(nc.semaphore(nm))

        block = es.enter_context(nc.Block())

        @block.sync
        def _(sync):
            for fn in S.ops["sync"]:
                fn(sync)

        @block.gpsimd
        def _(gpsimd):
            for fn in S.ops["gpsimd"]:
                fn(gpsimd)

        @block.tensor
        def _(tensor):
            for fn in S.ops["tensor"]:
                fn(tensor)

        @block.vector
        def _(vector):
            for fn in S.ops["vector"]:
                fn(vector)

        @block.scalar
        def _(scalar):
            for fn in S.ops["scalar"]:
                fn(scalar)

    return nc


def make_tables(T):
    D = 128
    j = np.arange(D // 2, dtype=np.float64)
    thetas = 10000.0 ** (-2.0 * j / D)
    m = np.arange(1, T + 1, dtype=np.float64)
    ang = np.concatenate([thetas, thetas])[:, None] * m[None, :]
    c = 128.0 ** (-0.25)
    cosv = np.cos(ang) * c
    sinv = np.sin(ang) * c
    sgn = np.ones((D, 1)); sgn[:D // 2] = -1.0
    sinv = sinv * sgn
    chi = cosv.astype(np.float32)
    clo = (cosv - chi.astype(np.float64)).astype(np.float32)
    shi = sinv.astype(np.float32)
    slo = (sinv - shi.astype(np.float64)).astype(np.float32)
    return chi, clo, shi, slo


def make_masks():
    i = np.arange(128)[:, None]
    jq = np.arange(512)[None, :]
    return np.concatenate([
        np.where(128 * rr + i <= jq, 0.0, NEGBIG).astype(np.float32)
        for rr in range(4)], axis=1)


def host_inputs(T, E, I, Wq, bq, Wk, bk, Wv, bv, Wo, bo, wg, wu, wd, lnw, wmod):
    EC = E // 128
    FT = E // 128
    NV = min(512, E); FSV = E // NV
    N2 = min(256, E); FS2 = E // N2
    JS = I // 512
    asc = np.ascontiguousarray
    f = np.float32
    d = {}
    d["wq_t"] = asc(np.asarray(Wq, f).reshape(FT, 128, EC, 128).transpose(0, 3, 2, 1).reshape(E, E))
    d["wk_t"] = asc(np.asarray(Wk, f).reshape(FT, 128, EC, 128).transpose(0, 3, 2, 1).reshape(E, E))
    d["wv_r"] = asc(np.asarray(Wv, f).T.reshape(EC, 128, FSV, NV).transpose(2, 1, 0, 3).reshape(FSV * 128, EC * NV))
    d["wo_r"] = asc(np.asarray(Wo, f).T.reshape(EC, 128, FS2, N2).transpose(2, 1, 0, 3).reshape(FS2 * 128, EC * N2))
    d["bq_c"] = asc(np.asarray(bq, f).reshape(EC, 128).T)
    d["bk_c"] = asc(np.asarray(bk, f).reshape(EC, 128).T)
    d["bv_row"] = asc(np.asarray(bv, f).reshape(1, E))
    d["bo_row"] = asc(np.asarray(bo, f).reshape(1, E))
    d["wmod"] = asc(np.asarray(wmod, f))
    d["wg_r"] = asc(np.asarray(wg, f).T.reshape(EC, 128, JS, 512).transpose(2, 1, 0, 3).reshape(JS * 128, EC * 512))
    d["wu_r"] = asc(np.asarray(wu, f).T.reshape(EC, 128, JS, 512).transpose(2, 1, 0, 3).reshape(JS * 128, EC * 512))
    d["wd_t"] = asc(np.asarray(wd, f).T)
    d["lnw_row"] = asc(np.asarray(lnw, f).reshape(1, I))
    chi, clo, shi, slo = make_tables(T)
    d["chi_in"] = chi; d["clo_in"] = clo; d["shi_in"] = shi; d["slo_in"] = slo
    d["mask_in"] = make_masks()
    d["id_in"] = np.eye(128, dtype=np.float32)
    return d


def kernel(L0_attn_x, L0_attn_Wq, L0_attn_bq, L0_attn_Wk, L0_attn_bk,
           L0_attn_Wv, L0_attn_bv, L0_attn_Wo, L0_attn_bo,
           L0_ffn_w_gate, L0_ffn_w_up, L0_ffn_w_down, L0_ffn_ln_weight,
           L0_c, L0_wmod):
    T, E, I = 2048, 2048, 8192
    xs = np.asarray(L0_attn_x, np.float32)
    cs = np.asarray(L0_c, np.float32)
    B = xs.shape[0]
    EC = E // 128
    nc = build_program(T, E, I)
    shared = host_inputs(T, E, I, L0_attn_Wq, L0_attn_bq, L0_attn_Wk, L0_attn_bk,
                         L0_attn_Wv, L0_attn_bv, L0_attn_Wo, L0_attn_bo,
                         L0_ffn_w_gate, L0_ffn_w_up, L0_ffn_w_down,
                         L0_ffn_ln_weight, L0_wmod)
    in_maps = []
    for b in range(B):
        m = dict(shared)
        m["x_in"] = np.ascontiguousarray(xs[b])
        m["c_col"] = np.ascontiguousarray(cs[b].reshape(EC, 128).T)
        in_maps.append(m)
    res = run_bass_kernel_spmd(nc, in_maps, core_ids=list(range(B)))
    out = np.stack([np.asarray(res.results[b]["y_out"]) for b in range(B)], axis=0)
    return out.astype(np.float32)


# version 12
# speedup vs baseline: 1.1743x; 1.0427x over previous
import numpy as np
from contextlib import ExitStack
import concourse.bass as bass
import concourse.mybir as mybir
from concourse.bass_utils import run_bass_kernel_spmd

F32 = mybir.dt.float32
BF16 = mybir.dt.bfloat16
AF = mybir.ActivationFunctionType
ALU = mybir.AluOpType
AX = mybir.AxisListType

EXP_BIAS = -16.0
NEGBIG = -1.0e38
ARENA = 53152


class Sched:
    ENGS = ("sync", "gpsimd", "tensor", "vector", "scalar")

    def __init__(self):
        self.ops = {e: [] for e in self.ENGS}
        self.cnt = {"pe": 0, "dv": 0, "ac": 0}
        self.sems = {}
        self.semh = {}
        self.ps6 = None

    def _add(self, eng, fn):
        self.ops[eng].append(fn)

    def op(self, eng, build, sem=None, inc=0):
        tgt = None
        if sem is not None:
            self.cnt[sem] += inc
            tgt = self.cnt[sem]

        def fn(e, b=build, s=sem, i=inc, self_=self):
            inst = b(e)
            if s is not None:
                inst.then_inc(self_.semh[s], i)
        self._add(eng, fn)
        return tgt

    def wait(self, eng, sem, val):
        if not val:
            return
        def fn(e, s=sem, v=val, self_=self):
            e.wait_ge(self_.semh[s], v)
        self._add(eng, fn)

    def waitp(self, eng, pair):
        if pair:
            self.wait(eng, pair[0], pair[1])

    def wait_total(self, eng, names):
        for n in names:
            if self.sems.get(n):
                self.wait(eng, n, self.sems[n])

    def v(self, build, inc=False):
        self._add("vector", lambda e: e.drain())
        return self.op("vector", build, "dv" if inc else None, 1 if inc else 0)

    def s(self, build, inc=False):
        self._add("scalar", lambda e: e.drain())
        return self.op("scalar", build, "ac" if inc else None, 1 if inc else 0)

    def pe(self, build, inc=False):
        return self.op("tensor", build, "pe" if inc else None, 1 if inc else 0)

    def _dma(self, eng, out, in_, sem):
        self.sems[sem] = self.sems.get(sem, 0) + 16
        tgt = self.sems[sem]

        def fn(e, o=out, i=in_, s=sem, self_=self):
            e.dma_start(out=o, in_=i).then_inc(self_.semh[s], 16)
        self._add(eng, fn)
        return (sem, tgt)

    def load(self, out, in_, sem):
        return self._dma("sync", out, in_, sem)

    def store(self, out, in_, sem):
        return self._dma("gpsimd", out, in_, sem)


def build_program(T, E, I):
    H = E // 128
    EC = E // 128
    TC = T // 128
    QT = T // 512
    NV = min(512, E); FSV = E // NV
    N2 = min(256, E); FS2 = E // N2
    JS = I // 512
    IC = I // 128
    NB6 = T // 256
    NF = min(512, E); NFT = E // NF
    STRIP_E = min(1024, E)
    STRIP_I = min(1024, I)

    nc = bass.Bass("TRN2", target_bir_lowering=False)
    dt = lambda n, s, d=F32, k="ExternalInput": nc.dram_tensor(n, s, d, kind=k).ap()

    x_in = dt("x_in", [T, E])
    wq_t = dt("wq_t", [E, E]); wk_t = dt("wk_t", [E, E])
    wv_r = dt("wv_r", [FSV * 128, EC * NV])
    wo_r = dt("wo_r", [FS2 * 128, EC * N2])
    bq_c = dt("bq_c", [128, EC]); bk_c = dt("bk_c", [128, EC])
    bv_row = dt("bv_row", [1, E]); bo_row = dt("bo_row", [1, E])
    wmod = dt("wmod", [E, 6 * E])
    c_col = dt("c_col", [128, EC])
    wg_r = dt("wg_r", [JS * 128, EC * 512])
    wu_r = dt("wu_r", [JS * 128, EC * 512])
    wd_t = dt("wd_t", [I, E])
    lnw_row = dt("lnw_row", [1, I])
    chi_in = dt("chi_in", [128, T]); clo_in = dt("clo_in", [128, T])
    shi_in = dt("shi_in", [128, T]); slo_in = dt("slo_in", [128, T])
    mask_in = dt("mask_in", [128, 4 * 512])
    id_in = dt("id_in", [128, 128])

    q_sc = dt("q_sc", [H * 128, T], F32, "Internal")
    k_sc = dt("k_sc", [H * 128, T], F32, "Internal")
    v_sc = dt("v_sc", [H * T, 128], F32, "Internal")
    x2_sc = dt("x2_sc", [T, E], F32, "Internal")
    h_sc = dt("h_sc", [T, I], F32, "Internal")
    tern_sc = dt("tern_sc", [I, E], BF16, "Internal")
    mod_sc = dt("mod_sc", [1, 6 * E], F32, "Internal")
    y_out = dt("y_out", [T, E], F32, "ExternalOutput")

    S = Sched()
    es = ExitStack()
    with es:
        AR = es.enter_context(nc.sbuf_tensor("arena", [128, ARENA], F32))
        ARB = AR.bitcast(BF16)

        def f32(off, n):
            assert off + n <= ARENA, ("f32 overflow", off, n)
            return AR[:, off:off + n]

        def b16(off_f32, n_b16):
            assert 2 * off_f32 + n_b16 <= 2 * ARENA, ("b16 overflow", off_f32, n_b16)
            return ARB[:, 2 * off_f32:2 * off_f32 + n_b16]

        class Bump:
            def __init__(self, start):
                self.off = start
            def f(self, n):
                o = self.off; self.off += n
                assert self.off <= ARENA, ("bump overflow", self.off)
                return f32(o, n)
            def b(self, n_b16):
                n = (n_b16 + 1) // 2
                o = self.off; self.off += n
                assert self.off <= ARENA, ("bump overflow b16", self.off)
                return b16(o, n_b16)

        st_ = Bump(0)
        idf = st_.f(128)
        idb = st_.b(128)
        rowst_full = st_.f(512); rowst = rowst_full[0:1, :]
        rowst2 = st_.f(512)[0:1, :]
        rowst3 = st_.f(512)[0:1, :]
        ones_blk = st_.f(128)
        ones_col = ones_blk[:, 0:1]
        ones_row = ones_blk[0:1, :]
        scal = st_.f(64)
        sm = lambda i: scal[:, i:i + 1]
        (s_s1, s_ss, s_m, s_mm2, s_var, s_rstd, s_amax, s_sx, s_zr, s_swg,
         s_hfg, s_swu, s_hfu, s_swd, s_hfd, s_ms, s_rstd2, s_rm) = [sm(i) for i in range(18)]
        s_invs2 = [[sm(18), sm(19)], [sm(28), sm(29)]]
        s_scld = [sm(20), sm(21)]
        s_t11 = sm(22)
        c_eps5, c_eps6, c_mhalf, c_phalf, c_expb = sm(23), sm(24), sm(25), sm(26), sm(27)
        sp_s1 = [sm(30), sm(31)]; sp_ss = [sm(32), sm(33)]
        sp_m = [sm(34), sm(35)]; sp_mm2 = [sm(36), sm(37)]
        sp_var = [sm(38), sm(39)]; sp_rstd = [sm(40), sm(41)]
        sp_amax = [sm(42), sm(43)]; sp_sx = [sm(44), sm(45)]
        sp_ms = [sm(46), sm(47)]; sp_rstd2 = [sm(48), sm(49)]
        sp_rm = [sm(50), sm(51)]; sp_sx2 = [sm(52), sm(53)]
        c_m128 = sm(54)
        bq_sb = st_.f(EC); bk_sb = st_.f(EC)
        cc_sb = st_.f(EC); scc_sb = st_.f(EC)
        invsx = st_.f(TC); sclg = st_.f(TC); sclu = st_.f(TC)
        parts = [st_.f(128) for _ in range(3)]
        PH = st_.off

        ps = [es.enter_context(nc.psum_tensor(f"ps{i}", [128, 512], F32)) for i in range(8)]
        psb = ps[7].bitcast(BF16)


        def ladder_sd(src_ap, scale_ap, dst_ap, width, ytiles, rtiles, stiles, sw):
            strips = [(o, min(sw, width - o)) for o in range(0, width, sw)]
            NG = len(ytiles)
            t_last = None
            for g0 in range(0, len(strips), NG):
                grp = strips[g0:g0 + NG]
                ydv = []
                for i, (o, w) in enumerate(grp):
                    ydv.append(S.v(lambda e, d=ytiles[i][:, 0:w], s0=src_ap[:, o:o + w],
                                   sc=scale_ap: e.tensor_scalar(
                        out=d, in0=s0, scalar1=sc, scalar2=128.5,
                        op0=ALU.mult, op1=ALU.add), inc=True))
                rdv = list(ydv)
                rsrc = [ytiles[i] for i in range(len(grp))]
                for li, t2 in enumerate((128.0, 64.0, 32.0, 16.0, 8.0, 4.0, 2.0, 1.0)):
                    sac = []
                    for i, (o, w) in enumerate(grp):
                        S.wait("scalar", "dv", rdv[i])
                        sac.append(S.s(lambda e, d=stiles[i][:, 0:w],
                                       s0=rsrc[i][:, 0:w]: e.activation(
                            out=d, in_=s0, func=AF.Sign, bias=c_m128), inc=True))
                    for i, (o, w) in enumerate(grp):
                        S.wait("vector", "ac", sac[i])
                        rdv[i] = S.v(lambda e, d=rtiles[i][:, 0:w], s0=stiles[i][:, 0:w],
                                     s1=rsrc[i][:, 0:w], tv=t2 / 2: e.scalar_tensor_tensor(
                            out=d, in0=s0, scalar=-tv, in1=s1,
                            op0=ALU.mult, op1=ALU.add), inc=True)
                    rsrc = [rtiles[i] for i in range(len(grp))]
                for i, (o, w) in enumerate(grp):
                    t_last = S.v(lambda e, d=dst_ap[:, o:o + w], s0=ytiles[i][:, 0:w],
                                 s1=rtiles[i][:, 0:w]: e.scalar_tensor_tensor(
                        out=d, in0=s0, scalar=-0.5, in1=s1,
                        op0=ALU.add, op1=ALU.subtract), inc=True)
            return t_last

        def fence_sync():
            S.wait("sync", "pe", S.cnt["pe"])
            S.wait("sync", "dv", S.cnt["dv"])
            S.wait("sync", "ac", S.cnt["ac"])
            S.wait_total("sync", sorted(S.sems))

        def bcast(dst, src_row_ap_fn, width, transform, st_guard=None):
            last = None
            seg = min(512, width)
            rw = rowst_full[0:1, 0:seg]
            rw2 = rowst2[0:1, 0:seg]
            rw3 = rowst3[0:1, 0:seg]
            for sl0 in range(width // seg):
                if st_guard:
                    S.wait_total("sync", st_guard)
                t_l = S.load(rw, src_row_ap_fn(sl0 * seg, seg), "row")
                t_r = None
                if transform == "add1":
                    S.waitp("vector", t_l)
                    t_r = S.v(lambda e: e.tensor_scalar(
                        out=rw, in0=rw, scalar1=1.0, scalar2=None, op0=ALU.add), inc=True)
                elif transform == "signclamp":
                    S.waitp("vector", t_l)
                    S.v(lambda e: e.tensor_scalar(
                        out=rw2, in0=rw, scalar1=0.0, scalar2=None, op0=ALU.is_ge))
                    S.v(lambda e: e.tensor_scalar(
                        out=rw2, in0=rw2, scalar1=2.0, scalar2=-1.0,
                        op0=ALU.mult, op1=ALU.add))
                    S.v(lambda e: e.tensor_scalar(
                        out=rw3, in0=rw, scalar1=-1.0, scalar2=None, op0=ALU.mult))
                    S.v(lambda e: e.tensor_tensor(out=rw, in0=rw, in1=rw3, op=ALU.max))
                    S.v(lambda e: e.tensor_scalar(
                        out=rw, in0=rw, scalar1=0.25, scalar2=None, op0=ALU.max))
                    t_r = S.v(lambda e: e.tensor_tensor(
                        out=rw, in0=rw, in1=rw2, op=ALU.mult), inc=True)
                if t_r:
                    S.wait("tensor", "dv", t_r)
                else:
                    S.waitp("tensor", t_l)
                S.waitp("tensor", S.ps6)
                t_mm = S.pe(lambda e: e.matmul(out=ps[6][:, 0:seg], lhsT=ones_row,
                                               rhs=rw, start=True, stop=True), inc=True)
                S.wait("scalar", "pe", t_mm)
                t_e = S.s(lambda e, d=dst[:, sl0 * seg:(sl0 + 1) * seg]: e.activation(
                    out=d, in_=ps[6][:, 0:seg], func=AF.Copy), inc=True)
                S.ps6 = ("ac", t_e)
                S.wait("sync", "pe", t_mm)
                last = t_e
            return last

        S.load(idf, id_in[:, :], "ini")
        S.load(bq_sb, bq_c[:, :], "ini"); S.load(bk_sb, bk_c[:, :], "ini")
        t_cc = S.load(cc_sb, c_col[:, :], "ini")
        S.waitp("vector", t_cc)
        S.v(lambda e: e.memset(ones_blk, 1.0))
        S.v(lambda e: e.memset(c_eps5, 1e-5))
        S.v(lambda e: e.memset(c_eps6, 1e-6))
        S.v(lambda e: e.memset(c_mhalf, -0.5))
        S.v(lambda e: e.memset(c_phalf, 0.5))
        S.v(lambda e: e.memset(c_expb, EXP_BIAS))
        S.v(lambda e: e.memset(c_m128, -128.0))
        S.v(lambda e: e.tensor_copy(out=idb, in_=idf), inc=True)
        S.waitp("scalar", t_cc)
        t_silu = S.s(lambda e: e.activation(out=scc_sb, in_=cc_sb, func=AF.Silu), inc=True)

        p0 = Bump(PH)
        wch = [p0.f(6 * E), p0.f(6 * E)]
        acc = p0.f(6 * E)
        modst = [f32(PH, 512)[0:1, :], f32(PH + 512, 512)[0:1, :]]
        use_t = [None, None]
        for ec in range(EC):
            sl = ec % 2
            if use_t[sl]:
                S.wait("sync", "dv", use_t[sl])
            t_w = S.load(wch[sl], wmod[ec * 128:(ec + 1) * 128, :], f"wch{sl}")
            S.waitp("vector", t_w)
            if ec == 0:
                S.wait("vector", "ac", t_silu)
                use_t[sl] = S.v(lambda e, w=wch[sl]: e.tensor_scalar(
                    out=acc, in0=w, scalar1=scc_sb[:, 0:1], scalar2=None,
                    op0=ALU.mult), inc=True)
            else:
                use_t[sl] = S.v(lambda e, w=wch[sl], c=ec: e.scalar_tensor_tensor(
                    out=acc, in0=w, scalar=scc_sb[:, c:c + 1], in1=acc,
                    op0=ALU.mult, op1=ALU.add), inc=True)
        t_acc = use_t[(EC - 1) % 2]
        S.wait("tensor", "dv", t_acc)
        mst_use = [None, None]
        t_cp = None
        for nt in range(6 * E // 512):
            sl = nt % 2
            if t_cp:
                S.wait("tensor", "dv", t_cp)
            t_mm = S.pe(lambda e, n=nt: e.matmul(
                out=ps[0][:, 0:512], lhsT=ones_blk,
                rhs=acc[:, n * 512:(n + 1) * 512], start=True, stop=True), inc=True)
            if mst_use[sl]:
                S.waitp("vector", mst_use[sl])
            S.wait("vector", "pe", t_mm)
            t_cp = S.v(lambda e, d=modst[sl]: e.tensor_copy(
                out=d, in_=ps[0][0:1, 0:512]), inc=True)
            S.wait("gpsimd", "dv", t_cp)
            mst_use[sl] = S.store(mod_sc[0:1, nt * 512:(nt + 1) * 512], modst[sl], f"mo{sl}")
        mod_st_done = ["mo0", "mo1"]

        def modrow(g):
            return lambda o, n, gg=g: mod_sc[0:1, gg * E + o: gg * E + o + n]

        p0b = Bump(PH)
        hF = p0b.f(EC * T)
        O_AFTER_HF = p0b.off
        bc0 = p0b.f(E); bc1 = p0b.f(E)
        xt = [p0b.f(E), p0b.f(E)]
        h1t = p0b.f(E)
        t_b0 = bcast(bc0, modrow(1), E, "add1", st_guard=mod_st_done)
        t_b1 = bcast(bc1, modrow(0), E, None, st_guard=mod_st_done)
        xt_use = [None, None]
        tr_use = [None, None]
        last_hF_evac = None
        for tt in range(TC):
            sl = tt % 2
            if xt_use[sl]:
                S.wait("sync", "dv", xt_use[sl])
            t_x = S.load(xt[sl], x_in[tt * 128:(tt + 1) * 128, :], f"xt{sl}")
            S.waitp("vector", t_x)
            S.v(lambda e, x=xt[sl]: e.reduce_sum(out=s_s1, in_=x, axis=AX.X))
            S.waitp("scalar", t_x)
            t_sq = S.s(lambda e, x=xt[sl]: e.activation(
                out=h1t, in_=x, func=AF.Square, accum_out=s_ss), inc=True)
            S.v(lambda e: e.tensor_scalar(out=s_m, in0=s_s1, scalar1=1.0 / E,
                                          scalar2=None, op0=ALU.mult))
            S.v(lambda e: e.tensor_tensor(out=s_mm2, in0=s_m, in1=s_m, op=ALU.mult))
            S.wait("vector", "ac", t_sq)
            t_var = S.v(lambda e: e.scalar_tensor_tensor(
                out=s_var, in0=s_ss, scalar=1.0 / E, in1=s_mm2,
                op0=ALU.mult, op1=ALU.subtract), inc=True)
            S.wait("scalar", "dv", t_var)
            t_rstd = S.s(lambda e: e.activation(
                out=s_rstd, in_=s_var, func=AF.Sqrt, bias=c_eps5), inc=True)
            S.wait("vector", "ac", t_rstd)
            S.v(lambda e: e.reciprocal(out=s_rstd, in_=s_rstd))
            if tt == 0:
                S.wait("vector", "ac", t_b0)
                S.wait("vector", "ac", t_b1)
            xt_use[sl] = S.v(lambda e, x=xt[sl]: e.tensor_scalar(
                out=h1t, in0=x, scalar1=s_m, scalar2=s_rstd,
                op0=ALU.subtract, op1=ALU.mult), inc=True)
            S.v(lambda e: e.tensor_tensor(out=h1t, in0=h1t, in1=bc0, op=ALU.mult))
            t_h1 = S.v(lambda e: e.tensor_tensor(out=h1t, in0=h1t, in1=bc1, op=ALU.add), inc=True)
            S.wait("tensor", "dv", t_h1)
            for ec in range(EC):
                bk = ec % 2
                if tr_use[bk]:
                    S.wait("tensor", "ac", tr_use[bk])
                t_tr = S.pe(lambda e, c=ec, b=bk: e.transpose(
                    out=ps[1 + b][:, 0:128], in_=h1t[:, c * 128:(c + 1) * 128],
                    identity=idf), inc=True)
                S.wait("scalar", "pe", t_tr)
                tr_use[bk] = S.s(lambda e, c=ec, b=bk, t=tt: e.activation(
                    out=hF[:, c * T + t * 128: c * T + t * 128 + 128],
                    in_=ps[1 + b][:, 0:128], func=AF.Copy), inc=True)
            last_hF_evac = tr_use[(EC - 1) % 2]

        p1 = Bump(O_AFTER_HF)
        chi = p1.f(T); clo = p1.f(T); shi = p1.f(T); slo = p1.f(T)
        wt = [p1.f(E), p1.f(E)]
        qkst = [p1.f(512), p1.f(512)]
        swpst = [p1.f(512), p1.f(512)]
        rs1 = [p1.f(512), p1.f(512)]
        rs2 = p1.f(512)
        fence_sync()
        S.load(chi, chi_in[:, :], "tab"); S.load(clo, clo_in[:, :], "tab")
        S.load(shi, shi_in[:, :], "tab")
        t_tab = S.load(slo, slo_in[:, :], "tab")
        S.waitp("vector", t_tab)
        S.wait("tensor", "ac", last_hF_evac)
        wt_use = [None, None]
        st_use = [None, None]
        sw_use = [None, None]
        rs_store = [None, None]
        psA_use = [None, None]
        for (wdram, bias_sb, dst_sc) in ((wq_t, bq_sb, q_sc), (wk_t, bk_sb, k_sc)):
            for ft in range(EC):
                sl = ft % 2
                if wt_use[sl]:
                    S.wait("sync", "pe", wt_use[sl])
                t_w = S.load(wt[sl], wdram[ft * 128:(ft + 1) * 128, :], f"wt{sl}")
                S.waitp("tensor", t_w)
                for tq in range(QT):
                    bk = tq % 2
                    pA = ps[0] if bk == 0 else ps[3]
                    S.waitp("tensor", psA_use[bk])
                    for ec in range(EC):
                        t_mm = S.pe(lambda e, c=ec, q=tq, w=wt[sl], p=pA: e.matmul(
                            out=p[:, 0:512], lhsT=w[:, c * 128:(c + 1) * 128],
                            rhs=hF[:, c * T + q * 512: c * T + (q + 1) * 512],
                            start=(c == 0), stop=(c == EC - 1),
                            skip_group_check=True), inc=(ec == EC - 1))
                    wt_use[sl] = t_mm
                    ssl = (ft * QT + tq) % 2
                    if st_use[ssl]:
                        S.wait("scalar", "dv", st_use[ssl])
                    S.wait("scalar", "pe", t_mm)
                    t_ev = S.s(lambda e, f=ft, d=qkst[ssl], p=pA, b=bias_sb: e.activation(
                        out=d, in_=p[:, 0:512], func=AF.Identity, bias=b[:, f:f + 1]), inc=True)
                    psA_use[bk] = ("ac", t_ev)
                    if sw_use[ssl]:
                        S.wait("gpsimd", "dv", sw_use[ssl])
                    S.wait("gpsimd", "ac", t_ev)
                    S.store(swpst[ssl][0:64, :], qkst[ssl][64:128, :], f"sw{ssl}")
                    t_sw = S.store(swpst[ssl][64:128, :], qkst[ssl][0:64, :], f"sw{ssl}")
                    if rs_store[ssl]:
                        S.waitp("vector", rs_store[ssl])
                    S.waitp("vector", t_sw)
                    c0, c1 = tq * 512, (tq + 1) * 512
                    r1 = rs1[ssl]
                    S.v(lambda e, s=qkst[ssl], a=c0, b2=c1, r=r1: e.tensor_tensor(
                        out=r, in0=s, in1=chi[:, a:b2], op=ALU.mult))
                    S.v(lambda e, s=qkst[ssl], a=c0, b2=c1: e.tensor_tensor(
                        out=rs2, in0=s, in1=clo[:, a:b2], op=ALU.mult))
                    st_use[ssl] = S.v(lambda e, r=r1: e.tensor_tensor(
                        out=r, in0=r, in1=rs2, op=ALU.add), inc=True)
                    S.v(lambda e, s=swpst[ssl], a=c0, b2=c1: e.tensor_tensor(
                        out=rs2, in0=s, in1=shi[:, a:b2], op=ALU.mult))
                    S.v(lambda e, r=r1: e.tensor_tensor(out=r, in0=r, in1=rs2, op=ALU.add))
                    sw_use[ssl] = S.v(lambda e, s=swpst[ssl], a=c0, b2=c1: e.tensor_tensor(
                        out=rs2, in0=s, in1=slo[:, a:b2], op=ALU.mult), inc=True)
                    t_rot = S.v(lambda e, r=r1: e.tensor_tensor(
                        out=r, in0=r, in1=rs2, op=ALU.add), inc=True)
                    S.wait("gpsimd", "dv", t_rot)
                    rs_store[ssl] = S.store(
                        dst_sc[ft * 128:(ft + 1) * 128, tq * 512:(tq + 1) * 512], r1, f"rq{ssl}")

        pv = Bump(O_AFTER_HF)
        vsl = pv.f(EC * NV)
        vst = [pv.f(NV), pv.f(NV)]
        bvB = pv.f(E)
        fence_sync()
        t_bv = bcast(bvB, lambda o, n: bv_row[0:1, o:o + n], E, None)
        vst_use = [None, None]
        last_vmm = None
        for fs in range(FSV):
            if last_vmm:
                S.wait("sync", "pe", last_vmm)
            t_v = S.load(vsl, wv_r[fs * 128:(fs + 1) * 128, :], "vsl")
            S.waitp("tensor", t_v)
            for tt in range(TC):
                bk = tt % 2
                pA = ps[0] if bk == 0 else ps[3]
                S.waitp("tensor", psA_use[bk])
                for ec in range(EC):
                    t_mm = S.pe(lambda e, c=ec, t=tt, p=pA: e.matmul(
                        out=p[:, 0:NV], lhsT=hF[:, c * T + t * 128: c * T + t * 128 + 128],
                        rhs=vsl[:, c * NV:(c + 1) * NV], start=(c == 0), stop=(c == EC - 1),
                        skip_group_check=True), inc=(ec == EC - 1))
                last_vmm = t_mm
                ssl = tt % 2
                if vst_use[ssl]:
                    S.waitp("vector", vst_use[ssl])
                S.wait("vector", "pe", t_mm)
                if fs == 0 and tt == 0:
                    S.wait("vector", "ac", t_bv)
                t_vv = S.v(lambda e, d=vst[ssl], p=pA, f=fs: e.tensor_tensor(
                    out=d, in0=p[:, 0:NV], in1=bvB[:, f * NV:(f + 1) * NV],
                    op=ALU.add), inc=True)
                psA_use[bk] = ("dv", t_vv)
                S.wait("gpsimd", "dv", t_vv)
                for hh in range(NV // 128):
                    h = fs * (NV // 128) + hh
                    vst_use[ssl] = S.store(
                        v_sc[h * T + tt * 128: h * T + (tt + 1) * 128, :],
                        vst[ssl][:, hh * 128:(hh + 1) * 128], f"vs{ssl}")
        spill_done = ["rq0", "rq1", "vs0", "vs1"]

        p2 = Bump(O_AFTER_HF)
        qrot = [p2.f(T), p2.f(T)]
        krot = [p2.f(T), p2.f(T)]
        VRW = TC * 129
        vrhs = [p2.f(VRW + 7), p2.f(VRW + 7)]
        pT = [p2.f(512) for _ in range(6)]
        yst = [p2.f(128), p2.f(128), p2.f(128), p2.f(128)]
        masks_t = p2.f(4 * 512)
        masks = [masks_t[:, i * 512:(i + 1) * 512] for i in range(4)]
        fence_sync()
        t_mload = S.load(masks_t, mask_in[:, :], "msk")
        t_ones = None
        for vb in range(2):
            for c in range(TC):
                t_ones = S.v(lambda e, b=vb, cc=c: e.memset(
                    vrhs[b][:, cc * 129 + 128: cc * 129 + 129], 1.0), inc=True)
        S.wait_total("sync", spill_done)
        S.wait("sync", "dv", t_ones)

        head_use_pe = [None, None]
        ptslot_use = [None] * 6
        yst_use = [None, None, None, None]
        ptr_use = None
        av_dv = [None] * 4
        first_mask_wait = True
        yF_done = None
        for h in range(H):
            hb = h % 2
            if head_use_pe[hb]:
                S.wait("sync", "pe", head_use_pe[hb])
            S.load(qrot[hb], q_sc[h * 128:(h + 1) * 128, :], f"hd{hb}")
            S.load(krot[hb], k_sc[h * 128:(h + 1) * 128, :], f"hd{hb}")
            for c in range(TC):
                t_v = S.load(vrhs[hb][:, c * 129:c * 129 + 128],
                             v_sc[h * T + c * 128: h * T + (c + 1) * 128, :], f"hd{hb}")
            S.waitp("tensor", t_v)
            exp_done = {}
            avlast = {}
            SBANKS = (0, 1, 7)

            def emit_scores(j, batch):
                for c in batch:
                    sbk = c % 3
                    pS = ps[SBANKS[sbk]]
                    if (c - 3) in exp_done:
                        S.wait("tensor", "ac", exp_done[c - 3])
                    spt = c % 6
                    t_sc2 = S.pe(lambda e, cc=c, jj=j, p=pS, b=hb: e.matmul(
                        out=p[:, 0:512], lhsT=krot[b][:, cc * 128:(cc + 1) * 128],
                        rhs=qrot[b][:, jj * 512:(jj + 1) * 512],
                        start=True, stop=True, skip_group_check=True), inc=True)
                    diag = c >= 4 * j
                    if diag:
                        nonlocal first_mask_wait
                        if first_mask_wait:
                            S.waitp("vector", t_mload)
                            first_mask_wait = False
                        S.wait("vector", "pe", t_sc2)
                        t_mk = S.v(lambda e, p=pS, r=c - 4 * j: e.tensor_tensor(
                            out=p[:, 0:512], in0=p[:, 0:512], in1=masks[r],
                            op=ALU.add), inc=True)
                        S.wait("scalar", "dv", t_mk)
                    else:
                        S.wait("scalar", "pe", t_sc2)
                    if ptslot_use[spt]:
                        S.wait("scalar", "pe", ptslot_use[spt])
                    t_exp = S.s(lambda e, p=pS, d=pT[spt]: e.activation(
                        out=d, in_=p[:, 0:512], func=AF.Exp, bias=c_expb), inc=True)
                    exp_done[c] = t_exp

            def emit_avs(j, batch):
                S.wait("tensor", "ac", exp_done[batch[-1]])
                for c in batch:
                    spt = c % 6
                    last = None
                    for qs in range(4):
                        qq = 4 * j + qs
                        if c > qq:
                            continue
                        if c == 0 and av_dv[qs] is not None:
                            S.wait("tensor", "dv", av_dv[qs])
                        last = S.pe(lambda e, cc=c, q=qs, b=hb, qqv=qq, sb2=spt: e.matmul(
                            out=ps[2 + q][:, 0:129], lhsT=pT[sb2][:, q * 128:(q + 1) * 128],
                            rhs=vrhs[b][:, cc * 129: cc * 129 + 129],
                            start=(cc == 0), stop=(cc == qqv),
                            skip_group_check=True), inc=True)
                    avlast[c] = last
                    ptslot_use[spt] = last

            for j in range(QT):
                nact = min(TC, 4 * (j + 1))
                batches = [list(range(c0, min(c0 + 3, nact)))
                           for c0 in range(0, nact, 3)]
                pend = None
                for batch in batches:
                    emit_scores(j, batch)
                    if pend is not None:
                        emit_avs(j, pend)
                    pend = batch
                emit_avs(j, pend)
                t_trl = None
                for qs in range(4):
                    S.wait("vector", "pe", avlast[4 * j + qs])
                    if yst_use[qs]:
                        S.wait("vector", "pe", yst_use[qs])
                    S.v(lambda e, q=qs: e.reciprocal(out=s_zr, in_=ps[2 + q][:, 128:129]))
                    av_dv[qs] = S.v(lambda e, q=qs, d=yst[qs]: e.tensor_scalar(
                        out=d, in0=ps[2 + q][:, 0:128], scalar1=s_zr, scalar2=None,
                        op0=ALU.mult), inc=True)
                    S.wait("tensor", "dv", av_dv[qs])
                    if qs == 0:
                        if ptr_use:
                            S.wait("tensor", "ac", ptr_use)
                        S.waitp("tensor", S.ps6)
                    t_trl = S.pe(lambda e, d=yst[qs], q=qs: e.transpose(
                        out=ps[6][:, q * 128:(q + 1) * 128], in_=d, identity=idf), inc=True)
                    yst_use[qs] = t_trl
                S.wait("scalar", "pe", t_trl)
                ptr_use = S.s(lambda e, hh2=h, jj2=j: e.activation(
                    out=hF[:, hh2 * T + jj2 * 512: hh2 * T + jj2 * 512 + 512],
                    in_=ps[6][:, 0:512], func=AF.Copy), inc=True)
                S.ps6 = ("ac", ptr_use)
            head_use_pe[hb] = avlast[nact - 1]
        yF_done = ptr_use

        p3 = Bump(O_AFTER_HF)
        bcA = p3.f(E); bcB = p3.f(E)
        wt2 = [p3.f(EC * N2), p3.f(EC * N2)]
        xinst = [p3.f(N2), p3.f(N2)]
        x2st = [p3.f(N2), p3.f(N2)]
        stst = [p3.f(1024), p3.f(1024)]
        stdmp = p3.f(1024)
        S.wait("tensor", "ac", yF_done)
        fence_sync()
        t_bo = bcast(bcA, lambda o, n: bo_row[0:1, o:o + n], E, None)
        t_ga = bcast(bcB, modrow(2), E, "signclamp", st_guard=mod_st_done)

        stats_jobs = []
        nch_m = [0, 0, 0]
        for mi, (wdram, rows, cols) in enumerate(((wg_r, JS * 128, EC * 512),
                                                  (wu_r, JS * 128, EC * 512),
                                                  (wd_t, I, E))):
            ccw = min(1024, cols)
            cpr = cols // ccw
            nch_m[mi] = (rows // 128) * cpr
            for ch in range(nch_m[mi]):
                stats_jobs.append((mi, wdram, (ch // cpr) * 128, (ch % cpr) * ccw, ch, ccw))
        st_slot_use = [None, None]
        nstats = len(stats_jobs)

        def emit_stats(idx):
            mi, wdram, r0, c0, ch, ccw = stats_jobs[idx]
            sl = idx % 2
            if st_slot_use[sl]:
                S.wait("sync", "ac", st_slot_use[sl])
            t_l = S.load(stst[sl][:, 0:ccw], wdram[r0:r0 + 128, c0:c0 + ccw], f"ss{sl}")
            S.waitp("scalar", t_l)
            st_slot_use[sl] = S.s(lambda e, s=stst[sl][:, 0:ccw], m=mi, p=ch, w=ccw: e.activation(
                out=stdmp[:, 0:w], in_=s, func=AF.Abs, accum_out=parts[m][:, p:p + 1]), inc=True)
            return st_slot_use[sl]

        wt2_use = [None, None]
        x2_dv = [None, None]
        x2st_st = [None, None]
        si = 0
        njobs_per = max(1, (nstats + FS2 * TC - 1) // (FS2 * TC))
        last_stats = None
        for fs in range(FS2):
            sl = fs % 2
            if wt2_use[sl]:
                S.wait("sync", "pe", wt2_use[sl])
            t_w = S.load(wt2[sl], wo_r[fs * 128:(fs + 1) * 128, :], f"wo{sl}")
            S.waitp("tensor", t_w)
            for tt in range(TC):
                bk = tt % 2
                pA = ps[0] if bk == 0 else ps[1]
                if x2_dv[bk]:
                    S.wait("tensor", "dv", x2_dv[bk])
                for ec in range(EC):
                    t_mm = S.pe(lambda e, c=ec, t=tt, p=pA, w=wt2[sl]: e.matmul(
                        out=p[:, 0:N2], lhsT=hF[:, c * T + t * 128: c * T + t * 128 + 128],
                        rhs=w[:, c * N2:(c + 1) * N2], start=(c == 0), stop=(c == EC - 1),
                        skip_group_check=True), inc=(ec == EC - 1))
                wt2_use[sl] = t_mm
                ssl = tt % 2
                if x2_dv[bk]:
                    S.wait("sync", "dv", x2_dv[bk])
                t_xl = S.load(xinst[ssl], x_in[tt * 128:(tt + 1) * 128, fs * N2:(fs + 1) * N2], f"xi{ssl}")
                if x2st_st[ssl]:
                    S.waitp("vector", x2st_st[ssl])
                S.wait("vector", "pe", t_mm)
                S.waitp("vector", t_xl)
                if fs == 0 and tt == 0:
                    S.wait("vector", "ac", t_bo)
                    S.wait("vector", "ac", t_ga)
                S.v(lambda e, p=pA, d=x2st[ssl], f=fs: e.tensor_tensor(
                    out=d, in0=p[:, 0:N2], in1=bcA[:, f * N2:(f + 1) * N2], op=ALU.add))
                S.v(lambda e, d=x2st[ssl], f=fs: e.tensor_tensor(
                    out=d, in0=d, in1=bcB[:, f * N2:(f + 1) * N2], op=ALU.mult))
                x2_dv[bk] = S.v(lambda e, d=x2st[ssl], x=xinst[ssl]: e.tensor_tensor(
                    out=d, in0=d, in1=x, op=ALU.add), inc=True)
                S.wait("gpsimd", "dv", x2_dv[bk])
                x2st_st[ssl] = S.store(
                    x2_sc[tt * 128:(tt + 1) * 128, fs * N2:(fs + 1) * N2], x2st[ssl], f"xs{ssl}")
                for _ in range(njobs_per):
                    if si < nstats:
                        last_stats = emit_stats(si); si += 1
        while si < nstats:
            last_stats = emit_stats(si); si += 1
        x2_done = ["xs0", "xs1"]

        S.wait("vector", "ac", last_stats)
        t_rc = None
        for mi, (s_sw, s_hf) in enumerate(((s_swg, s_hfg), (s_swu, s_hfu), (s_swd, s_hfd))):
            t_cs = S.v(lambda e, m=mi: e.reduce_sum(
                out=s_zr, in_=parts[m][:, 0:nch_m[m]], axis=AX.X), inc=True)
            S.wait("tensor", "dv", t_cs)
            S.waitp("tensor", S.ps6)
            t_mm = S.pe(lambda e: e.matmul(out=ps[6][:, 0:1], lhsT=ones_blk,
                                           rhs=s_zr, start=True, stop=True), inc=True)
            S.wait("vector", "pe", t_mm)
            S.v(lambda e, d=s_sw: e.tensor_scalar(
                out=d, in0=ps[6][:, 0:1], scalar1=1.0 / (I * E),
                scalar2=None, op0=ALU.mult))
            t_mn = S.v(lambda e, d=s_sw: e.tensor_scalar(
                out=d, in0=d, scalar1=1e-5, scalar2=None, op0=ALU.max), inc=True)
            S.ps6 = ("dv", t_mn)
            S.v(lambda e, d=s_hf, s0=s_sw: e.tensor_scalar(
                out=d, in0=s0, scalar1=0.5, scalar2=None, op0=ALU.mult))
            t_rc = S.v(lambda e, d=s_sw: e.reciprocal(out=d, in_=d), inc=True)
        stats_done = t_rc

        p3b = Bump(PH)
        qxF = p3b.b(EC * T)
        O_AFTER_QXF = p3b.off
        x2t = [p3b.f(E), p3b.f(E)]; h2t = [p3b.f(E), p3b.f(E)]
        LW3 = min(1024, E // 2)
        lad_y = p3b.f(LW3); lad_r = p3b.f(LW3); lad_b = p3b.f(LW3)
        lad_y2 = p3b.f(LW3); lad_r2 = p3b.f(LW3); lad_b2 = p3b.f(LW3)
        sqdump = p3b.f(E)
        bcC = p3b.f(E); bcD = p3b.f(E)
        wdst = [p3b.f(E), p3b.f(E)]
        ternb = [p3b.b(E), p3b.b(E)]
        scrd = p3b.b(E)
        fence_sync()
        t_cb = bcast(bcC, modrow(4), E, "add1", st_guard=mod_st_done)
        t_sb = bcast(bcD, modrow(3), E, "none")
        S.wait_total("sync", x2_done)

        wd_use = [None, None]
        tb_use = [None, None]
        tern_st = [None, None]

        def emit_tern(ic):
            sl = ic % 2
            if wd_use[sl]:
                S.wait("sync", "ac", wd_use[sl])
            t_l = S.load(wdst[sl], wd_t[ic * 128:(ic + 1) * 128, :], f"wd{sl}")
            S.waitp("scalar", t_l)
            if ic == 0:
                S.wait("scalar", "dv", stats_done)
            if tern_st[sl]:
                S.waitp("scalar", tern_st[sl])
            for g in tb_use:
                if g:
                    S.wait("scalar", "dv", g)
            S.s(lambda e, s=wdst[sl], d=ternb[sl]: e.activation(
                out=d, in_=s, func=AF.Sign, scale=s_swd, bias=c_mhalf))
            wd_use[sl] = S.s(lambda e, s=wdst[sl]: e.activation(
                out=scrd, in_=s, func=AF.Sign, scale=s_swd, bias=c_phalf), inc=True)
            S.wait("vector", "ac", wd_use[sl])
            tb_use[sl] = S.v(lambda e, d=ternb[sl]: e.tensor_tensor(
                out=d, in0=d, in1=scrd, op=ALU.add), inc=True)
            S.wait("gpsimd", "dv", tb_use[sl])
            tern_st[sl] = S.store(tern_sc[ic * 128:(ic + 1) * 128, :], ternb[sl], f"ts{sl}")

        qx_tr_use = [None, None]
        last_qx_evac = None
        tr_last = [None, None]
        x2t_use = [None, None]
        sq_done = [None, None]
        ld_done = [None, None]
        ic_done = 0

        def p3b_stage1(tt):
            p = tt % 2
            x2s = x2t[p]
            if x2t_use[p]:
                S.wait("sync", "dv", x2t_use[p])
            t_x2 = S.load(x2s, x2_sc[tt * 128:(tt + 1) * 128, :], f"x2t{p}")
            ld_done[p] = t_x2
            S.waitp("scalar", t_x2)
            sq_done[p] = S.s(lambda e, x=x2s, p2=p: e.activation(
                out=sqdump, in_=x, func=AF.Square, accum_out=sp_ss[p2]), inc=True)

        def p3b_stage2(tt):
            p = tt % 2
            x2s = x2t[p]
            h2s = h2t[p]
            S.waitp("vector", ld_done[p])
            S.v(lambda e, x=x2s, p2=p: e.reduce_sum(out=sp_s1[p2], in_=x, axis=AX.X))
            S.v(lambda e, p2=p: e.tensor_scalar(out=sp_m[p2], in0=sp_s1[p2],
                scalar1=1.0 / E, scalar2=None, op0=ALU.mult))
            S.v(lambda e, p2=p: e.tensor_tensor(out=sp_mm2[p2], in0=sp_m[p2],
                in1=sp_m[p2], op=ALU.mult))
            S.wait("vector", "ac", sq_done[p])
            t_var = S.v(lambda e, p2=p: e.scalar_tensor_tensor(
                out=sp_var[p2], in0=sp_ss[p2], scalar=1.0 / E, in1=sp_mm2[p2],
                op0=ALU.mult, op1=ALU.subtract), inc=True)
            S.wait("scalar", "dv", t_var)
            t_rstd = S.s(lambda e, p2=p: e.activation(
                out=sp_rstd[p2], in_=sp_var[p2], func=AF.Sqrt, bias=c_eps5), inc=True)
            S.wait("vector", "ac", t_rstd)
            S.v(lambda e, p2=p: e.reciprocal(out=sp_rstd[p2], in_=sp_rstd[p2]))
            if tt == 0:
                S.wait("vector", "ac", t_cb)
                S.wait("vector", "ac", t_sb)
            if tr_last[p]:
                S.wait("vector", "pe", tr_last[p])
            x2t_use[p] = S.v(lambda e, x=x2s, d=h2s, p2=p: e.tensor_scalar(
                out=d, in0=x, scalar1=sp_m[p2], scalar2=sp_rstd[p2],
                op0=ALU.subtract, op1=ALU.mult), inc=True)
            S.v(lambda e, d=h2s: e.tensor_tensor(out=d, in0=d, in1=bcC, op=ALU.mult))
            S.v(lambda e, d=h2s: e.tensor_tensor(out=d, in0=d, in1=bcD, op=ALU.add))
            S.v(lambda e, d=h2s, p2=p: e.reduce_max(out=sp_amax[p2], in_=d, axis=AX.X,
                                                    apply_absolute_value=True))
            S.v(lambda e, p2=p: e.tensor_scalar(
                out=sp_amax[p2], in0=sp_amax[p2], scalar1=1e-5, scalar2=None, op0=ALU.max))
            S.v(lambda e, t=tt, p2=p: e.tensor_scalar(
                out=invsx[:, t:t + 1], in0=sp_amax[p2], scalar1=1.0 / 127.0,
                scalar2=None, op0=ALU.mult))
            S.v(lambda e, t=tt, p2=p: e.reciprocal(out=sp_sx[p2], in_=invsx[:, t:t + 1]))

        def p3b_ladder(tt):
            nonlocal last_qx_evac, ic_done
            p = tt % 2
            h2s = h2t[p]
            t_lad = ladder_sd(h2s, sp_sx[p], h2s, E,
                              [lad_y, lad_y2], [lad_r, lad_r2], [lad_b, lad_b2], LW3)
            S.wait("tensor", "dv", t_lad)
            for e0 in range(0, EC, 4):
                npk = min(4, EC - e0)
                bsl = (e0 // 4) % 2
                if qx_tr_use[bsl]:
                    S.wait("tensor", "ac", qx_tr_use[bsl])
                t_tr = None
                for k in range(npk):
                    t_tr = S.pe(lambda e, c=e0 + k, b=bsl, d=h2s, kk=k: e.transpose(
                        out=ps[1 + b][:, kk * 128:(kk + 1) * 128],
                        in_=d[:, c * 128:(c + 1) * 128],
                        identity=idf), inc=True)
                tr_last[p] = t_tr
                S.wait("scalar", "pe", t_tr)
                qx_tr_use[bsl] = S.s(lambda e, c=e0, b=bsl, t=tt, w=npk * 128: e.activation(
                    out=qxF[:, t * E + c * 128: t * E + c * 128 + w],
                    in_=ps[1 + b][:, 0:w], func=AF.Copy), inc=True)
            last_qx_evac = qx_tr_use[((EC - 1) // 4) % 2]
            ntern = (IC * (tt + 1)) // TC
            while ic_done < ntern:
                emit_tern(ic_done); ic_done += 1

        p3b_stage1(0)
        for tt in range(TC):
            if tt + 1 < TC:
                p3b_stage1(tt + 1)
            p3b_stage2(tt)
            p3b_ladder(tt)
        while ic_done < IC:
            emit_tern(ic_done); ic_done += 1
        tern_done = ["ts0", "ts1"]

        t_scl = None
        for tt in range(TC):
            S.v(lambda e, t=tt: e.tensor_tensor(
                out=sclg[:, t:t + 1], in0=invsx[:, t:t + 1], in1=s_hfg, op=ALU.mult))
            t_scl = S.v(lambda e, t=tt: e.tensor_tensor(
                out=sclu[:, t:t + 1], in0=invsx[:, t:t + 1], in1=s_hfu, op=ALU.mult), inc=True)

        p4 = Bump(O_AFTER_QXF)
        gst = p4.f(EC * 512)
        ternG = [p4.b(EC * 512), p4.b(EC * 512)]
        ternU = [p4.b(EC * 512), p4.b(EC * 512)]
        scrg = p4.b(EC * 512)
        gs_t = [p4.f(512), p4.f(512)]
        hst = [p4.f(512), p4.f(512)]

        fence_sync()
        tG_use = [None, None]; tU_use = [None, None]
        tU_rdy_prev = None
        gst_free = None
        hst_use = [None, None]
        psG_use = [None, None]; psU_use = [None, None]
        for js in range(JS):
            sl = js % 2
            if gst_free:
                S.wait("sync", "ac", gst_free)
            if tG_use[sl]:
                S.wait("sync", "pe", tG_use[sl])
            t_g = S.load(gst, wg_r[js * 128:(js + 1) * 128, :], "gst")
            S.waitp("scalar", t_g)
            if js == 0:
                S.wait("scalar", "dv", stats_done)
            if tU_rdy_prev:
                S.wait("scalar", "dv", tU_rdy_prev)
            S.s(lambda e, d=ternG[sl]: e.activation(
                out=d, in_=gst, func=AF.Sign, scale=s_swg, bias=c_mhalf))
            t_sg = S.s(lambda e: e.activation(
                out=scrg, in_=gst, func=AF.Sign, scale=s_swg, bias=c_phalf), inc=True)
            gst_free = t_sg
            S.wait("vector", "ac", t_sg)
            tG_rdy = S.v(lambda e, d=ternG[sl]: e.tensor_tensor(
                out=d, in0=d, in1=scrg, op=ALU.add), inc=True)
            if tU_use[sl]:
                S.wait("sync", "pe", tU_use[sl])
            S.wait("sync", "ac", t_sg)
            t_u = S.load(gst, wu_r[js * 128:(js + 1) * 128, :], "gst")
            S.waitp("scalar", t_u)
            S.wait("scalar", "dv", tG_rdy)
            S.s(lambda e, d=ternU[sl]: e.activation(
                out=d, in_=gst, func=AF.Sign, scale=s_swu, bias=c_mhalf))
            t_su = S.s(lambda e: e.activation(
                out=scrg, in_=gst, func=AF.Sign, scale=s_swu, bias=c_phalf), inc=True)
            gst_free = t_su
            S.wait("vector", "ac", t_su)
            tU_rdy = S.v(lambda e, d=ternU[sl]: e.tensor_tensor(
                out=d, in0=d, in1=scrg, op=ALU.add), inc=True)
            tU_rdy_prev = tU_rdy
            if js == 0:
                S.wait("tensor", "ac", last_qx_evac)
            S.wait("tensor", "dv", tU_rdy)
            for tt in range(TC):
                bg = tt % 2
                pG = ps[0] if bg == 0 else ps[1]
                pU = ps[2] if bg == 0 else ps[3]
                S.waitp("tensor", psG_use[bg])
                for ec in range(EC):
                    t_gmm = S.pe(lambda e, c=ec, t=tt, p=pG, d=ternG[sl]: e.matmul(
                        out=p[:, 0:512], lhsT=qxF[:, t * E + c * 128: t * E + c * 128 + 128],
                        rhs=d[:, c * 512:(c + 1) * 512], start=(c == 0), stop=(c == EC - 1),
                        skip_group_check=True), inc=(ec == EC - 1))
                S.waitp("tensor", psU_use[bg])
                for ec in range(EC):
                    t_umm = S.pe(lambda e, c=ec, t=tt, p=pU, d=ternU[sl]: e.matmul(
                        out=p[:, 0:512], lhsT=qxF[:, t * E + c * 128: t * E + c * 128 + 128],
                        rhs=d[:, c * 512:(c + 1) * 512], start=(c == 0), stop=(c == EC - 1),
                        skip_group_check=True), inc=(ec == EC - 1))
                tG_use[sl] = t_gmm; tU_use[sl] = t_umm
                ssl = tt % 2
                S.wait("scalar", "pe", t_gmm)
                if js == 0 and tt == 0:
                    S.wait("scalar", "dv", t_scl)
                if psU_use[bg]:
                    S.wait("scalar", "dv", psU_use[bg][1])
                t_si = S.s(lambda e, p=pG, d=gs_t[ssl], t=tt: e.activation(
                    out=d, in_=p[:, 0:512], func=AF.Silu, scale=sclg[:, t:t + 1]), inc=True)
                psG_use[bg] = ("ac", t_si)
                if hst_use[ssl]:
                    S.waitp("vector", hst_use[ssl])
                S.wait("vector", "pe", t_umm)
                S.wait("vector", "ac", t_si)
                t_h = S.v(lambda e, p=pU, d=hst[ssl], g=gs_t[ssl], t=tt: e.scalar_tensor_tensor(
                    out=d, in0=p[:, 0:512], scalar=sclu[:, t:t + 1], in1=g,
                    op0=ALU.mult, op1=ALU.mult), inc=True)
                psU_use[bg] = ("dv", t_h)
                S.wait("gpsimd", "dv", t_h)
                hst_use[ssl] = S.store(
                    h_sc[tt * 128:(tt + 1) * 128, js * 512:(js + 1) * 512], hst[ssl], f"hs{ssl}")
        h_done = ["hs0", "hs1"]

        p5 = Bump(PH)
        lnwB = p5.f(I)
        h2t2 = [p5.f(I), p5.f(I)]
        qh2F = p5.b(IC * 256)
        ternD = [p5.b(E), p5.b(E)]
        x2l = p5.f(E)
        xout = p5.f(E)
        gbB = p5.f(E)
        LW5 = min(1024, I // 2)
        dl_y = [p5.f(LW5) for _ in range(3)]
        dl_r = [p5.f(LW5) for _ in range(3)]
        dl_b = [p5.f(LW5) for _ in range(3)]
        dl_dump = p5.f(512)
        sqparts = p5.f(2 * (I // 512))

        fence_sync()
        t_lnw = bcast(lnwB, lambda o, n: lnw_row[0:1, o:o + n], I, None)
        t_gb = bcast(gbB, modrow(5), E, "signclamp", st_guard=mod_st_done)
        S.wait_total("sync", h_done)
        S.wait_total("sync", tern_done)

        h2buf_use = [None, None]
        ternD_use = [None, None]
        tr2_use = [None, None]
        qh2_evac = None
        prev_epi_dv = None
        xout_st = [None, None]

        ld5_done = [None, None]
        sq5_done = [None, None]
        h2_guard_p = [None, None]

        def p5_stage1(b6, sub):
            tt = 2 * b6 + sub
            hb2 = h2t2[sub]
            if h2buf_use[sub]:
                S.wait("sync", "pe", h2buf_use[sub])
            if h2_guard_p[sub]:
                S.wait("sync", "dv", h2_guard_p[sub])
            t_h = S.load(hb2, h_sc[tt * 128:(tt + 1) * 128, :], f"h2l{sub}")
            ld5_done[sub] = t_h
            S.waitp("scalar", t_h)
            if b6 == 0 and sub == 0:
                S.wait("scalar", "ac", t_lnw)
            t_sq = None
            nstrip = I // 512
            for si in range(nstrip):
                t_sq = S.s(lambda e, o=si * 512, p=si, s2=sub, h=hb2: e.activation(
                    out=dl_dump, in_=h[:, o:o + 512], func=AF.Square,
                    accum_out=sqparts[:, s2 * nstrip + p: s2 * nstrip + p + 1]), inc=True)
            sq5_done[sub] = t_sq

        def p5_stage2lad(b6, sub):
            tt = 2 * b6 + sub
            hb2 = h2t2[sub]
            nstrip = I // 512
            S.waitp("vector", ld5_done[sub])
            S.wait("vector", "ac", sq5_done[sub])
            S.v(lambda e, s2=sub: e.reduce_sum(
                out=sp_ms[s2], in_=sqparts[:, s2 * nstrip:(s2 + 1) * nstrip], axis=AX.X))
            S.v(lambda e, h=hb2: e.tensor_tensor(out=h, in0=h, in1=lnwB, op=ALU.mult))
            S.v(lambda e, h=hb2, s2=sub: e.reduce_max(
                out=sp_rm[s2], in_=h, axis=AX.X, apply_absolute_value=True))
            t_ms = S.v(lambda e, s2=sub: e.tensor_scalar(
                out=sp_ms[s2], in0=sp_ms[s2], scalar1=1.0 / I, scalar2=None,
                op0=ALU.mult), inc=True)
            S.wait("scalar", "dv", t_ms)
            t_r2 = S.s(lambda e, s2=sub: e.activation(
                out=sp_rstd2[s2], in_=sp_ms[s2], func=AF.Sqrt, bias=c_eps6), inc=True)
            S.wait("vector", "ac", t_r2)
            S.v(lambda e, s2=sub: e.reciprocal(out=sp_rstd2[s2], in_=sp_rstd2[s2]))
            S.v(lambda e, s2=sub: e.tensor_tensor(
                out=sp_rm[s2], in0=sp_rm[s2], in1=sp_rstd2[s2], op=ALU.mult))
            S.v(lambda e, s2=sub: e.tensor_scalar(
                out=sp_rm[s2], in0=sp_rm[s2], scalar1=1e-5, scalar2=None, op0=ALU.max))
            S.v(lambda e, s2=sub, p=b6 % 2: e.tensor_scalar(
                out=s_invs2[p][s2], in0=sp_rm[s2], scalar1=1.0 / 127.0,
                scalar2=None, op0=ALU.mult))
            S.v(lambda e, s2=sub, p=b6 % 2: e.reciprocal(out=sp_sx2[s2], in_=s_invs2[p][s2]))
            S.v(lambda e, s2=sub: e.tensor_tensor(
                out=sp_sx2[s2], in0=sp_sx2[s2], in1=sp_rstd2[s2], op=ALU.mult))
            t_lad = ladder_sd(hb2, sp_sx2[sub], hb2, I, dl_y, dl_r, dl_b, LW5)
            h2_guard_p[sub] = t_lad
            return t_lad

        def emit_transposes(b6, t_lad):
            nonlocal qh2_evac, h2buf_use
            S.wait("tensor", "dv", t_lad)
            if prev_epi_dv:
                S.wait("tensor", "dv", prev_epi_dv)
            for sub in range(2):
                last_tr = None
                for i0 in range(0, IC, 4):
                    npk = min(4, IC - i0)
                    bsl = (i0 // 4) % 2
                    if tr2_use[bsl]:
                        S.wait("tensor", "ac", tr2_use[bsl])
                    t_tr = None
                    for k in range(npk):
                        t_tr = S.pe(lambda e, c=i0 + k, b=bsl, s2=sub, kk=k: e.transpose(
                            out=ps[1 + b][:, kk * 128:(kk + 1) * 128],
                            in_=h2t2[s2][:, c * 128:c * 128 + 128],
                            identity=idf), inc=True)
                    S.wait("scalar", "pe", t_tr)
                    tr2_use[bsl] = S.s(lambda e, c=i0, b=bsl, s2=sub, w=npk * 128: e.activation(
                        out=qh2F[:, s2 * IC * 128 + c * 128: s2 * IC * 128 + c * 128 + w],
                        in_=ps[1 + b][:, 0:w], func=AF.Copy), inc=True)
                    last_tr = t_tr
                h2buf_use[sub] = last_tr
            qh2_evac = tr2_use[(((IC - 1) // 4) % 2)]

        def emit_down(b6):
            nonlocal prev_epi_dv, ternD_use
            S.wait("tensor", "ac", qh2_evac)
            if prev_epi_dv:
                S.wait("tensor", "dv", prev_epi_dv)
            mm_fin = None
            for ic in range(IC):
                sl = ic % 2
                if ternD_use[sl]:
                    S.wait("sync", "pe", ternD_use[sl])
                t_td = S.load(ternD[sl], tern_sc[ic * 128:(ic + 1) * 128, :], f"td{sl}")
                S.waitp("tensor", t_td)
                lastmm = None
                for tsub in range(2):
                    for ft in range(NFT):
                        bank = tsub * NFT + ft
                        lastmm = S.pe(lambda e, c=ic, ts=tsub, f=ft, b=bank, d=ternD[sl]:
                                      e.matmul(out=ps[b][:, 0:NF],
                                               lhsT=qh2F[:, ts * IC * 128 + c * 128: ts * IC * 128 + c * 128 + 128],
                                               rhs=d[:, f * NF:(f + 1) * NF],
                                               start=(c == 0), stop=(c == IC - 1),
                                               skip_group_check=True),
                                      inc=(tsub == 1 and ft == NFT - 1))
                ternD_use[sl] = lastmm
                mm_fin = lastmm
            S.wait("vector", "pe", mm_fin)
            if b6 == 0:
                S.wait("vector", "ac", t_gb)
            for sub in range(2):
                tt = 2 * b6 + sub
                if prev_epi_dv:
                    S.wait("sync", "dv", prev_epi_dv)
                t_x2 = S.load(x2l, x2_sc[tt * 128:(tt + 1) * 128, :], "x2l")
                S.waitp("vector", t_x2)
                if xout_st[0]:
                    S.waitp("vector", xout_st[0])
                S.v(lambda e, s2=sub, p=b6 % 2: e.tensor_tensor(
                    out=s_scld[s2], in0=s_invs2[p][s2], in1=s_hfd, op=ALU.mult))
                t_epi = None
                for ft in range(NFT):
                    bank = sub * NFT + ft
                    d_ap = xout[:, ft * NF:(ft + 1) * NF]
                    S.v(lambda e, d=d_ap, b=bank, s2=sub: e.tensor_scalar(
                        out=d, in0=ps[b][:, 0:NF], scalar1=s_scld[s2], scalar2=None,
                        op0=ALU.mult))
                    S.v(lambda e, d=d_ap, f=ft: e.tensor_tensor(
                        out=d, in0=d, in1=gbB[:, f * NF:(f + 1) * NF], op=ALU.mult))
                    t_epi = S.v(lambda e, d=d_ap, f=ft: e.tensor_tensor(
                        out=d, in0=d, in1=x2l[:, f * NF:(f + 1) * NF], op=ALU.add), inc=True)
                S.wait("gpsimd", "dv", t_epi)
                xout_st[0] = S.store(y_out[tt * 128:(tt + 1) * 128, :], xout, "yo0")
                prev_epi_dv = t_epi

        pending = None
        for b6 in range(NB6):
            p5_stage1(b6, 0)
            p5_stage2lad(b6, 0)
            p5_stage1(b6, 1)
            t_lad1 = p5_stage2lad(b6, 1)
            if pending is not None:
                emit_down(pending)
            emit_transposes(b6, t_lad1)
            pending = b6
        emit_down(pending)

        S.wait_total("sync", ["yo0", "yo1"])

        for nm in ("pe", "dv", "ac"):
            S.semh[nm] = es.enter_context(nc.semaphore(nm))
        for nm in sorted(S.sems):
            S.semh[nm] = es.enter_context(nc.semaphore(nm))

        block = es.enter_context(nc.Block())

        @block.sync
        def _(sync):
            for fn in S.ops["sync"]:
                fn(sync)

        @block.gpsimd
        def _(gpsimd):
            for fn in S.ops["gpsimd"]:
                fn(gpsimd)

        @block.tensor
        def _(tensor):
            for fn in S.ops["tensor"]:
                fn(tensor)

        @block.vector
        def _(vector):
            for fn in S.ops["vector"]:
                fn(vector)

        @block.scalar
        def _(scalar):
            for fn in S.ops["scalar"]:
                fn(scalar)

    return nc


def make_tables(T):
    D = 128
    j = np.arange(D // 2, dtype=np.float64)
    thetas = 10000.0 ** (-2.0 * j / D)
    m = np.arange(1, T + 1, dtype=np.float64)
    ang = np.concatenate([thetas, thetas])[:, None] * m[None, :]
    c = 128.0 ** (-0.25)
    cosv = np.cos(ang) * c
    sinv = np.sin(ang) * c
    sgn = np.ones((D, 1)); sgn[:D // 2] = -1.0
    sinv = sinv * sgn
    chi = cosv.astype(np.float32)
    clo = (cosv - chi.astype(np.float64)).astype(np.float32)
    shi = sinv.astype(np.float32)
    slo = (sinv - shi.astype(np.float64)).astype(np.float32)
    return chi, clo, shi, slo


def make_masks():
    i = np.arange(128)[:, None]
    jq = np.arange(512)[None, :]
    return np.concatenate([
        np.where(128 * rr + i <= jq, 0.0, NEGBIG).astype(np.float32)
        for rr in range(4)], axis=1)


def host_inputs(T, E, I, Wq, bq, Wk, bk, Wv, bv, Wo, bo, wg, wu, wd, lnw, wmod):
    EC = E // 128
    FT = E // 128
    NV = min(512, E); FSV = E // NV
    N2 = min(256, E); FS2 = E // N2
    JS = I // 512
    asc = np.ascontiguousarray
    f = np.float32
    d = {}
    d["wq_t"] = asc(np.asarray(Wq, f).reshape(FT, 128, EC, 128).transpose(0, 3, 2, 1).reshape(E, E))
    d["wk_t"] = asc(np.asarray(Wk, f).reshape(FT, 128, EC, 128).transpose(0, 3, 2, 1).reshape(E, E))
    d["wv_r"] = asc(np.asarray(Wv, f).T.reshape(EC, 128, FSV, NV).transpose(2, 1, 0, 3).reshape(FSV * 128, EC * NV))
    d["wo_r"] = asc(np.asarray(Wo, f).T.reshape(EC, 128, FS2, N2).transpose(2, 1, 0, 3).reshape(FS2 * 128, EC * N2))
    d["bq_c"] = asc(np.asarray(bq, f).reshape(EC, 128).T)
    d["bk_c"] = asc(np.asarray(bk, f).reshape(EC, 128).T)
    d["bv_row"] = asc(np.asarray(bv, f).reshape(1, E))
    d["bo_row"] = asc(np.asarray(bo, f).reshape(1, E))
    d["wmod"] = asc(np.asarray(wmod, f))
    d["wg_r"] = asc(np.asarray(wg, f).T.reshape(EC, 128, JS, 512).transpose(2, 1, 0, 3).reshape(JS * 128, EC * 512))
    d["wu_r"] = asc(np.asarray(wu, f).T.reshape(EC, 128, JS, 512).transpose(2, 1, 0, 3).reshape(JS * 128, EC * 512))
    d["wd_t"] = asc(np.asarray(wd, f).T)
    d["lnw_row"] = asc(np.asarray(lnw, f).reshape(1, I))
    chi, clo, shi, slo = make_tables(T)
    d["chi_in"] = chi; d["clo_in"] = clo; d["shi_in"] = shi; d["slo_in"] = slo
    d["mask_in"] = make_masks()
    d["id_in"] = np.eye(128, dtype=np.float32)
    return d


def kernel(L0_attn_x, L0_attn_Wq, L0_attn_bq, L0_attn_Wk, L0_attn_bk,
           L0_attn_Wv, L0_attn_bv, L0_attn_Wo, L0_attn_bo,
           L0_ffn_w_gate, L0_ffn_w_up, L0_ffn_w_down, L0_ffn_ln_weight,
           L0_c, L0_wmod):
    T, E, I = 2048, 2048, 8192
    xs = np.asarray(L0_attn_x, np.float32)
    cs = np.asarray(L0_c, np.float32)
    B = xs.shape[0]
    EC = E // 128
    nc = build_program(T, E, I)
    shared = host_inputs(T, E, I, L0_attn_Wq, L0_attn_bq, L0_attn_Wk, L0_attn_bk,
                         L0_attn_Wv, L0_attn_bv, L0_attn_Wo, L0_attn_bo,
                         L0_ffn_w_gate, L0_ffn_w_up, L0_ffn_w_down,
                         L0_ffn_ln_weight, L0_wmod)
    in_maps = []
    for b in range(B):
        m = dict(shared)
        m["x_in"] = np.ascontiguousarray(xs[b])
        m["c_col"] = np.ascontiguousarray(cs[b].reshape(EC, 128).T)
        in_maps.append(m)
    res = run_bass_kernel_spmd(nc, in_maps, core_ids=list(range(B)))
    out = np.stack([np.asarray(res.results[b]["y_out"]) for b in range(B)], axis=0)
    return out.astype(np.float32)


# version 13 (final)
# speedup vs baseline: 1.2002x; 1.0220x over previous
import numpy as np
from contextlib import ExitStack
import concourse.bass as bass
import concourse.mybir as mybir
from concourse.bass_utils import run_bass_kernel_spmd

F32 = mybir.dt.float32
BF16 = mybir.dt.bfloat16
AF = mybir.ActivationFunctionType
ALU = mybir.AluOpType
AX = mybir.AxisListType

EXP_BIAS = -16.0
NEGBIG = -1.0e38
ARENA = 53152


class Sched:
    ENGS = ("sync", "gpsimd", "tensor", "vector", "scalar")

    def __init__(self):
        self.ops = {e: [] for e in self.ENGS}
        self.cnt = {"pe": 0, "dv": 0, "ac": 0}
        self.sems = {}
        self.semh = {}
        self.ps6 = None

    def _add(self, eng, fn):
        self.ops[eng].append(fn)

    def op(self, eng, build, sem=None, inc=0):
        tgt = None
        if sem is not None:
            self.cnt[sem] += inc
            tgt = self.cnt[sem]

        def fn(e, b=build, s=sem, i=inc, self_=self):
            inst = b(e)
            if s is not None:
                inst.then_inc(self_.semh[s], i)
        self._add(eng, fn)
        return tgt

    def wait(self, eng, sem, val):
        if not val:
            return
        def fn(e, s=sem, v=val, self_=self):
            e.wait_ge(self_.semh[s], v)
        self._add(eng, fn)

    def waitp(self, eng, pair):
        if pair:
            self.wait(eng, pair[0], pair[1])

    def wait_total(self, eng, names):
        for n in names:
            if self.sems.get(n):
                self.wait(eng, n, self.sems[n])

    def v(self, build, inc=False):
        self._add("vector", lambda e: e.drain())
        return self.op("vector", build, "dv" if inc else None, 1 if inc else 0)

    def s(self, build, inc=False):
        self._add("scalar", lambda e: e.drain())
        return self.op("scalar", build, "ac" if inc else None, 1 if inc else 0)

    def pe(self, build, inc=False):
        return self.op("tensor", build, "pe" if inc else None, 1 if inc else 0)

    def _dma(self, eng, out, in_, sem):
        self.sems[sem] = self.sems.get(sem, 0) + 16
        tgt = self.sems[sem]

        def fn(e, o=out, i=in_, s=sem, self_=self):
            e.dma_start(out=o, in_=i).then_inc(self_.semh[s], 16)
        self._add(eng, fn)
        return (sem, tgt)

    def load(self, out, in_, sem):
        return self._dma("sync", out, in_, sem)

    def store(self, out, in_, sem):
        return self._dma("gpsimd", out, in_, sem)


def build_program(T, E, I):
    H = E // 128
    EC = E // 128
    TC = T // 128
    QT = T // 512
    NV = min(512, E); FSV = E // NV
    N2 = min(256, E); FS2 = E // N2
    JS = I // 512
    IC = I // 128
    NB6 = T // 256
    NF = min(512, E); NFT = E // NF
    STRIP_E = min(1024, E)
    STRIP_I = min(1024, I)

    nc = bass.Bass("TRN2", target_bir_lowering=False)
    dt = lambda n, s, d=F32, k="ExternalInput": nc.dram_tensor(n, s, d, kind=k).ap()

    x_in = dt("x_in", [T, E])
    wq_t = dt("wq_t", [E, E]); wk_t = dt("wk_t", [E, E])
    wv_r = dt("wv_r", [FSV * 128, EC * NV])
    wo_r = dt("wo_r", [FS2 * 128, EC * N2])
    bq_c = dt("bq_c", [128, EC]); bk_c = dt("bk_c", [128, EC])
    bv_row = dt("bv_row", [1, E]); bo_row = dt("bo_row", [1, E])
    wmod = dt("wmod", [E, 6 * E])
    c_col = dt("c_col", [128, EC])
    wg_r = dt("wg_r", [JS * 128, EC * 512])
    wu_r = dt("wu_r", [JS * 128, EC * 512])
    wd_t = dt("wd_t", [I, E])
    lnw_row = dt("lnw_row", [1, I])
    chi_in = dt("chi_in", [128, T]); clo_in = dt("clo_in", [128, T])
    shi_in = dt("shi_in", [128, T]); slo_in = dt("slo_in", [128, T])
    mask_in = dt("mask_in", [128, 4 * 512])
    id_in = dt("id_in", [128, 128])

    q_sc = dt("q_sc", [H * 128, T], F32, "Internal")
    k_sc = dt("k_sc", [H * 128, T], F32, "Internal")
    v_sc = dt("v_sc", [H * T, 128], F32, "Internal")
    x2_sc = dt("x2_sc", [T, E], F32, "Internal")
    h_sc = dt("h_sc", [T, I], F32, "Internal")
    tern_sc = dt("tern_sc", [I, E], BF16, "Internal")
    mod_sc = dt("mod_sc", [1, 6 * E], F32, "Internal")
    y_out = dt("y_out", [T, E], F32, "ExternalOutput")

    S = Sched()
    es = ExitStack()
    with es:
        AR = es.enter_context(nc.sbuf_tensor("arena", [128, ARENA], F32))
        ARB = AR.bitcast(BF16)

        def f32(off, n):
            assert off + n <= ARENA, ("f32 overflow", off, n)
            return AR[:, off:off + n]

        def b16(off_f32, n_b16):
            assert 2 * off_f32 + n_b16 <= 2 * ARENA, ("b16 overflow", off_f32, n_b16)
            return ARB[:, 2 * off_f32:2 * off_f32 + n_b16]

        class Bump:
            def __init__(self, start):
                self.off = start
            def f(self, n):
                o = self.off; self.off += n
                assert self.off <= ARENA, ("bump overflow", self.off)
                return f32(o, n)
            def b(self, n_b16):
                n = (n_b16 + 1) // 2
                o = self.off; self.off += n
                assert self.off <= ARENA, ("bump overflow b16", self.off)
                return b16(o, n_b16)

        st_ = Bump(0)
        idf = st_.f(128)
        idb = st_.b(128)
        rowst_full = st_.f(512); rowst = rowst_full[0:1, :]
        rowst2 = st_.f(512)[0:1, :]
        rowst3 = st_.f(512)[0:1, :]
        ones_blk = st_.f(128)
        ones_col = ones_blk[:, 0:1]
        ones_row = ones_blk[0:1, :]
        scal = st_.f(64)
        sm = lambda i: scal[:, i:i + 1]
        (s_s1, s_ss, s_m, s_mm2, s_var, s_rstd, s_amax, s_sx, s_zr, s_swg,
         s_hfg, s_swu, s_hfu, s_swd, s_hfd, s_ms, s_rstd2, s_rm) = [sm(i) for i in range(18)]
        s_invs2 = [[sm(18), sm(19)], [sm(28), sm(29)]]
        s_scld = [sm(20), sm(21)]
        s_t11 = sm(22)
        c_eps5, c_eps6, c_mhalf, c_phalf, c_expb = sm(23), sm(24), sm(25), sm(26), sm(27)
        sp_s1 = [sm(30), sm(31)]; sp_ss = [sm(32), sm(33)]
        sp_m = [sm(34), sm(35)]; sp_mm2 = [sm(36), sm(37)]
        sp_var = [sm(38), sm(39)]; sp_rstd = [sm(40), sm(41)]
        sp_amax = [sm(42), sm(43)]; sp_sx = [sm(44), sm(45)]
        sp_ms = [sm(46), sm(47)]; sp_rstd2 = [sm(48), sm(49)]
        sp_rm = [sm(50), sm(51)]; sp_sx2 = [sm(52), sm(53)]
        c_m128 = sm(54)
        bq_sb = st_.f(EC); bk_sb = st_.f(EC)
        cc_sb = st_.f(EC); scc_sb = st_.f(EC)
        invsx = st_.f(TC); sclg = st_.f(TC); sclu = st_.f(TC)
        parts = [st_.f(128) for _ in range(3)]
        PH = st_.off

        ps = [es.enter_context(nc.psum_tensor(f"ps{i}", [128, 512], F32)) for i in range(8)]
        psb = ps[7].bitcast(BF16)


        def ladder_sd(src_ap, scale_ap, dst_ap, width, ytiles, rtiles, stiles, sw):
            strips = [(o, min(sw, width - o)) for o in range(0, width, sw)]
            NG = len(ytiles)
            t_last = None
            for g0 in range(0, len(strips), NG):
                grp = strips[g0:g0 + NG]
                ydv = []
                for i, (o, w) in enumerate(grp):
                    ydv.append(S.v(lambda e, d=ytiles[i][:, 0:w], s0=src_ap[:, o:o + w],
                                   sc=scale_ap: e.tensor_scalar(
                        out=d, in0=s0, scalar1=sc, scalar2=128.5,
                        op0=ALU.mult, op1=ALU.add), inc=True))
                rdv = list(ydv)
                rsrc = [ytiles[i] for i in range(len(grp))]
                for li, t2 in enumerate((128.0, 64.0, 32.0, 16.0, 8.0, 4.0, 2.0, 1.0)):
                    sac = []
                    for i, (o, w) in enumerate(grp):
                        S.wait("scalar", "dv", rdv[i])
                        sac.append(S.s(lambda e, d=stiles[i][:, 0:w],
                                       s0=rsrc[i][:, 0:w]: e.activation(
                            out=d, in_=s0, func=AF.Sign, bias=c_m128), inc=True))
                    for i, (o, w) in enumerate(grp):
                        S.wait("vector", "ac", sac[i])
                        rdv[i] = S.v(lambda e, d=rtiles[i][:, 0:w], s0=stiles[i][:, 0:w],
                                     s1=rsrc[i][:, 0:w], tv=t2 / 2: e.scalar_tensor_tensor(
                            out=d, in0=s0, scalar=-tv, in1=s1,
                            op0=ALU.mult, op1=ALU.add), inc=True)
                    rsrc = [rtiles[i] for i in range(len(grp))]
                for i, (o, w) in enumerate(grp):
                    t_last = S.v(lambda e, d=dst_ap[:, o:o + w], s0=ytiles[i][:, 0:w],
                                 s1=rtiles[i][:, 0:w]: e.scalar_tensor_tensor(
                        out=d, in0=s0, scalar=-0.5, in1=s1,
                        op0=ALU.add, op1=ALU.subtract), inc=True)
            return t_last

        def fence_sync():
            S.wait("sync", "pe", S.cnt["pe"])
            S.wait("sync", "dv", S.cnt["dv"])
            S.wait("sync", "ac", S.cnt["ac"])
            S.wait_total("sync", sorted(S.sems))

        def bcast(dst, src_row_ap_fn, width, transform, st_guard=None):
            last = None
            seg = min(512, width)
            rw = rowst_full[0:1, 0:seg]
            rw2 = rowst2[0:1, 0:seg]
            rw3 = rowst3[0:1, 0:seg]
            for sl0 in range(width // seg):
                if st_guard:
                    S.wait_total("sync", st_guard)
                t_l = S.load(rw, src_row_ap_fn(sl0 * seg, seg), "row")
                t_r = None
                if transform == "add1":
                    S.waitp("vector", t_l)
                    t_r = S.v(lambda e: e.tensor_scalar(
                        out=rw, in0=rw, scalar1=1.0, scalar2=None, op0=ALU.add), inc=True)
                elif transform == "signclamp":
                    S.waitp("vector", t_l)
                    S.v(lambda e: e.tensor_scalar(
                        out=rw2, in0=rw, scalar1=0.0, scalar2=None, op0=ALU.is_ge))
                    S.v(lambda e: e.tensor_scalar(
                        out=rw2, in0=rw2, scalar1=2.0, scalar2=-1.0,
                        op0=ALU.mult, op1=ALU.add))
                    S.v(lambda e: e.tensor_scalar(
                        out=rw3, in0=rw, scalar1=-1.0, scalar2=None, op0=ALU.mult))
                    S.v(lambda e: e.tensor_tensor(out=rw, in0=rw, in1=rw3, op=ALU.max))
                    S.v(lambda e: e.tensor_scalar(
                        out=rw, in0=rw, scalar1=0.25, scalar2=None, op0=ALU.max))
                    t_r = S.v(lambda e: e.tensor_tensor(
                        out=rw, in0=rw, in1=rw2, op=ALU.mult), inc=True)
                if t_r:
                    S.wait("tensor", "dv", t_r)
                else:
                    S.waitp("tensor", t_l)
                S.waitp("tensor", S.ps6)
                t_mm = S.pe(lambda e: e.matmul(out=ps[6][:, 0:seg], lhsT=ones_row,
                                               rhs=rw, start=True, stop=True), inc=True)
                S.wait("scalar", "pe", t_mm)
                t_e = S.s(lambda e, d=dst[:, sl0 * seg:(sl0 + 1) * seg]: e.activation(
                    out=d, in_=ps[6][:, 0:seg], func=AF.Copy), inc=True)
                S.ps6 = ("ac", t_e)
                S.wait("sync", "pe", t_mm)
                last = t_e
            return last

        S.load(idf, id_in[:, :], "ini")
        S.load(bq_sb, bq_c[:, :], "ini"); S.load(bk_sb, bk_c[:, :], "ini")
        t_cc = S.load(cc_sb, c_col[:, :], "ini")
        S.waitp("vector", t_cc)
        S.v(lambda e: e.memset(ones_blk, 1.0))
        S.v(lambda e: e.memset(c_eps5, 1e-5))
        S.v(lambda e: e.memset(c_eps6, 1e-6))
        S.v(lambda e: e.memset(c_mhalf, -0.5))
        S.v(lambda e: e.memset(c_phalf, 0.5))
        S.v(lambda e: e.memset(c_expb, EXP_BIAS))
        S.v(lambda e: e.memset(c_m128, -128.0))
        S.v(lambda e: e.tensor_copy(out=idb, in_=idf), inc=True)
        S.waitp("scalar", t_cc)
        t_silu = S.s(lambda e: e.activation(out=scc_sb, in_=cc_sb, func=AF.Silu), inc=True)

        p0 = Bump(PH)
        wch = [p0.f(6 * E), p0.f(6 * E)]
        acc = p0.f(6 * E)
        modst = [f32(PH, 512)[0:1, :], f32(PH + 512, 512)[0:1, :]]
        use_t = [None, None]
        for ec in range(EC):
            sl = ec % 2
            if use_t[sl]:
                S.wait("sync", "dv", use_t[sl])
            t_w = S.load(wch[sl], wmod[ec * 128:(ec + 1) * 128, :], f"wch{sl}")
            S.waitp("vector", t_w)
            if ec == 0:
                S.wait("vector", "ac", t_silu)
                use_t[sl] = S.v(lambda e, w=wch[sl]: e.tensor_scalar(
                    out=acc, in0=w, scalar1=scc_sb[:, 0:1], scalar2=None,
                    op0=ALU.mult), inc=True)
            else:
                use_t[sl] = S.v(lambda e, w=wch[sl], c=ec: e.scalar_tensor_tensor(
                    out=acc, in0=w, scalar=scc_sb[:, c:c + 1], in1=acc,
                    op0=ALU.mult, op1=ALU.add), inc=True)
        t_acc = use_t[(EC - 1) % 2]
        S.wait("tensor", "dv", t_acc)
        mst_use = [None, None]
        t_cp = None
        for nt in range(6 * E // 512):
            sl = nt % 2
            if t_cp:
                S.wait("tensor", "dv", t_cp)
            t_mm = S.pe(lambda e, n=nt: e.matmul(
                out=ps[0][:, 0:512], lhsT=ones_blk,
                rhs=acc[:, n * 512:(n + 1) * 512], start=True, stop=True), inc=True)
            if mst_use[sl]:
                S.waitp("vector", mst_use[sl])
            S.wait("vector", "pe", t_mm)
            t_cp = S.v(lambda e, d=modst[sl]: e.tensor_copy(
                out=d, in_=ps[0][0:1, 0:512]), inc=True)
            S.wait("gpsimd", "dv", t_cp)
            mst_use[sl] = S.store(mod_sc[0:1, nt * 512:(nt + 1) * 512], modst[sl], f"mo{sl}")
        mod_st_done = ["mo0", "mo1"]

        def modrow(g):
            return lambda o, n, gg=g: mod_sc[0:1, gg * E + o: gg * E + o + n]

        p0b = Bump(PH)
        hF = p0b.f(EC * T)
        O_AFTER_HF = p0b.off
        bc0 = p0b.f(E); bc1 = p0b.f(E)
        xt = [p0b.f(E), p0b.f(E)]
        h1t = p0b.f(E)
        t_b0 = bcast(bc0, modrow(1), E, "add1", st_guard=mod_st_done)
        t_b1 = bcast(bc1, modrow(0), E, None, st_guard=mod_st_done)
        xt_use = [None, None]
        tr_use = [None, None]
        last_hF_evac = None
        for tt in range(TC):
            sl = tt % 2
            if xt_use[sl]:
                S.wait("sync", "dv", xt_use[sl])
            t_x = S.load(xt[sl], x_in[tt * 128:(tt + 1) * 128, :], f"xt{sl}")
            S.waitp("vector", t_x)
            S.v(lambda e, x=xt[sl]: e.reduce_sum(out=s_s1, in_=x, axis=AX.X))
            S.waitp("scalar", t_x)
            t_sq = S.s(lambda e, x=xt[sl]: e.activation(
                out=h1t, in_=x, func=AF.Square, accum_out=s_ss), inc=True)
            S.v(lambda e: e.tensor_scalar(out=s_m, in0=s_s1, scalar1=1.0 / E,
                                          scalar2=None, op0=ALU.mult))
            S.v(lambda e: e.tensor_tensor(out=s_mm2, in0=s_m, in1=s_m, op=ALU.mult))
            S.wait("vector", "ac", t_sq)
            t_var = S.v(lambda e: e.scalar_tensor_tensor(
                out=s_var, in0=s_ss, scalar=1.0 / E, in1=s_mm2,
                op0=ALU.mult, op1=ALU.subtract), inc=True)
            S.wait("scalar", "dv", t_var)
            t_rstd = S.s(lambda e: e.activation(
                out=s_rstd, in_=s_var, func=AF.Sqrt, bias=c_eps5), inc=True)
            S.wait("vector", "ac", t_rstd)
            S.v(lambda e: e.reciprocal(out=s_rstd, in_=s_rstd))
            if tt == 0:
                S.wait("vector", "ac", t_b0)
                S.wait("vector", "ac", t_b1)
            xt_use[sl] = S.v(lambda e, x=xt[sl]: e.tensor_scalar(
                out=h1t, in0=x, scalar1=s_m, scalar2=s_rstd,
                op0=ALU.subtract, op1=ALU.mult), inc=True)
            S.v(lambda e: e.tensor_tensor(out=h1t, in0=h1t, in1=bc0, op=ALU.mult))
            t_h1 = S.v(lambda e: e.tensor_tensor(out=h1t, in0=h1t, in1=bc1, op=ALU.add), inc=True)
            S.wait("tensor", "dv", t_h1)
            for ec in range(EC):
                bk = ec % 2
                if tr_use[bk]:
                    S.wait("tensor", "ac", tr_use[bk])
                t_tr = S.pe(lambda e, c=ec, b=bk: e.transpose(
                    out=ps[1 + b][:, 0:128], in_=h1t[:, c * 128:(c + 1) * 128],
                    identity=idf), inc=True)
                S.wait("scalar", "pe", t_tr)
                tr_use[bk] = S.s(lambda e, c=ec, b=bk, t=tt: e.activation(
                    out=hF[:, c * T + t * 128: c * T + t * 128 + 128],
                    in_=ps[1 + b][:, 0:128], func=AF.Copy), inc=True)
            last_hF_evac = tr_use[(EC - 1) % 2]

        p1 = Bump(O_AFTER_HF)
        chi = p1.f(T); clo = p1.f(T); shi = p1.f(T); slo = p1.f(T)
        wt = [p1.f(E), p1.f(E)]
        qkst = [p1.f(512), p1.f(512)]
        swpst = [p1.f(512), p1.f(512)]
        rs1 = [p1.f(512), p1.f(512)]
        rs2 = p1.f(512)
        fence_sync()
        S.load(chi, chi_in[:, :], "tab"); S.load(clo, clo_in[:, :], "tab")
        S.load(shi, shi_in[:, :], "tab")
        t_tab = S.load(slo, slo_in[:, :], "tab")
        S.waitp("vector", t_tab)
        S.wait("tensor", "ac", last_hF_evac)
        wt_use = [None, None]
        st_use = [None, None]
        sw_use = [None, None]
        rs_store = [None, None]
        psA_use = [None, None]
        for (wdram, bias_sb, dst_sc) in ((wq_t, bq_sb, q_sc), (wk_t, bk_sb, k_sc)):
            for ft in range(EC):
                sl = ft % 2
                if wt_use[sl]:
                    S.wait("sync", "pe", wt_use[sl])
                t_w = S.load(wt[sl], wdram[ft * 128:(ft + 1) * 128, :], f"wt{sl}")
                S.waitp("tensor", t_w)
                for tq in range(QT):
                    bk = tq % 2
                    pA = ps[0] if bk == 0 else ps[3]
                    S.waitp("tensor", psA_use[bk])
                    for ec in range(EC):
                        t_mm = S.pe(lambda e, c=ec, q=tq, w=wt[sl], p=pA: e.matmul(
                            out=p[:, 0:512], lhsT=w[:, c * 128:(c + 1) * 128],
                            rhs=hF[:, c * T + q * 512: c * T + (q + 1) * 512],
                            start=(c == 0), stop=(c == EC - 1),
                            skip_group_check=True), inc=(ec == EC - 1))
                    wt_use[sl] = t_mm
                    ssl = (ft * QT + tq) % 2
                    if st_use[ssl]:
                        S.wait("scalar", "dv", st_use[ssl])
                    S.wait("scalar", "pe", t_mm)
                    t_ev = S.s(lambda e, f=ft, d=qkst[ssl], p=pA, b=bias_sb: e.activation(
                        out=d, in_=p[:, 0:512], func=AF.Identity, bias=b[:, f:f + 1]), inc=True)
                    psA_use[bk] = ("ac", t_ev)
                    if sw_use[ssl]:
                        S.wait("gpsimd", "dv", sw_use[ssl])
                    S.wait("gpsimd", "ac", t_ev)
                    S.store(swpst[ssl][0:64, :], qkst[ssl][64:128, :], f"sw{ssl}")
                    t_sw = S.store(swpst[ssl][64:128, :], qkst[ssl][0:64, :], f"sw{ssl}")
                    if rs_store[ssl]:
                        S.waitp("vector", rs_store[ssl])
                    S.waitp("vector", t_sw)
                    c0, c1 = tq * 512, (tq + 1) * 512
                    r1 = rs1[ssl]
                    S.v(lambda e, s=qkst[ssl], a=c0, b2=c1, r=r1: e.tensor_tensor(
                        out=r, in0=s, in1=chi[:, a:b2], op=ALU.mult))
                    S.v(lambda e, s=qkst[ssl], a=c0, b2=c1: e.tensor_tensor(
                        out=rs2, in0=s, in1=clo[:, a:b2], op=ALU.mult))
                    st_use[ssl] = S.v(lambda e, r=r1: e.tensor_tensor(
                        out=r, in0=r, in1=rs2, op=ALU.add), inc=True)
                    S.v(lambda e, s=swpst[ssl], a=c0, b2=c1: e.tensor_tensor(
                        out=rs2, in0=s, in1=shi[:, a:b2], op=ALU.mult))
                    S.v(lambda e, r=r1: e.tensor_tensor(out=r, in0=r, in1=rs2, op=ALU.add))
                    sw_use[ssl] = S.v(lambda e, s=swpst[ssl], a=c0, b2=c1: e.tensor_tensor(
                        out=rs2, in0=s, in1=slo[:, a:b2], op=ALU.mult), inc=True)
                    t_rot = S.v(lambda e, r=r1: e.tensor_tensor(
                        out=r, in0=r, in1=rs2, op=ALU.add), inc=True)
                    S.wait("gpsimd", "dv", t_rot)
                    rs_store[ssl] = S.store(
                        dst_sc[ft * 128:(ft + 1) * 128, tq * 512:(tq + 1) * 512], r1, f"rq{ssl}")

        pv = Bump(O_AFTER_HF)
        vsl = pv.f(EC * NV)
        vst = [pv.f(NV), pv.f(NV)]
        bvB = pv.f(E)
        fence_sync()
        t_bv = bcast(bvB, lambda o, n: bv_row[0:1, o:o + n], E, None)
        vst_use = [None, None]
        last_vmm = None
        for fs in range(FSV):
            if last_vmm:
                S.wait("sync", "pe", last_vmm)
            t_v = S.load(vsl, wv_r[fs * 128:(fs + 1) * 128, :], "vsl")
            S.waitp("tensor", t_v)
            for tt in range(TC):
                bk = tt % 2
                pA = ps[0] if bk == 0 else ps[3]
                S.waitp("tensor", psA_use[bk])
                for ec in range(EC):
                    t_mm = S.pe(lambda e, c=ec, t=tt, p=pA: e.matmul(
                        out=p[:, 0:NV], lhsT=hF[:, c * T + t * 128: c * T + t * 128 + 128],
                        rhs=vsl[:, c * NV:(c + 1) * NV], start=(c == 0), stop=(c == EC - 1),
                        skip_group_check=True), inc=(ec == EC - 1))
                last_vmm = t_mm
                ssl = tt % 2
                if vst_use[ssl]:
                    S.waitp("vector", vst_use[ssl])
                S.wait("vector", "pe", t_mm)
                if fs == 0 and tt == 0:
                    S.wait("vector", "ac", t_bv)
                t_vv = S.v(lambda e, d=vst[ssl], p=pA, f=fs: e.tensor_tensor(
                    out=d, in0=p[:, 0:NV], in1=bvB[:, f * NV:(f + 1) * NV],
                    op=ALU.add), inc=True)
                psA_use[bk] = ("dv", t_vv)
                S.wait("gpsimd", "dv", t_vv)
                for hh in range(NV // 128):
                    h = fs * (NV // 128) + hh
                    vst_use[ssl] = S.store(
                        v_sc[h * T + tt * 128: h * T + (tt + 1) * 128, :],
                        vst[ssl][:, hh * 128:(hh + 1) * 128], f"vs{ssl}")
        spill_done = ["rq0", "rq1", "vs0", "vs1"]

        p2 = Bump(O_AFTER_HF)
        qrot = [p2.f(T), p2.f(T)]
        krot = [p2.f(T), p2.f(T)]
        VRW = TC * 129
        vrhs = [p2.f(VRW + 7), p2.f(VRW + 7)]
        pT = [p2.f(512) for _ in range(6)]
        yst = [p2.f(128), p2.f(128), p2.f(128), p2.f(128)]
        masks_t = p2.f(4 * 512)
        masks = [masks_t[:, i * 512:(i + 1) * 512] for i in range(4)]
        fence_sync()
        t_mload = S.load(masks_t, mask_in[:, :], "msk")
        t_ones = None
        S.wait_total("vector", ["vs0", "vs1", "rq0", "rq1"])
        for vb in range(2):
            for c in range(TC):
                t_ones = S.v(lambda e, b=vb, cc=c: e.memset(
                    vrhs[b][:, cc * 129 + 128: cc * 129 + 129], 1.0), inc=True)
        S.wait_total("sync", spill_done)
        S.wait("sync", "dv", t_ones)

        head_use_pe = [None, None]
        ptslot_use = [None] * 6
        yst_use = [None, None, None, None]
        ptr_use = None
        av_dv = [None] * 4
        first_mask_wait = True
        yF_done = None
        for h in range(H):
            hb = h % 2
            if head_use_pe[hb]:
                S.wait("sync", "pe", head_use_pe[hb])
            S.load(qrot[hb], q_sc[h * 128:(h + 1) * 128, :], f"hd{hb}")
            S.load(krot[hb], k_sc[h * 128:(h + 1) * 128, :], f"hd{hb}")
            for c in range(TC):
                t_v = S.load(vrhs[hb][:, c * 129:c * 129 + 128],
                             v_sc[h * T + c * 128: h * T + (c + 1) * 128, :], f"hd{hb}")
            S.waitp("tensor", t_v)
            exp_done = {}
            avlast = {}
            SBANKS = (0, 1, 7)

            def emit_scores(j, batch):
                for c in batch:
                    sbk = c % 3
                    pS = ps[SBANKS[sbk]]
                    if (c - 3) in exp_done:
                        S.wait("tensor", "ac", exp_done[c - 3])
                    spt = c % 6
                    t_sc2 = S.pe(lambda e, cc=c, jj=j, p=pS, b=hb: e.matmul(
                        out=p[:, 0:512], lhsT=krot[b][:, cc * 128:(cc + 1) * 128],
                        rhs=qrot[b][:, jj * 512:(jj + 1) * 512],
                        start=True, stop=True, skip_group_check=True), inc=True)
                    diag = c >= 4 * j
                    if diag:
                        nonlocal first_mask_wait
                        if first_mask_wait:
                            S.waitp("vector", t_mload)
                            first_mask_wait = False
                        S.wait("vector", "pe", t_sc2)
                        t_mk = S.v(lambda e, p=pS, r=c - 4 * j: e.tensor_tensor(
                            out=p[:, 0:512], in0=p[:, 0:512], in1=masks[r],
                            op=ALU.add), inc=True)
                        S.wait("scalar", "dv", t_mk)
                    else:
                        S.wait("scalar", "pe", t_sc2)
                    if ptslot_use[spt]:
                        S.wait("scalar", "pe", ptslot_use[spt])
                    t_exp = S.s(lambda e, p=pS, d=pT[spt]: e.activation(
                        out=d, in_=p[:, 0:512], func=AF.Exp, bias=c_expb), inc=True)
                    exp_done[c] = t_exp

            def emit_avs(j, batch):
                S.wait("tensor", "ac", exp_done[batch[-1]])
                for c in batch:
                    spt = c % 6
                    last = None
                    for qs in range(4):
                        qq = 4 * j + qs
                        if c > qq:
                            continue
                        if c == 0 and av_dv[qs] is not None:
                            S.wait("tensor", "dv", av_dv[qs])
                        last = S.pe(lambda e, cc=c, q=qs, b=hb, qqv=qq, sb2=spt: e.matmul(
                            out=ps[2 + q][:, 0:129], lhsT=pT[sb2][:, q * 128:(q + 1) * 128],
                            rhs=vrhs[b][:, cc * 129: cc * 129 + 129],
                            start=(cc == 0), stop=(cc == qqv),
                            skip_group_check=True), inc=True)
                    avlast[c] = last
                    ptslot_use[spt] = last

            for j in range(QT):
                nact = min(TC, 4 * (j + 1))
                batches = [list(range(c0, min(c0 + 3, nact)))
                           for c0 in range(0, nact, 3)]
                pend = None
                for batch in batches:
                    emit_scores(j, batch)
                    if pend is not None:
                        emit_avs(j, pend)
                    pend = batch
                emit_avs(j, pend)
                t_trl = None
                for qs in range(4):
                    S.wait("vector", "pe", avlast[4 * j + qs])
                    if yst_use[qs]:
                        S.wait("vector", "pe", yst_use[qs])
                    S.v(lambda e, q=qs: e.reciprocal(out=s_zr, in_=ps[2 + q][:, 128:129]))
                    av_dv[qs] = S.v(lambda e, q=qs, d=yst[qs]: e.tensor_scalar(
                        out=d, in0=ps[2 + q][:, 0:128], scalar1=s_zr, scalar2=None,
                        op0=ALU.mult), inc=True)
                    S.wait("tensor", "dv", av_dv[qs])
                    if qs == 0:
                        if ptr_use:
                            S.wait("tensor", "ac", ptr_use)
                        S.waitp("tensor", S.ps6)
                    t_trl = S.pe(lambda e, d=yst[qs], q=qs: e.transpose(
                        out=ps[6][:, q * 128:(q + 1) * 128], in_=d, identity=idf), inc=True)
                    yst_use[qs] = t_trl
                S.wait("scalar", "pe", t_trl)
                ptr_use = S.s(lambda e, hh2=h, jj2=j: e.activation(
                    out=hF[:, hh2 * T + jj2 * 512: hh2 * T + jj2 * 512 + 512],
                    in_=ps[6][:, 0:512], func=AF.Copy), inc=True)
                S.ps6 = ("ac", ptr_use)
            head_use_pe[hb] = avlast[nact - 1]
        yF_done = ptr_use

        p3 = Bump(O_AFTER_HF)
        bcA = p3.f(E); bcB = p3.f(E)
        wt2 = [p3.f(EC * N2), p3.f(EC * N2)]
        xinst = [p3.f(N2), p3.f(N2)]
        x2st = [p3.f(N2), p3.f(N2)]
        stst = [p3.f(1024), p3.f(1024)]
        stdmp = p3.f(1024)
        S.wait("tensor", "ac", yF_done)
        fence_sync()
        t_bo = bcast(bcA, lambda o, n: bo_row[0:1, o:o + n], E, None)
        t_ga = bcast(bcB, modrow(2), E, "signclamp", st_guard=mod_st_done)

        stats_jobs = []
        nch_m = [0, 0, 0]
        for mi, (wdram, rows, cols) in enumerate(((wg_r, JS * 128, EC * 512),
                                                  (wu_r, JS * 128, EC * 512),
                                                  (wd_t, I, E))):
            ccw = min(1024, cols)
            cpr = cols // ccw
            nch_m[mi] = (rows // 128) * cpr
            for ch in range(nch_m[mi]):
                stats_jobs.append((mi, wdram, (ch // cpr) * 128, (ch % cpr) * ccw, ch, ccw))
        st_slot_use = [None, None]
        nstats = len(stats_jobs)

        def emit_stats(idx):
            mi, wdram, r0, c0, ch, ccw = stats_jobs[idx]
            sl = idx % 2
            if st_slot_use[sl]:
                S.wait("sync", "ac", st_slot_use[sl])
            t_l = S.load(stst[sl][:, 0:ccw], wdram[r0:r0 + 128, c0:c0 + ccw], f"ss{sl}")
            S.waitp("scalar", t_l)
            st_slot_use[sl] = S.s(lambda e, s=stst[sl][:, 0:ccw], m=mi, p=ch, w=ccw: e.activation(
                out=stdmp[:, 0:w], in_=s, func=AF.Abs, accum_out=parts[m][:, p:p + 1]), inc=True)
            return st_slot_use[sl]

        wt2_use = [None, None]
        x2_dv = [None, None]
        x2st_st = [None, None]
        si = 0
        njobs_per = max(1, (nstats + FS2 * TC - 1) // (FS2 * TC))
        last_stats = None
        for fs in range(FS2):
            sl = fs % 2
            if wt2_use[sl]:
                S.wait("sync", "pe", wt2_use[sl])
            t_w = S.load(wt2[sl], wo_r[fs * 128:(fs + 1) * 128, :], f"wo{sl}")
            S.waitp("tensor", t_w)
            for tt in range(TC):
                bk = tt % 2
                pA = ps[0] if bk == 0 else ps[1]
                if x2_dv[bk]:
                    S.wait("tensor", "dv", x2_dv[bk])
                for ec in range(EC):
                    t_mm = S.pe(lambda e, c=ec, t=tt, p=pA, w=wt2[sl]: e.matmul(
                        out=p[:, 0:N2], lhsT=hF[:, c * T + t * 128: c * T + t * 128 + 128],
                        rhs=w[:, c * N2:(c + 1) * N2], start=(c == 0), stop=(c == EC - 1),
                        skip_group_check=True), inc=(ec == EC - 1))
                wt2_use[sl] = t_mm
                ssl = tt % 2
                if x2_dv[bk]:
                    S.wait("sync", "dv", x2_dv[bk])
                t_xl = S.load(xinst[ssl], x_in[tt * 128:(tt + 1) * 128, fs * N2:(fs + 1) * N2], f"xi{ssl}")
                if x2st_st[ssl]:
                    S.waitp("vector", x2st_st[ssl])
                S.wait("vector", "pe", t_mm)
                S.waitp("vector", t_xl)
                if fs == 0 and tt == 0:
                    S.wait("vector", "ac", t_bo)
                    S.wait("vector", "ac", t_ga)
                S.v(lambda e, p=pA, d=x2st[ssl], f=fs: e.tensor_tensor(
                    out=d, in0=p[:, 0:N2], in1=bcA[:, f * N2:(f + 1) * N2], op=ALU.add))
                S.v(lambda e, d=x2st[ssl], f=fs: e.tensor_tensor(
                    out=d, in0=d, in1=bcB[:, f * N2:(f + 1) * N2], op=ALU.mult))
                x2_dv[bk] = S.v(lambda e, d=x2st[ssl], x=xinst[ssl]: e.tensor_tensor(
                    out=d, in0=d, in1=x, op=ALU.add), inc=True)
                S.wait("gpsimd", "dv", x2_dv[bk])
                x2st_st[ssl] = S.store(
                    x2_sc[tt * 128:(tt + 1) * 128, fs * N2:(fs + 1) * N2], x2st[ssl], f"xs{ssl}")
                for _ in range(njobs_per):
                    if si < nstats:
                        last_stats = emit_stats(si); si += 1
        while si < nstats:
            last_stats = emit_stats(si); si += 1
        x2_done = ["xs0", "xs1"]

        S.wait("vector", "ac", last_stats)
        t_rc = None
        for mi, (s_sw, s_hf) in enumerate(((s_swg, s_hfg), (s_swu, s_hfu), (s_swd, s_hfd))):
            t_cs = S.v(lambda e, m=mi: e.reduce_sum(
                out=s_zr, in_=parts[m][:, 0:nch_m[m]], axis=AX.X), inc=True)
            S.wait("tensor", "dv", t_cs)
            S.waitp("tensor", S.ps6)
            t_mm = S.pe(lambda e: e.matmul(out=ps[6][:, 0:1], lhsT=ones_blk,
                                           rhs=s_zr, start=True, stop=True), inc=True)
            S.wait("vector", "pe", t_mm)
            S.v(lambda e, d=s_sw: e.tensor_scalar(
                out=d, in0=ps[6][:, 0:1], scalar1=1.0 / (I * E),
                scalar2=None, op0=ALU.mult))
            t_mn = S.v(lambda e, d=s_sw: e.tensor_scalar(
                out=d, in0=d, scalar1=1e-5, scalar2=None, op0=ALU.max), inc=True)
            S.ps6 = ("dv", t_mn)
            S.v(lambda e, d=s_hf, s0=s_sw: e.tensor_scalar(
                out=d, in0=s0, scalar1=0.5, scalar2=None, op0=ALU.mult))
            t_rc = S.v(lambda e, d=s_sw: e.reciprocal(out=d, in_=d), inc=True)
        stats_done = t_rc

        p3b = Bump(PH)
        qxF = p3b.b(EC * T)
        O_AFTER_QXF = p3b.off
        x2t = [p3b.f(E), p3b.f(E)]; h2t = [p3b.f(E), p3b.f(E)]
        LW3 = min(1024, E // 2)
        lad_y = p3b.f(LW3); lad_r = p3b.f(LW3); lad_b = p3b.f(LW3)
        lad_y2 = p3b.f(LW3); lad_r2 = p3b.f(LW3); lad_b2 = p3b.f(LW3)
        sqdump = p3b.f(E)
        bcC = p3b.f(E); bcD = p3b.f(E)
        wdst = [p3b.f(E), p3b.f(E)]
        ternb = [p3b.b(E), p3b.b(E)]
        scrd = p3b.b(E)
        fence_sync()
        t_cb = bcast(bcC, modrow(4), E, "add1", st_guard=mod_st_done)
        t_sb = bcast(bcD, modrow(3), E, "none")
        S.wait_total("sync", x2_done)

        wd_use = [None, None]
        tb_use = [None, None]
        tern_st = [None, None]

        def emit_tern(ic):
            sl = ic % 2
            if wd_use[sl]:
                S.wait("sync", "ac", wd_use[sl])
            t_l = S.load(wdst[sl], wd_t[ic * 128:(ic + 1) * 128, :], f"wd{sl}")
            S.waitp("scalar", t_l)
            if ic == 0:
                S.wait("scalar", "dv", stats_done)
            if tern_st[sl]:
                S.waitp("scalar", tern_st[sl])
            for g in tb_use:
                if g:
                    S.wait("scalar", "dv", g)
            S.s(lambda e, s=wdst[sl], d=ternb[sl]: e.activation(
                out=d, in_=s, func=AF.Sign, scale=s_swd, bias=c_mhalf))
            wd_use[sl] = S.s(lambda e, s=wdst[sl]: e.activation(
                out=scrd, in_=s, func=AF.Sign, scale=s_swd, bias=c_phalf), inc=True)
            S.wait("vector", "ac", wd_use[sl])
            tb_use[sl] = S.v(lambda e, d=ternb[sl]: e.tensor_tensor(
                out=d, in0=d, in1=scrd, op=ALU.add), inc=True)
            S.wait("gpsimd", "dv", tb_use[sl])
            tern_st[sl] = S.store(tern_sc[ic * 128:(ic + 1) * 128, :], ternb[sl], f"ts{sl}")

        qx_tr_use = [None, None]
        last_qx_evac = None
        tr_last = [None, None]
        x2t_use = [None, None]
        sq_done = [None, None]
        ld_done = [None, None]
        ic_done = 0

        def p3b_stage1(tt):
            p = tt % 2
            x2s = x2t[p]
            if x2t_use[p]:
                S.wait("sync", "dv", x2t_use[p])
            t_x2 = S.load(x2s, x2_sc[tt * 128:(tt + 1) * 128, :], f"x2t{p}")
            ld_done[p] = t_x2
            S.waitp("scalar", t_x2)
            sq_done[p] = S.s(lambda e, x=x2s, p2=p: e.activation(
                out=sqdump, in_=x, func=AF.Square, accum_out=sp_ss[p2]), inc=True)

        def p3b_stage2(tt):
            p = tt % 2
            x2s = x2t[p]
            h2s = h2t[p]
            S.waitp("vector", ld_done[p])
            S.v(lambda e, x=x2s, p2=p: e.reduce_sum(out=sp_s1[p2], in_=x, axis=AX.X))
            S.v(lambda e, p2=p: e.tensor_scalar(out=sp_m[p2], in0=sp_s1[p2],
                scalar1=1.0 / E, scalar2=None, op0=ALU.mult))
            S.v(lambda e, p2=p: e.tensor_tensor(out=sp_mm2[p2], in0=sp_m[p2],
                in1=sp_m[p2], op=ALU.mult))
            S.wait("vector", "ac", sq_done[p])
            t_var = S.v(lambda e, p2=p: e.scalar_tensor_tensor(
                out=sp_var[p2], in0=sp_ss[p2], scalar=1.0 / E, in1=sp_mm2[p2],
                op0=ALU.mult, op1=ALU.subtract), inc=True)
            S.wait("scalar", "dv", t_var)
            t_rstd = S.s(lambda e, p2=p: e.activation(
                out=sp_rstd[p2], in_=sp_var[p2], func=AF.Sqrt, bias=c_eps5), inc=True)
            S.wait("vector", "ac", t_rstd)
            S.v(lambda e, p2=p: e.reciprocal(out=sp_rstd[p2], in_=sp_rstd[p2]))
            if tt == 0:
                S.wait("vector", "ac", t_cb)
                S.wait("vector", "ac", t_sb)
            if tr_last[p]:
                S.wait("vector", "pe", tr_last[p])
            x2t_use[p] = S.v(lambda e, x=x2s, d=h2s, p2=p: e.tensor_scalar(
                out=d, in0=x, scalar1=sp_m[p2], scalar2=sp_rstd[p2],
                op0=ALU.subtract, op1=ALU.mult), inc=True)
            S.v(lambda e, d=h2s: e.tensor_tensor(out=d, in0=d, in1=bcC, op=ALU.mult))
            S.v(lambda e, d=h2s: e.tensor_tensor(out=d, in0=d, in1=bcD, op=ALU.add))
            S.v(lambda e, d=h2s, p2=p: e.reduce_max(out=sp_amax[p2], in_=d, axis=AX.X,
                                                    apply_absolute_value=True))
            S.v(lambda e, p2=p: e.tensor_scalar(
                out=sp_amax[p2], in0=sp_amax[p2], scalar1=1e-5, scalar2=None, op0=ALU.max))
            S.v(lambda e, t=tt, p2=p: e.tensor_scalar(
                out=invsx[:, t:t + 1], in0=sp_amax[p2], scalar1=1.0 / 127.0,
                scalar2=None, op0=ALU.mult))
            S.v(lambda e, t=tt, p2=p: e.reciprocal(out=sp_sx[p2], in_=invsx[:, t:t + 1]))

        def p3b_ladder(tt):
            nonlocal last_qx_evac, ic_done
            p = tt % 2
            h2s = h2t[p]
            t_lad = ladder_sd(h2s, sp_sx[p], h2s, E,
                              [lad_y, lad_y2], [lad_r, lad_r2], [lad_b, lad_b2], LW3)
            S.wait("tensor", "dv", t_lad)
            for e0 in range(0, EC, 4):
                npk = min(4, EC - e0)
                bsl = (e0 // 4) % 2
                if qx_tr_use[bsl]:
                    S.wait("tensor", "ac", qx_tr_use[bsl])
                t_tr = None
                for k in range(npk):
                    t_tr = S.pe(lambda e, c=e0 + k, b=bsl, d=h2s, kk=k: e.transpose(
                        out=ps[1 + b][:, kk * 128:(kk + 1) * 128],
                        in_=d[:, c * 128:(c + 1) * 128],
                        identity=idf), inc=True)
                tr_last[p] = t_tr
                S.wait("scalar", "pe", t_tr)
                qx_tr_use[bsl] = S.s(lambda e, c=e0, b=bsl, t=tt, w=npk * 128: e.activation(
                    out=qxF[:, t * E + c * 128: t * E + c * 128 + w],
                    in_=ps[1 + b][:, 0:w], func=AF.Copy), inc=True)
            last_qx_evac = qx_tr_use[((EC - 1) // 4) % 2]
            ntern = (IC * (tt + 1)) // TC
            while ic_done < ntern:
                emit_tern(ic_done); ic_done += 1

        p3b_stage1(0)
        for tt in range(TC):
            if tt + 1 < TC:
                p3b_stage1(tt + 1)
            p3b_stage2(tt)
            p3b_ladder(tt)
        while ic_done < IC:
            emit_tern(ic_done); ic_done += 1
        tern_done = ["ts0", "ts1"]

        t_scl = None
        for tt in range(TC):
            S.v(lambda e, t=tt: e.tensor_tensor(
                out=sclg[:, t:t + 1], in0=invsx[:, t:t + 1], in1=s_hfg, op=ALU.mult))
            t_scl = S.v(lambda e, t=tt: e.tensor_tensor(
                out=sclu[:, t:t + 1], in0=invsx[:, t:t + 1], in1=s_hfu, op=ALU.mult), inc=True)

        p4 = Bump(O_AFTER_QXF)
        gst = p4.f(EC * 512)
        ternG = [p4.b(EC * 512), p4.b(EC * 512)]
        ternU = [p4.b(EC * 512), p4.b(EC * 512)]
        scrg = p4.b(EC * 512)
        gs_t = [p4.f(512), p4.f(512)]
        hst = [p4.f(512), p4.f(512)]

        fence_sync()
        tG_use = [None, None]; tU_use = [None, None]
        tU_rdy_prev = None
        gst_free = None
        hst_use = [None, None]
        psG_use = [None, None]; psU_use = [None, None]
        for js in range(JS):
            sl = js % 2
            if gst_free:
                S.wait("sync", "ac", gst_free)
            if tG_use[sl]:
                S.wait("sync", "pe", tG_use[sl])
            t_g = S.load(gst, wg_r[js * 128:(js + 1) * 128, :], "gst")
            S.waitp("scalar", t_g)
            if js == 0:
                S.wait("scalar", "dv", stats_done)
            if tU_rdy_prev:
                S.wait("scalar", "dv", tU_rdy_prev)
            S.s(lambda e, d=ternG[sl]: e.activation(
                out=d, in_=gst, func=AF.Sign, scale=s_swg, bias=c_mhalf))
            t_sg = S.s(lambda e: e.activation(
                out=scrg, in_=gst, func=AF.Sign, scale=s_swg, bias=c_phalf), inc=True)
            gst_free = t_sg
            S.wait("vector", "ac", t_sg)
            tG_rdy = S.v(lambda e, d=ternG[sl]: e.tensor_tensor(
                out=d, in0=d, in1=scrg, op=ALU.add), inc=True)
            if tU_use[sl]:
                S.wait("sync", "pe", tU_use[sl])
            S.wait("sync", "ac", t_sg)
            t_u = S.load(gst, wu_r[js * 128:(js + 1) * 128, :], "gst")
            S.waitp("scalar", t_u)
            S.wait("scalar", "dv", tG_rdy)
            S.s(lambda e, d=ternU[sl]: e.activation(
                out=d, in_=gst, func=AF.Sign, scale=s_swu, bias=c_mhalf))
            t_su = S.s(lambda e: e.activation(
                out=scrg, in_=gst, func=AF.Sign, scale=s_swu, bias=c_phalf), inc=True)
            gst_free = t_su
            S.wait("vector", "ac", t_su)
            tU_rdy = S.v(lambda e, d=ternU[sl]: e.tensor_tensor(
                out=d, in0=d, in1=scrg, op=ALU.add), inc=True)
            tU_rdy_prev = tU_rdy
            if js == 0:
                S.wait("tensor", "ac", last_qx_evac)
            S.wait("tensor", "dv", tU_rdy)
            for tt in range(TC):
                bg = tt % 2
                pG = ps[0] if bg == 0 else ps[1]
                pU = ps[2] if bg == 0 else ps[3]
                S.waitp("tensor", psG_use[bg])
                for ec in range(EC):
                    t_gmm = S.pe(lambda e, c=ec, t=tt, p=pG, d=ternG[sl]: e.matmul(
                        out=p[:, 0:512], lhsT=qxF[:, t * E + c * 128: t * E + c * 128 + 128],
                        rhs=d[:, c * 512:(c + 1) * 512], start=(c == 0), stop=(c == EC - 1),
                        skip_group_check=True), inc=(ec == EC - 1))
                S.waitp("tensor", psU_use[bg])
                for ec in range(EC):
                    t_umm = S.pe(lambda e, c=ec, t=tt, p=pU, d=ternU[sl]: e.matmul(
                        out=p[:, 0:512], lhsT=qxF[:, t * E + c * 128: t * E + c * 128 + 128],
                        rhs=d[:, c * 512:(c + 1) * 512], start=(c == 0), stop=(c == EC - 1),
                        skip_group_check=True), inc=(ec == EC - 1))
                tG_use[sl] = t_gmm; tU_use[sl] = t_umm
                ssl = tt % 2
                S.wait("scalar", "pe", t_gmm)
                if js == 0 and tt == 0:
                    S.wait("scalar", "dv", t_scl)
                if psU_use[bg]:
                    S.wait("scalar", "dv", psU_use[bg][1])
                t_si = S.s(lambda e, p=pG, d=gs_t[ssl], t=tt: e.activation(
                    out=d, in_=p[:, 0:512], func=AF.Silu, scale=sclg[:, t:t + 1]), inc=True)
                psG_use[bg] = ("ac", t_si)
                if hst_use[ssl]:
                    S.waitp("vector", hst_use[ssl])
                S.wait("vector", "pe", t_umm)
                S.wait("vector", "ac", t_si)
                t_h = S.v(lambda e, p=pU, d=hst[ssl], g=gs_t[ssl], t=tt: e.scalar_tensor_tensor(
                    out=d, in0=p[:, 0:512], scalar=sclu[:, t:t + 1], in1=g,
                    op0=ALU.mult, op1=ALU.mult), inc=True)
                psU_use[bg] = ("dv", t_h)
                S.wait("gpsimd", "dv", t_h)
                hst_use[ssl] = S.store(
                    h_sc[tt * 128:(tt + 1) * 128, js * 512:(js + 1) * 512], hst[ssl], f"hs{ssl}")
        h_done = ["hs0", "hs1"]

        p5 = Bump(PH)
        lnwB = p5.f(I)
        h2t2 = [p5.f(I), p5.f(I)]
        qh2F = p5.b(IC * 256)
        ternD = [p5.b(E), p5.b(E)]
        x2l = p5.f(E)
        xout = p5.f(E)
        gbB = p5.f(E)
        LW5 = min(1024, I // 2)
        dl_y = [p5.f(LW5) for _ in range(3)]
        dl_r = [p5.f(LW5) for _ in range(3)]
        dl_b = [p5.f(LW5) for _ in range(3)]
        dl_dump = p5.f(512)
        sqparts = p5.f(2 * (I // 512))

        fence_sync()
        t_lnw = bcast(lnwB, lambda o, n: lnw_row[0:1, o:o + n], I, None)
        t_gb = bcast(gbB, modrow(5), E, "signclamp", st_guard=mod_st_done)
        S.wait_total("sync", h_done)
        S.wait_total("sync", tern_done)

        h2buf_use = [None, None]
        ternD_use = [None, None]
        tr2_use = [None, None]
        qh2_evac = None
        prev_epi_dv = None
        xout_st = [None, None]

        ld5_done = [None, None]
        sq5_done = [None, None]
        h2_guard_p = [None, None]

        def p5_stage1(b6, sub):
            tt = 2 * b6 + sub
            hb2 = h2t2[sub]
            if h2buf_use[sub]:
                S.wait("sync", "pe", h2buf_use[sub])
            if h2_guard_p[sub]:
                S.wait("sync", "dv", h2_guard_p[sub])
            t_h = S.load(hb2, h_sc[tt * 128:(tt + 1) * 128, :], f"h2l{sub}")
            ld5_done[sub] = t_h
            S.waitp("scalar", t_h)
            if b6 == 0 and sub == 0:
                S.wait("scalar", "ac", t_lnw)
            t_sq = None
            nstrip = I // 512
            for si in range(nstrip):
                t_sq = S.s(lambda e, o=si * 512, p=si, s2=sub, h=hb2: e.activation(
                    out=dl_dump, in_=h[:, o:o + 512], func=AF.Square,
                    accum_out=sqparts[:, s2 * nstrip + p: s2 * nstrip + p + 1]), inc=True)
            sq5_done[sub] = t_sq

        def p5_stage2lad(b6, sub):
            tt = 2 * b6 + sub
            hb2 = h2t2[sub]
            nstrip = I // 512
            S.waitp("vector", ld5_done[sub])
            S.wait("vector", "ac", sq5_done[sub])
            S.v(lambda e, s2=sub: e.reduce_sum(
                out=sp_ms[s2], in_=sqparts[:, s2 * nstrip:(s2 + 1) * nstrip], axis=AX.X))
            S.v(lambda e, h=hb2: e.tensor_tensor(out=h, in0=h, in1=lnwB, op=ALU.mult))
            S.v(lambda e, h=hb2, s2=sub: e.reduce_max(
                out=sp_rm[s2], in_=h, axis=AX.X, apply_absolute_value=True))
            t_ms = S.v(lambda e, s2=sub: e.tensor_scalar(
                out=sp_ms[s2], in0=sp_ms[s2], scalar1=1.0 / I, scalar2=None,
                op0=ALU.mult), inc=True)
            S.wait("scalar", "dv", t_ms)
            t_r2 = S.s(lambda e, s2=sub: e.activation(
                out=sp_rstd2[s2], in_=sp_ms[s2], func=AF.Sqrt, bias=c_eps6), inc=True)
            S.wait("vector", "ac", t_r2)
            S.v(lambda e, s2=sub: e.reciprocal(out=sp_rstd2[s2], in_=sp_rstd2[s2]))
            S.v(lambda e, s2=sub: e.tensor_tensor(
                out=sp_rm[s2], in0=sp_rm[s2], in1=sp_rstd2[s2], op=ALU.mult))
            S.v(lambda e, s2=sub: e.tensor_scalar(
                out=sp_rm[s2], in0=sp_rm[s2], scalar1=1e-5, scalar2=None, op0=ALU.max))
            S.v(lambda e, s2=sub, p=b6 % 2: e.tensor_scalar(
                out=s_invs2[p][s2], in0=sp_rm[s2], scalar1=1.0 / 127.0,
                scalar2=None, op0=ALU.mult))
            S.v(lambda e, s2=sub, p=b6 % 2: e.reciprocal(out=sp_sx2[s2], in_=s_invs2[p][s2]))
            S.v(lambda e, s2=sub: e.tensor_tensor(
                out=sp_sx2[s2], in0=sp_sx2[s2], in1=sp_rstd2[s2], op=ALU.mult))
            t_lad = ladder_sd(hb2, sp_sx2[sub], hb2, I, dl_y, dl_r, dl_b, LW5)
            h2_guard_p[sub] = t_lad
            return t_lad

        def emit_transposes(b6, t_lad):
            nonlocal qh2_evac, h2buf_use
            S.wait("tensor", "dv", t_lad)
            if prev_epi_dv:
                S.wait("tensor", "dv", prev_epi_dv)
            for sub in range(2):
                last_tr = None
                for i0 in range(0, IC, 4):
                    npk = min(4, IC - i0)
                    bsl = (i0 // 4) % 2
                    if tr2_use[bsl]:
                        S.wait("tensor", "ac", tr2_use[bsl])
                    t_tr = None
                    for k in range(npk):
                        t_tr = S.pe(lambda e, c=i0 + k, b=bsl, s2=sub, kk=k: e.transpose(
                            out=ps[1 + b][:, kk * 128:(kk + 1) * 128],
                            in_=h2t2[s2][:, c * 128:c * 128 + 128],
                            identity=idf), inc=True)
                    S.wait("scalar", "pe", t_tr)
                    tr2_use[bsl] = S.s(lambda e, c=i0, b=bsl, s2=sub, w=npk * 128: e.activation(
                        out=qh2F[:, s2 * IC * 128 + c * 128: s2 * IC * 128 + c * 128 + w],
                        in_=ps[1 + b][:, 0:w], func=AF.Copy), inc=True)
                    last_tr = t_tr
                h2buf_use[sub] = last_tr
            qh2_evac = tr2_use[(((IC - 1) // 4) % 2)]

        def emit_down(b6):
            nonlocal prev_epi_dv, ternD_use
            S.wait("tensor", "ac", qh2_evac)
            if prev_epi_dv:
                S.wait("tensor", "dv", prev_epi_dv)
            mm_fin = None
            for ic in range(IC):
                sl = ic % 2
                if ternD_use[sl]:
                    S.wait("sync", "pe", ternD_use[sl])
                t_td = S.load(ternD[sl], tern_sc[ic * 128:(ic + 1) * 128, :], f"td{sl}")
                S.waitp("tensor", t_td)
                lastmm = None
                for tsub in range(2):
                    for ft in range(NFT):
                        bank = tsub * NFT + ft
                        lastmm = S.pe(lambda e, c=ic, ts=tsub, f=ft, b=bank, d=ternD[sl]:
                                      e.matmul(out=ps[b][:, 0:NF],
                                               lhsT=qh2F[:, ts * IC * 128 + c * 128: ts * IC * 128 + c * 128 + 128],
                                               rhs=d[:, f * NF:(f + 1) * NF],
                                               start=(c == 0), stop=(c == IC - 1),
                                               skip_group_check=True),
                                      inc=(tsub == 1 and ft == NFT - 1))
                ternD_use[sl] = lastmm
                mm_fin = lastmm
            S.wait("vector", "pe", mm_fin)
            if b6 == 0:
                S.wait("vector", "ac", t_gb)
            for sub in range(2):
                tt = 2 * b6 + sub
                if prev_epi_dv:
                    S.wait("sync", "dv", prev_epi_dv)
                t_x2 = S.load(x2l, x2_sc[tt * 128:(tt + 1) * 128, :], "x2l")
                S.waitp("vector", t_x2)
                if xout_st[0]:
                    S.waitp("vector", xout_st[0])
                S.v(lambda e, s2=sub, p=b6 % 2: e.tensor_tensor(
                    out=s_scld[s2], in0=s_invs2[p][s2], in1=s_hfd, op=ALU.mult))
                t_epi = None
                for ft in range(NFT):
                    bank = sub * NFT + ft
                    d_ap = xout[:, ft * NF:(ft + 1) * NF]
                    S.v(lambda e, d=d_ap, b=bank, s2=sub: e.tensor_scalar(
                        out=d, in0=ps[b][:, 0:NF], scalar1=s_scld[s2], scalar2=None,
                        op0=ALU.mult))
                    S.v(lambda e, d=d_ap, f=ft: e.tensor_tensor(
                        out=d, in0=d, in1=gbB[:, f * NF:(f + 1) * NF], op=ALU.mult))
                    t_epi = S.v(lambda e, d=d_ap, f=ft: e.tensor_tensor(
                        out=d, in0=d, in1=x2l[:, f * NF:(f + 1) * NF], op=ALU.add), inc=True)
                S.wait("gpsimd", "dv", t_epi)
                xout_st[0] = S.store(y_out[tt * 128:(tt + 1) * 128, :], xout, "yo0")
                prev_epi_dv = t_epi

        pending = None
        for b6 in range(NB6):
            p5_stage1(b6, 0)
            p5_stage2lad(b6, 0)
            p5_stage1(b6, 1)
            t_lad1 = p5_stage2lad(b6, 1)
            if pending is not None:
                emit_down(pending)
            emit_transposes(b6, t_lad1)
            pending = b6
        emit_down(pending)

        S.wait_total("sync", ["yo0", "yo1"])

        for nm in ("pe", "dv", "ac"):
            S.semh[nm] = es.enter_context(nc.semaphore(nm))
        for nm in sorted(S.sems):
            S.semh[nm] = es.enter_context(nc.semaphore(nm))

        block = es.enter_context(nc.Block())

        @block.sync
        def _(sync):
            for fn in S.ops["sync"]:
                fn(sync)

        @block.gpsimd
        def _(gpsimd):
            for fn in S.ops["gpsimd"]:
                fn(gpsimd)

        @block.tensor
        def _(tensor):
            for fn in S.ops["tensor"]:
                fn(tensor)

        @block.vector
        def _(vector):
            for fn in S.ops["vector"]:
                fn(vector)

        @block.scalar
        def _(scalar):
            for fn in S.ops["scalar"]:
                fn(scalar)

    return nc


def make_tables(T):
    D = 128
    j = np.arange(D // 2, dtype=np.float64)
    thetas = 10000.0 ** (-2.0 * j / D)
    m = np.arange(1, T + 1, dtype=np.float64)
    ang = np.concatenate([thetas, thetas])[:, None] * m[None, :]
    c = 128.0 ** (-0.25)
    cosv = np.cos(ang) * c
    sinv = np.sin(ang) * c
    sgn = np.ones((D, 1)); sgn[:D // 2] = -1.0
    sinv = sinv * sgn
    chi = cosv.astype(np.float32)
    clo = (cosv - chi.astype(np.float64)).astype(np.float32)
    shi = sinv.astype(np.float32)
    slo = (sinv - shi.astype(np.float64)).astype(np.float32)
    return chi, clo, shi, slo


def make_masks():
    i = np.arange(128)[:, None]
    jq = np.arange(512)[None, :]
    return np.concatenate([
        np.where(128 * rr + i <= jq, 0.0, NEGBIG).astype(np.float32)
        for rr in range(4)], axis=1)


def host_inputs(T, E, I, Wq, bq, Wk, bk, Wv, bv, Wo, bo, wg, wu, wd, lnw, wmod):
    EC = E // 128
    FT = E // 128
    NV = min(512, E); FSV = E // NV
    N2 = min(256, E); FS2 = E // N2
    JS = I // 512
    asc = np.ascontiguousarray
    f = np.float32
    d = {}
    d["wq_t"] = asc(np.asarray(Wq, f).reshape(FT, 128, EC, 128).transpose(0, 3, 2, 1).reshape(E, E))
    d["wk_t"] = asc(np.asarray(Wk, f).reshape(FT, 128, EC, 128).transpose(0, 3, 2, 1).reshape(E, E))
    d["wv_r"] = asc(np.asarray(Wv, f).T.reshape(EC, 128, FSV, NV).transpose(2, 1, 0, 3).reshape(FSV * 128, EC * NV))
    d["wo_r"] = asc(np.asarray(Wo, f).T.reshape(EC, 128, FS2, N2).transpose(2, 1, 0, 3).reshape(FS2 * 128, EC * N2))
    d["bq_c"] = asc(np.asarray(bq, f).reshape(EC, 128).T)
    d["bk_c"] = asc(np.asarray(bk, f).reshape(EC, 128).T)
    d["bv_row"] = asc(np.asarray(bv, f).reshape(1, E))
    d["bo_row"] = asc(np.asarray(bo, f).reshape(1, E))
    d["wmod"] = asc(np.asarray(wmod, f))
    d["wg_r"] = asc(np.asarray(wg, f).T.reshape(EC, 128, JS, 512).transpose(2, 1, 0, 3).reshape(JS * 128, EC * 512))
    d["wu_r"] = asc(np.asarray(wu, f).T.reshape(EC, 128, JS, 512).transpose(2, 1, 0, 3).reshape(JS * 128, EC * 512))
    d["wd_t"] = asc(np.asarray(wd, f).T)
    d["lnw_row"] = asc(np.asarray(lnw, f).reshape(1, I))
    chi, clo, shi, slo = make_tables(T)
    d["chi_in"] = chi; d["clo_in"] = clo; d["shi_in"] = shi; d["slo_in"] = slo
    d["mask_in"] = make_masks()
    d["id_in"] = np.eye(128, dtype=np.float32)
    return d


def kernel(L0_attn_x, L0_attn_Wq, L0_attn_bq, L0_attn_Wk, L0_attn_bk,
           L0_attn_Wv, L0_attn_bv, L0_attn_Wo, L0_attn_bo,
           L0_ffn_w_gate, L0_ffn_w_up, L0_ffn_w_down, L0_ffn_ln_weight,
           L0_c, L0_wmod):
    T, E, I = 2048, 2048, 8192
    xs = np.asarray(L0_attn_x, np.float32)
    cs = np.asarray(L0_c, np.float32)
    B = xs.shape[0]
    EC = E // 128
    nc = build_program(T, E, I)
    shared = host_inputs(T, E, I, L0_attn_Wq, L0_attn_bq, L0_attn_Wk, L0_attn_bk,
                         L0_attn_Wv, L0_attn_bv, L0_attn_Wo, L0_attn_bo,
                         L0_ffn_w_gate, L0_ffn_w_up, L0_ffn_w_down,
                         L0_ffn_ln_weight, L0_wmod)
    in_maps = []
    for b in range(B):
        m = dict(shared)
        m["x_in"] = np.ascontiguousarray(xs[b])
        m["c_col"] = np.ascontiguousarray(cs[b].reshape(EC, 128).T)
        in_maps.append(m)
    res = run_bass_kernel_spmd(nc, in_maps, core_ids=list(range(B)))
    out = np.stack([np.asarray(res.results[b]["y_out"]) for b in range(B)], axis=0)
    return out.astype(np.float32)
